# version 10
# baseline (speedup 1.0000x reference)
import contextlib
import numpy as np
import ml_dtypes

import concourse.bass as bass
import concourse.mybir as mybir
from concourse.bass_utils import run_bass_kernel_spmd

F32 = mybir.dt.float32
BF16 = mybir.dt.bfloat16
AF = mybir.ActivationFunctionType
ALU = mybir.AluOpType
AX = mybir.AxisListType
BF = ml_dtypes.bfloat16

B, S, E = 4, 2048, 2048
WINDOW, NMETA = 512, 16
HDIM = 128
NH = 16
LAT = 512
RD = 64
IEXP = 1024
NEXP = 8
EPS = 1e-5
SCALE = float(1.0 / np.sqrt(128.0))

RLOC = 2688
NBLK = 21
HALO0, OWN0, REST0 = 129, 640, 1664
NQ = 2048
NOWN = 1024
NEG = -1e30

TAPS = False
PHASES_MAX = 9


class Buf:
    __slots__ = ("ws", "r", "name")

    def __init__(self, name=""):
        self.ws = []
        self.r = []
        self.name = name


NLANE = 8


class Sched:
    ENGS = ("pe", "act", "dve", "pool", "sync")

    def __init__(self):
        self.ops = {e: [] for e in self.ENGS}
        self.marked = {e: set() for e in self.ENGS}
        self.dma_seq = {e: 0 for e in self.ENGS}
        self.lane_use = {e: [0] * NLANE for e in self.ENGS}
        self.lane_last = {e: [None] * NLANE for e in self.ENGS}
        self.last_compute = {e: None for e in self.ENGS}

    def op(self, eng, thunk, reads=(), writes=(), dma=False):
        idx = len(self.ops[eng])
        deps = []
        for b in reads:
            deps.extend(b.ws)
        for b in writes:
            if b.r:
                deps.extend(b.r)
            else:
                deps.extend(b.ws)
        waits = {}
        for (de, di) in deps:
            dep_lane = self.ops[de][di][3]
            if de == eng:
                dep_dma = dep_lane is not None
                if dep_dma and dma:
                    continue
                if (not dep_dma) and (not dma) and eng == "pe":
                    continue
            key = (de, "c") if dep_lane is None else (de, dep_lane[0])
            if waits.get(key, -1) < di:
                waits[key] = di
        for (de, _k), di in waits.items():
            if self.ops[de][di][3] is None:
                self.marked[de].add(di)
        for b in reads:
            b.r.append((eng, idx))
        for b in writes:
            if b.r:
                b.ws = [(eng, idx)]
                b.r = []
            else:
                b.ws.append((eng, idx))
        lane_info = None
        if dma:
            lane = self.dma_seq[eng] % NLANE
            self.dma_seq[eng] += 1
            use = self.lane_use[eng][lane]
            self.lane_use[eng][lane] += 1
            lane_info = (lane, use)
            self.lane_last[eng][lane] = idx
        else:
            self.last_compute[eng] = idx
        self.ops[eng].append([thunk, waits, 16 if dma else 1, lane_info])

    def barrier(self):
        targets = {}
        for oe in self.ENGS:
            t = []
            if self.last_compute[oe] is not None:
                t.append(self.last_compute[oe])
                self.marked[oe].add(self.last_compute[oe])
            for li in self.lane_last[oe]:
                if li is not None:
                    t.append(li)
            targets[oe] = t
        for e in self.ENGS:
            waits = {}
            ext = []
            for oe in self.ENGS:
                if oe == e:
                    continue
                for ti in targets[oe]:
                    ext.append((oe, ti))
            self.ops[e].append([None, waits, 1, None, ext])
            self.last_compute[e] = len(self.ops[e]) - 1

    def finalize(self, nc, sems, engmap):
        cum = {}
        for e in self.ENGS:
            vals, tot = {}, 0
            for i, (_t, _w, amt) in enumerate(self.ops[e]):
                if i in self.marked[e]:
                    tot += amt
                vals[i] = tot
            cum[e] = vals
        n_w = sum(len(w) for e in self.ENGS for (_t, w, _a) in self.ops[e])
        n_o = sum(len(self.ops[e]) for e in self.ENGS)

        def body(eng_name):
            eng = engmap[eng_name]
            waited = {}
            for i, (thunk, waits, amt) in enumerate(self.ops[eng_name]):
                for we in sorted(waits):
                    v = cum[we][waits[we]]
                    if waited.get(we, 0) < v:
                        eng.wait_ge(sems[we], v)
                        waited[we] = v
                if thunk is None:
                    inst = eng.nop()
                else:
                    inst = thunk(eng)
                if i in self.marked[eng_name]:
                    inst.then_inc(sems[eng_name], amt)
        return body, n_o, n_w


def _posmap(h):
    pos = np.full(RLOC, -1, np.int64)
    pos[0:16] = np.arange(16)
    if h == 1:
        pos[HALO0:HALO0 + 511] = np.arange(513, 1024)
    pos[OWN0:OWN0 + 1024] = np.arange(1024 * h, 1024 * h + 1024)
    pos[REST0:REST0 + 1024] = np.arange(1024 * (1 - h), 1024 * (1 - h) + 1024)
    return pos


def _rope_sparse(pos):
    inv = 10000.0 ** (-np.arange(0, HDIM, 2, dtype=np.float64) / HDIM)
    p = np.where(pos < 0, 0, pos).astype(np.float64)
    ang = p[None, :] * inv[:, None]
    cos = np.repeat(np.cos(ang), 2, axis=0)
    sin = np.repeat(np.sin(ang), 2, axis=0)
    return cos.astype(np.float32), sin.astype(np.float32)


def _rope_mla(pos):
    inv = 10000.0 ** (-np.arange(0, RD, 2, dtype=np.float64) / RD)
    p = np.where(pos < 0, 0, pos).astype(np.float64)
    ang = p[None, :] * inv[:, None]
    c = np.concatenate([np.cos(ang), np.cos(ang)], 0)
    s = np.concatenate([np.sin(ang), np.sin(ang)], 0)
    return (np.concatenate([c, c], 0).astype(np.float32),
            np.concatenate([s, s], 0).astype(np.float32))


def _host_constants(h):
    pos = _posmap(h)
    valid = pos >= 0
    qpos = pos[OWN0:]
    d = {}
    d["cosq_sp"], d["sinq_sp"] = _rope_sparse(qpos)
    d["cosk_sp"], d["sink_sp"] = _rope_sparse(pos)
    d["cos_mla"], d["sin_mla"] = _rope_mla(qpos)

    Rsp = np.zeros((HDIM, HDIM), np.float32)
    for i in range(HDIM // 2):
        Rsp[2 * i, 2 * i + 1] = -1.0
        Rsp[2 * i + 1, 2 * i] = 1.0
    d["RspT"] = Rsp.T.astype(BF).copy()
    R64 = np.zeros((RD, RD), np.float32)
    for i in range(RD // 2):
        R64[i, i + RD // 2] = -1.0
        R64[i + RD // 2, i] = 1.0
    Rm = np.zeros((HDIM, HDIM), np.float32)
    Rm[:RD, :RD] = R64
    Rm[RD:, RD:] = R64
    d["RmlaT"] = Rm.T.astype(BF).copy()

    mask_win = np.zeros((4, 9, 128, 512), BF)
    for g in range(4):
        qp = qpos[512 * g: 512 * g + 512][None, :].astype(np.int64)
        kp0 = pos[0:128][:, None]
        mask_win[g, 0, 0:16] = ((kp0[0:16] <= qp) & (qp - kp0[0:16] > 511)).astype(BF)
        for r in range(1, 9):
            kb = 4 * g + r
            kp = pos[128 * kb:128 * kb + 128][:, None]
            vl = valid[128 * kb:128 * kb + 128][:, None]
            mask_win[g, r] = (vl & (kp <= qp) & (qp - kp <= 511)).astype(BF)
    d["mask_win"] = mask_win

    bias_mla = np.zeros((128, 16), np.float32)
    for j in range(16):
        kp = qpos[128 * j: 128 * j + 128]
        le = kp[:, None] <= qpos[None, 0:1024]
        if le.all():
            bias_mla[:, j] = 0.0
        elif not le.any():
            bias_mla[:, j] = NEG
    d["bias_mla"] = bias_mla
    tri = np.zeros((8, 128, NOWN), BF)
    for j in range(8):
        kk = np.arange(128)[:, None] + 128 * j
        qq = np.arange(NOWN)[None, :]
        tri[j] = (kk <= qq).astype(BF)
    d["tri_mla"] = tri

    d["ident_f32"] = np.eye(128, dtype=np.float32)
    d["iota7"] = np.tile(np.arange(7, dtype=np.float32), (128, 1))
    return d


def _stage_inputs(inputs):
    x = np.asarray(inputs["L0_attn_x"])
    w = {k: np.asarray(v) for k, v in inputs.items()}
    shared = {
        "wq": w["L0_attn_Wq"], "wkv": w["L0_attn_Wkv"], "wo": w["L0_attn_Wo"],
        "kvd": w["L0_ffn_kv_d_w"], "qd": w["L0_ffn_q_d_w"],
        "ku": w["L0_ffn_k_u_w"], "vu": w["L0_ffn_v_u_w"], "qu": w["L0_ffn_q_u_w"],
        "ropek": w["L0_ffn_rope_k_w"], "ropeq": w["L0_ffn_rope_q_w"],
        "ow": w["L0_ffn_o_w"],
        "routerw": np.ascontiguousarray(w["L0_ffn_router_w"]),
        "routerb": np.ascontiguousarray(w["L0_ffn_router_b"].reshape(1, 7)),
        "gate_all": np.ascontiguousarray(np.concatenate(
            [w["L0_ffn_sh_gate"][None], w["L0_ffn_r_gate"]], 0).reshape(NEXP * E, IEXP)),
        "up_all": np.ascontiguousarray(np.concatenate(
            [w["L0_ffn_sh_up"][None], w["L0_ffn_r_up"]], 0).reshape(NEXP * E, IEXP)),
        "down_all": np.ascontiguousarray(np.concatenate(
            [w["L0_ffn_sh_down"][None], w["L0_ffn_r_down"]], 0).reshape(NEXP * IEXP, E)),
    }

    def pc(v):
        return np.ascontiguousarray(v.reshape(16, 128).T)
    shared["bq_pc"] = pc(w["L0_attn_bq"])
    shared["bo_pc"] = pc(w["L0_attn_bo"])
    shared["bkvk_pc"] = np.ascontiguousarray(w["L0_attn_bkv"][0:512].reshape(4, 128).T)
    shared["bkvv_row"] = np.ascontiguousarray(w["L0_attn_bkv"][512:1024].reshape(1, 512))
    shared["ln1w_pc"] = pc(w["L0_ffn_ln1_w"])
    shared["ln2w_pc"] = pc(w["L0_ffn_ln2_w"])
    shared["ln1w_row"] = np.ascontiguousarray(w["L0_ffn_ln1_w"].reshape(1, 2048))

    cvar = [_host_constants(0), _host_constants(1)]
    in_maps = []
    for c in range(8):
        b, h = c // 2, c % 2
        xb = x[b]
        xloc = np.zeros((RLOC, E), np.float32)
        xloc[0:16] = xb[0:16]
        if h == 1:
            xloc[HALO0:HALO0 + 511] = xb[513:1024]
        xloc[OWN0:OWN0 + 1024] = xb[1024 * h: 1024 * h + 1024]
        xloc[REST0:REST0 + 1024] = xb[1024 * (1 - h): 1024 * (1 - h) + 1024]
        m = dict(shared)
        m["xloc"] = np.ascontiguousarray(xloc.T.reshape(16, 128, RLOC))
        m.update(cvar[h])
        in_maps.append(m)
    return in_maps


def build_program():
    nc = bass.Bass("TRN2", target_bir_lowering=False, debug=False)
    Sd = Sched()
    bufs = {}

    def Bf(name):
        if name not in bufs:
            bufs[name] = Buf(name)
        return bufs[name]

    def din(name, shape, dt=F32):
        return nc.dram_tensor(name, shape, dt, kind="ExternalInput").ap()

    X = din("xloc", [16, 128, RLOC])
    WRAW = {nm: din(nm, shape) for nm, shape in [
        ("wq", [E, E]), ("wkv", [E, 1024]), ("wo", [E, E]),
        ("kvd", [E, LAT]), ("qd", [E, LAT]), ("ku", [LAT, 1024]),
        ("vu", [LAT, E]), ("qu", [LAT, 1024]), ("ropek", [E, 1024]),
        ("ropeq", [LAT, 1024]), ("ow", [E, E]), ("routerw", [E, 7]),
        ("gate_all", [NEXP * E, IEXP]), ("up_all", [NEXP * E, IEXP]),
        ("down_all", [NEXP * IEXP, E]),
    ]}
    BQ = din("bq_pc", [128, 16])
    BO = din("bo_pc", [128, 16])
    BKVK = din("bkvk_pc", [128, 4])
    BKVV = din("bkvv_row", [1, 512])
    LN1P = din("ln1w_pc", [128, 16])
    LN2P = din("ln2w_pc", [128, 16])
    LN1R = din("ln1w_row", [1, 2048])
    BROUT = din("routerb", [1, 7])
    COSQ = din("cosq_sp", [128, NQ])
    SINQ = din("sinq_sp", [128, NQ])
    COSK = din("cosk_sp", [128, RLOC])
    SINK = din("sink_sp", [128, RLOC])
    COSM = din("cos_mla", [128, NQ])
    SINM = din("sin_mla", [128, NQ])
    RSPT = din("RspT", [128, 128], BF16)
    RMLAT = din("RmlaT", [128, 128], BF16)
    MWIN = din("mask_win", [4, 9, 128, 512], BF16)
    BMLA = din("bias_mla", [128, 16])
    TRIM = din("tri_mla", [8, 128, NOWN], BF16)
    IDF = din("ident_f32", [128, 128])
    IOTA7 = din("iota7", [128, 7])

    OUT = nc.dram_tensor("out", [16, 128, NOWN], F32, kind="ExternalOutput").ap()

    ikind = "ExternalOutput" if TAPS else "Internal"

    def dint(name, shape, dt=BF16, tap=False):
        return nc.dram_tensor(name, shape, dt,
                              kind=(ikind if tap else "Internal")).ap()

    WBF = {}
    for nm, shape in [
        ("wq", [E, E]), ("wkv", [E, 1024]), ("wo", [E, E]), ("kvd", [E, LAT]),
        ("qd", [E, LAT]), ("ku", [LAT, 1024]), ("vu", [LAT, E]),
        ("qu", [LAT, 1024]), ("ropek", [E, 1024]), ("ropeq", [LAT, 1024]),
        ("ow", [E, E]), ("routerw", [E, 7]),
        ("gate_all", [NEXP * E, IEXP]), ("up_all", [NEXP * E, IEXP]),
        ("down_all", [NEXP * IEXP, E]),
    ]:
        WBF[nm] = dint("wbf_" + nm, shape)

    QSPD = dint("qsp_d", [16, 128, NQ], tap=True)
    KSPD = dint("ksp_d", [4, 128, RLOC], tap=True)
    VSPD = dint("vsp_d", [NBLK, 128, 512], tap=True)
    XMIDD = dint("xmid_d", [16, 128, NQ], F32, tap=True)
    KPD = dint("kp_d", [8, 128, NQ], tap=True)
    KRD = dint("kr_d", [8, 128, NQ], tap=True)
    VMLAD = dint("vmla_d", [16, 128, E], tap=True)
    HDD = dint("h_d", [16, 128, NOWN], F32, tap=True)
    SDD = dint("s_d", [16, 128, NOWN], F32, tap=True)
    OAD = dint("oa_d", [16, 128, NOWN], F32, tap=True)
    Y2D = dint("y2_d", [4, 128, 16 * 512], tap=True)
    PTMD = dint("ptm_d", [128, 5 * 64], F32, tap=True)
    WROWD = dint("wrow_d", [8, 1024], F32, tap=True)
    ROWD = dint("row_d", [8, RLOC], F32, tap=True)

    def rb(ap):
        return ap.rearrange("(n p) c -> n p c", p=128)

    es = contextlib.ExitStack()
    with es:
        sems = {e: es.enter_context(nc.semaphore("sem_" + e)) for e in Sched.ENGS}
        sems_dl = {e: [es.enter_context(nc.semaphore(f"semd_{e}{j}"))
                       for j in range(NLANE)]
                   for e in ("sync", "act")}
        sb = {}

        def alloc(name, shape, dt=F32, stack=None):
            t = (stack or es).enter_context(nc.sbuf_tensor(name, shape, dt))
            sb[name] = t
            return t

        def psum(name, shape, dt=F32, stack=None):
            t = (stack or es).enter_context(nc.psum_tensor(name, shape, dt))
            sb[name] = t
            return t

        def load_packed(name, nparts, n, q, dt, src3, stack=None):
            t = alloc(name, [nparts, n * q], dt, stack)
            Sd.op("sync", lambda e, t=t, src3=src3, n=n, q=q: e.dma_start(
                out=t[:].rearrange("p (n q) -> p n q", q=q), in_=src3),
                writes=[Bf(name)], dma=True)
            return t

        def load_const(name, shape, dt, src, stack=None):
            t = alloc(name, shape, dt, stack)
            Sd.op("sync", lambda e, t=t, src=src: e.dma_start(out=t[:], in_=src),
                  writes=[Bf(name)], dma=True)
            return t

        load_const("c_bq", [128, 16], F32, BQ)
        Sd.op("dve", lambda e: e.tensor_scalar_mul(sb["c_bq"][:], sb["c_bq"][:], SCALE),
              reads=[Bf("c_bq")], writes=[Bf("c_bq")])
        load_const("c_bo", [128, 16], F32, BO)
        load_const("c_bkvk", [128, 4], F32, BKVK)
        load_const("c_ln1p", [128, 16], F32, LN1P)
        load_const("c_ln2p", [128, 16], F32, LN2P)
        load_const("c_bmla", [128, 16], F32, BMLA)
        load_const("c_rspt", [128, 128], BF16, RSPT)
        load_const("c_rmlat", [128, 128], BF16, RMLAT)
        load_const("c_idf", [128, 128], F32, IDF)
        load_const("c_iota7", [128, 7], F32, IOTA7)
        load_const("c_bkvv", [1, 512], F32, BKVV)
        load_const("c_brout", [1, 7], F32, BROUT)
        for nm, src in [("c_bkvv_bf", "c_bkvv"), ("c_brout_bf", "c_brout")]:
            t = alloc(nm, list(sb[src].shape), BF16)
            Sd.op("dve", lambda e, t=t, s=sb[src]: e.tensor_copy(t[:], s[:]),
                  reads=[Bf(src)], writes=[Bf(nm)])
        for nm, val, shape, dt in [("c_ones1", 1.0, [1, 128], BF16),
                                   ("c_ones1f", 1.0, [1, 128], F32),
                                   ("c_ones128", 1.0, [128, 1], BF16),
                                   ("c_ones128f", 1.0, [128, 1], F32),
                                   ("c_eps", EPS, [1, 1], F32),
                                   ("c_shw", 0.125, [1, NOWN], F32)]:
            t = alloc(nm, shape, dt)
            Sd.op("dve", lambda e, t=t, v=val: e.memset(t[:], v), writes=[Bf(nm)])

        def row_coefs(rm, rmB, re_, reB, rc, rcB, ra, raB, t1, t1B, t2, t2B, width):
            Sd.op("dve", lambda e: e.tensor_scalar_mul(rm[:, 0:width], rm[:, 0:width], 1.0 / E),
                  reads=[rmB], writes=[rmB])
            Sd.op("dve", lambda e: e.tensor_scalar_mul(re_[:, 0:width], re_[:, 0:width], 1.0 / E),
                  reads=[reB], writes=[reB])
            Sd.op("dve", lambda e: e.tensor_mul(t1[:, 0:width], rm[:, 0:width], rm[:, 0:width]),
                  reads=[rmB], writes=[t1B])
            Sd.op("dve", lambda e: e.tensor_sub(t1[:, 0:width], re_[:, 0:width], t1[:, 0:width]),
                  reads=[reB], writes=[t1B])
            Sd.op("act", lambda e: e.activation(t2[:, 0:width], t1[:, 0:width], AF.Sqrt, bias=sb["c_eps"][:]),
                  reads=[t1B, Bf("c_eps")], writes=[t2B])
            Sd.op("dve", lambda e: e.reciprocal(rc[:, 0:width], t2[:, 0:width]),
                  reads=[t2B], writes=[rcB])
            Sd.op("dve", lambda e: e.tensor_mul(ra[:, 0:width], rm[:, 0:width], rc[:, 0:width]),
                  reads=[rmB, rcB], writes=[raB])
            Sd.op("dve", lambda e: e.tensor_scalar_mul(ra[:, 0:width], ra[:, 0:width], -1.0),
                  reads=[raB], writes=[raB])

        def rope4(rawf_ap, rawfbuf, rawb_ap, rawbbuf, cos_ap, sin_ap,
                  out_ap, outbuf, w_, rot_lhs, ps_rot, psrotbuf,
                  t1, t1buf, t2, t2buf, trigbuf, trigbuf2=None):
            Sd.op("dve", lambda e: e.tensor_copy(rawb_ap, rawf_ap),
                  reads=[rawfbuf], writes=[rawbbuf])
            Sd.op("pe", lambda e: e.matmul(ps_rot[:, 0:w_], sb[rot_lhs][:],
                                           rawb_ap, start=True, stop=True),
                  reads=[rawbbuf, Bf(rot_lhs)], writes=[psrotbuf])
            Sd.op("dve", lambda e: e.tensor_mul(t1[:, 0:w_], rawf_ap, cos_ap),
                  reads=[rawfbuf, trigbuf], writes=[t1buf])
            Sd.op("dve", lambda e: e.scalar_tensor_tensor(
                t2[:, 0:w_], ps_rot[:, 0:w_], 1.0, sin_ap, ALU.mult, ALU.mult),
                reads=[psrotbuf, trigbuf2 or trigbuf], writes=[t2buf])
            Sd.op("pool", lambda e: e.tensor_add(out_ap, t1[:, 0:w_], t2[:, 0:w_]),
                  reads=[t1buf, t2buf], writes=[outbuf])

        p0s = contextlib.ExitStack()

        wfp = [alloc(f"p0wf{i}", [128, 1024], F32) for i in range(2)]
        wbp = [alloc(f"p0wb{i}", [128, 1024], BF16) for i in range(2)]
        pump_tiles = []
        pump_end = {}
        for nm in ["wkv", "wq", "wo", "kvd", "ku", "vu", "ropek", "qd",
                   "qu", "ropeq", "ow", "routerw", "gate_all", "up_all",
                   "down_all"]:
            wsrc = WRAW[nm]
            rows, cols = wsrc.tensor.shape
            scaled = nm in ("qu", "ropeq", "wq")
            sv, dv = rb(wsrc), rb(WBF[nm])
            for rblk in range(rows // 128):
                for c0_ in range(0, cols, 1024):
                    cw = min(1024, cols - c0_)
                    pump_tiles.append((nm, sv, dv, rblk, c0_, cw, scaled))
            pump_end[nm] = len(pump_tiles)
        pump_pos = [0]
        pump_pend = [None]

        def _pump_one():
            ti = pump_pos[0]
            nm, sv, dv, rblk, c0_, cw, scaled = pump_tiles[ti]
            pump_pos[0] = ti + 1
            sl = ti % 2
            Sd.op("sync", lambda e, sl=sl, sv=sv, rblk=rblk, c0_=c0_, cw=cw:
                  e.dma_start(out=wfp[sl][:, 0:cw],
                              in_=sv[rblk][:, c0_:c0_ + cw]),
                  writes=[Bf(f"p0wf{sl}")], dma=True)
            if scaled:
                Sd.op("dve", lambda e, sl=sl, cw=cw:
                      e.tensor_scalar_mul(wbp[sl][:, 0:cw], wfp[sl][:, 0:cw], SCALE),
                      reads=[Bf(f"p0wf{sl}")], writes=[Bf(f"p0wb{sl}")])
            elif ti % 2 == 0:
                Sd.op("dve", lambda e, sl=sl, cw=cw:
                      e.tensor_copy(wbp[sl][:, 0:cw], wfp[sl][:, 0:cw]),
                      reads=[Bf(f"p0wf{sl}")], writes=[Bf(f"p0wb{sl}")])
            else:
                Sd.op("act", lambda e, sl=sl, cw=cw:
                      e.copy(wbp[sl][:, 0:cw], wfp[sl][:, 0:cw]),
                      reads=[Bf(f"p0wf{sl}")], writes=[Bf(f"p0wb{sl}")])
            if pump_pend[0] is not None:
                pump_pend[0]()
            def _store(sl=sl, nm=nm, dv=dv, rblk=rblk, c0_=c0_, cw=cw):
                Sd.op("act", lambda e, sl=sl, dv=dv, rblk=rblk, c0_=c0_, cw=cw:
                      e.dma_start(out=dv[rblk][:, c0_:c0_ + cw],
                                  in_=wbp[sl][:, 0:cw]),
                      reads=[Bf(f"p0wb{sl}")], writes=[Bf("wbf_" + nm)], dma=True)
            pump_pend[0] = _store

        def pump(k):
            for _ in range(k):
                if pump_pos[0] < len(pump_tiles):
                    _pump_one()

        def pump_until(nm):
            while pump_pos[0] < pump_end[nm]:
                _pump_one()
            if pump_pend[0] is not None:
                pump_pend[0]()
                pump_pend[0] = None

        def phase0():
            xs = [alloc(f"p0x{i}", [128, 512], F32, p0s) for i in range(2)]
            xq = [alloc(f"p0q{i}", [128, 512], F32, p0s) for i in range(2)]
            r_m = alloc("p0rm", [1, RLOC], F32, p0s)
            r_e = alloc("p0re", [1, RLOC], F32, p0s)
            r_t1 = alloc("p0t1", [1, RLOC], F32, p0s)
            r_t2 = alloc("p0t2", [1, RLOC], F32, p0s)
            ps_m = psum("p0m", [1, 512], F32)
            ps_e = psum("p0e", [1, 512], F32)
            st = 0
            for col0 in range(0, RLOC, 512):
                w_ = min(512, RLOC - col0)
                for c in range(16):
                    sl = st % 2
                    st += 1
                    Sd.op("sync", lambda e, sl=sl, c=c, col0=col0, w_=w_:
                          e.dma_start(out=xs[sl][:, 0:w_], in_=X[c, :, col0:col0 + w_]),
                          writes=[Bf(f"p0xx{sl}")], dma=True)
                    Sd.op("pool", lambda e, sl=sl, w_=w_:
                          e.tensor_mul(xq[sl][:, 0:w_], xs[sl][:, 0:w_], xs[sl][:, 0:w_]),
                          reads=[Bf(f"p0xx{sl}")], writes=[Bf(f"p0qq{sl}")])
                    Sd.op("pe", lambda e, sl=sl, c=c, w_=w_:
                          e.matmul(ps_m[:, 0:w_], sb["c_ones128f"][:], xs[sl][:, 0:w_],
                                   start=(c == 0), stop=(c == 15)),
                          reads=[Bf(f"p0xx{sl}"), Bf("c_ones128f")], writes=[Bf("p0m")])
                    Sd.op("pe", lambda e, sl=sl, c=c, w_=w_:
                          e.matmul(ps_e[:, 0:w_], sb["c_ones128f"][:], xq[sl][:, 0:w_],
                                   start=(c == 0), stop=(c == 15)),
                          reads=[Bf(f"p0qq{sl}"), Bf("c_ones128f")], writes=[Bf("p0e")])
                    pump(1)
                Sd.op("act", lambda e, col0=col0, w_=w_:
                      e.copy(r_m[:, col0:col0 + w_], ps_m[:, 0:w_]),
                      reads=[Bf("p0m")], writes=[Bf("p0rm")])
                Sd.op("act", lambda e, col0=col0, w_=w_:
                      e.copy(r_e[:, col0:col0 + w_], ps_e[:, 0:w_]),
                      reads=[Bf("p0e")], writes=[Bf("p0re")])
            row_coefs(r_m, Bf("p0rm"), r_e, Bf("p0re"),
                      r_t2, Bf("p0t2"), r_e, Bf("p0re"),
                      r_t1, Bf("p0t1"), r_t2, Bf("p0t2"), RLOC)
            Sd.op("act", lambda e: e.dma_start(out=ROWD[0:1, :], in_=r_t2[:]),
                  reads=[Bf("p0t2")], writes=[Bf("row_c0")], dma=True)
            Sd.op("act", lambda e: e.dma_start(out=ROWD[1:2, :], in_=r_e[:]),
                  reads=[Bf("p0re")], writes=[Bf("row_a0")], dma=True)

        phase0()

        def phase1():
          with contextlib.ExitStack() as p1:
            tck = alloc("p1tck", [128, 640], F32, p1)
            tsk = alloc("p1tsk", [128, 640], F32, p1)
            tcq = alloc("p1tcq", [128, 512], F32, p1)
            tsq = alloc("p1tsq", [128, 512], F32, p1)
            rc0 = alloc("p1c0", [1, RLOC], F32, p1)
            ra0 = alloc("p1a0", [1, RLOC], F32, p1)
            Sd.op("sync", lambda e: e.dma_start(out=rc0[:], in_=ROWD[0:1, :]),
                  reads=[Bf("row_c0")], writes=[Bf("p1c0")], dma=True)
            Sd.op("sync", lambda e: e.dma_start(out=ra0[:], in_=ROWD[1:2, :]),
                  reads=[Bf("row_a0")], writes=[Bf("p1a0")], dma=True)

            wkvl = alloc("p1wkv", [128, 16 * 1024], BF16, p1)
            pump_until("wkv")
            for c in range(16):
                Sd.op("sync", lambda e, c=c: e.dma_start(
                    out=wkvl[:, 1024 * c:1024 * (c + 1)], in_=rb(WBF["wkv"])[c]),
                    reads=[Bf("wbf_wkv")], writes=[Bf("p1wkv")], dma=True)

            ln0 = [alloc("p1ln0", [128, 16 * 640], BF16, p1),
                   alloc("p1ln1", [128, 16 * 512], BF16, p1)]
            xs = [alloc(f"p1x{i}", [128, 640], F32, p1) for i in range(3)]
            wqt = [alloc(f"p1wqt{i}", [128, 256], BF16, p1) for i in range(2)]
            ps_bc = psum("p1bc", [128, 1024], F32, p1)
            ps_q = [psum(f"p1q{i}", [128, 512], F32, p1) for i in range(2)]
            ps_a = psum("p1a_", [128, 512], F32, p1)
            ps_b = psum("p1b_", [128, 512], F32, p1)
            scf = alloc("p1scf", [128, 512], F32, p1)
            rawf = [alloc(f"p1rawf{i}", [128, 512], F32, p1) for i in range(2)]
            rawb = [alloc(f"p1rawb{i}", [128, 512], BF16, p1) for i in range(2)]
            t1f = [alloc(f"p1t1f{i}", [128, 512], F32, p1) for i in range(2)]
            t2f = [alloc(f"p1t2f{i}", [128, 512], F32, p1) for i in range(2)]
            stg = [alloc(f"p1stg{i}", [128, 512], BF16, p1) for i in range(2)]

            for G in range(5):
                blocks = list(range(0, 5)) if G == 0 else list(range(4 * G + 1, 4 * G + 5))
                col0 = 128 * blocks[0]
                w_ = 128 * len(blocks)
                lsl = G % 2
                lstr = 640 if lsl == 0 else 512
                Sd.op("sync", lambda e, col0=col0, w_=w_: e.dma_start(
                    out=tck[:, 0:w_], in_=COSK[:, col0:col0 + w_]),
                    writes=[Bf("ctck")], dma=True)
                Sd.op("sync", lambda e, col0=col0, w_=w_: e.dma_start(
                    out=tsk[:, 0:w_], in_=SINK[:, col0:col0 + w_]),
                    writes=[Bf("ctsk")], dma=True)
                if G >= 1:
                    g_ = G - 1
                    Sd.op("sync", lambda e, g_=g_: e.dma_start(
                        out=tcq[:], in_=COSQ[:, 512 * g_:512 * g_ + 512]),
                        writes=[Bf("ctcq")], dma=True)
                    Sd.op("sync", lambda e, g_=g_: e.dma_start(
                        out=tsq[:], in_=SINQ[:, 512 * g_:512 * g_ + 512]),
                        writes=[Bf("ctsq")], dma=True)
                subs = [(0, 512), (512, 128)] if G == 0 else [(0, 512)]
                for (s0, sw) in subs:
                    Sd.op("pe", lambda e, col0=col0, s0=s0, sw=sw:
                          e.matmul(ps_bc[:, 0:sw], sb["c_ones1f"][:],
                                   rc0[:, col0 + s0:col0 + s0 + sw],
                                   start=True, stop=True),
                          reads=[Bf("p1c0"), Bf("c_ones1f")], writes=[Bf("p1bc")])
                    Sd.op("pe", lambda e, col0=col0, s0=s0, sw=sw:
                          e.matmul(ps_bc[:, 512:512 + sw], sb["c_ones1f"][:],
                                   ra0[:, col0 + s0:col0 + s0 + sw],
                                   start=True, stop=True),
                          reads=[Bf("p1a0"), Bf("c_ones1f")], writes=[Bf("p1bc")])
                    for c in range(16):
                        xsl = (G * 16 + c) % 3
                        Sd.op("sync", lambda e, c=c, xsl=xsl, col0=col0, s0=s0, sw=sw:
                              e.dma_start(out=xs[xsl][:, 0:sw],
                                          in_=X[c, :, col0 + s0:col0 + s0 + sw]),
                              writes=[Bf(f"p1x{xsl}")], dma=True)
                        Sd.op("dve", lambda e, xsl=xsl, sw=sw:
                              e.tensor_tensor(scf[:, 0:sw], xs[xsl][:, 0:sw],
                                              ps_bc[:, 0:sw], ALU.mult),
                              reads=[Bf(f"p1x{xsl}"), Bf("p1bc")], writes=[Bf("p1scf")])
                        Sd.op("dve", lambda e, lsl=lsl, lstr=lstr, c=c, s0=s0, sw=sw:
                              e.tensor_tensor(ln0[lsl][:, lstr * c + s0:lstr * c + s0 + sw],
                                              scf[:, 0:sw], ps_bc[:, 512:512 + sw], ALU.add),
                              reads=[Bf("p1scf"), Bf("p1bc")], writes=[Bf(f"p1ln{lsl}")])
                        pump(1)
                for kc in range(4):
                    for si, (s0, sw) in enumerate(subs):
                        pp = (kc + si) % 2
                        psk = [ps_a, ps_b][pp]
                        pskb = Bf(f"p1ab{pp}")
                        for c in range(16):
                            Sd.op("pe", lambda e, kc=kc, c=c, lsl=lsl, lstr=lstr, s0=s0, sw=sw, psk=psk:
                                  e.matmul(psk[:, 0:sw],
                                           wkvl[:, 1024 * c + 128 * kc:1024 * c + 128 * kc + 128],
                                           ln0[lsl][:, lstr * c + s0:lstr * c + s0 + sw],
                                           start=(c == 0), stop=(c == 15)),
                                  reads=[Bf("p1wkv"), Bf(f"p1ln{lsl}")], writes=[pskb])
                        Sd.op("act", lambda e, kc=kc, sw=sw, psk=psk, pp=pp:
                              e.add(rawf[pp][:, 0:sw], psk[:, 0:sw],
                                    sb["c_bkvk"][:, kc:kc + 1]),
                              reads=[pskb, Bf("c_bkvk")], writes=[Bf(f"p1rawf{pp}")])
                        rope4(rawf[pp][:, 0:sw], Bf(f"p1rawf{pp}"),
                              rawb[pp][:, 0:sw], Bf(f"p1rawb{pp}"),
                              tck[:, s0:s0 + sw], tsk[:, s0:s0 + sw],
                              stg[pp][:, 0:sw], Bf(f"p1stg{pp}"), sw,
                              "c_rspt", ps_bc, Bf("p1bc"),
                              t1f[pp], Bf(f"p1t1f{pp}"), t2f[pp], Bf(f"p1t2f{pp}"),
                              Bf("ctck"), Bf("ctsk"))
                        Sd.op("act", lambda e, kc=kc, col0=col0, s0=s0, sw=sw, pp=pp:
                              e.dma_start(out=KSPD[kc, :, col0 + s0:col0 + s0 + sw],
                                          in_=stg[pp][:, 0:sw]),
                              reads=[Bf(f"p1stg{pp}")], writes=[Bf("ksp_done")], dma=True)
                        pump(1)
                for bi, blk in enumerate(blocks):
                    pp = bi % 2
                    psv = [ps_a, ps_b][pp]
                    psvb = Bf(f"p1ab{pp}")
                    for c in range(16):
                        Sd.op("pe", lambda e, c=c, bi=bi, lsl=lsl, lstr=lstr, psv=psv:
                              e.matmul(psv[:],
                                       ln0[lsl][:, lstr * c + 128 * bi:lstr * c + 128 * bi + 128],
                                       wkvl[:, 1024 * c + 512:1024 * c + 1024],
                                       start=(c == 0), stop=False),
                              reads=[Bf(f"p1ln{lsl}"), Bf("p1wkv")], writes=[psvb])
                    Sd.op("pe", lambda e, psv=psv:
                          e.matmul(psv[:], sb["c_ones1"][:], sb["c_bkvv_bf"][:],
                                   start=False, stop=True),
                          reads=[Bf("c_ones1"), Bf("c_bkvv_bf")], writes=[psvb])
                    Sd.op("act", lambda e, pp=pp, psv=psv: e.copy(stg[pp][:], psv[:]),
                          reads=[psvb], writes=[Bf(f"p1stg{pp}")])
                    Sd.op("act", lambda e, blk=blk, pp=pp:
                          e.dma_start(out=VSPD[blk], in_=stg[pp][:]),
                          reads=[Bf(f"p1stg{pp}")], writes=[Bf("vsp_done")], dma=True)
                    pump(1)
                if G >= 1:
                    g = G - 1
                    pump_until("wq")
                    for mg in range(8):
                        for c in range(16):
                            wsl = c % 2
                            Sd.op("sync", lambda e, mg=mg, c=c, wsl=wsl: e.dma_start(
                                out=wqt[wsl][:],
                                in_=rb(WBF["wq"])[c][:, 256 * mg:256 * mg + 256]),
                                reads=[Bf("wbf_wq")], writes=[Bf(f"p1wqt{wsl}")], dma=True)
                            for mi in range(2):
                                Sd.op("pe", lambda e, mi=mi, c=c, lsl=lsl, lstr=lstr, wsl=wsl:
                                      e.matmul(ps_q[mi][:],
                                               wqt[wsl][:, 128 * mi:128 * mi + 128],
                                               ln0[lsl][:, lstr * c:lstr * c + 512],
                                               start=(c == 0), stop=(c == 15)),
                                      reads=[Bf(f"p1wqt{wsl}"), Bf(f"p1ln{lsl}")],
                                      writes=[Bf(f"p1q{mi}")])
                        for mi in range(2):
                            m = 2 * mg + mi
                            pp = mi % 2
                            Sd.op("act", lambda e, m=m, mi=mi, pp=pp:
                                  e.add(rawf[pp][:], ps_q[mi][:], sb["c_bq"][:, m:m + 1]),
                                  reads=[Bf(f"p1q{mi}"), Bf("c_bq")], writes=[Bf(f"p1rawf{pp}")])
                            rope4(rawf[pp][:], Bf(f"p1rawf{pp}"),
                                  rawb[pp][:], Bf(f"p1rawb{pp}"),
                                  tcq[:], tsq[:],
                                  stg[pp][:], Bf(f"p1stg{pp}"), 512,
                                  "c_rspt", ps_bc, Bf("p1bc"),
                                  t1f[pp], Bf(f"p1t1f{pp}"), t2f[pp], Bf(f"p1t2f{pp}"),
                                  Bf("ctcq"), Bf("ctsq"))
                            Sd.op("act", lambda e, m=m, g=g, pp=pp:
                                  e.dma_start(out=QSPD[m, :, 512 * g:512 * g + 512],
                                              in_=stg[pp][:]),
                                  reads=[Bf(f"p1stg{pp}")], writes=[Bf("qsp_done")], dma=True)
                        pump(1)
            Sd.barrier()

        phase1()
        p0s.close()

        def phase2():
          with contextlib.ExitStack() as p2:
            load_packed("c_mwin", 128, 36, 512, BF16,
                        MWIN.rearrange("g r p q -> p (g r) q"), p2)
            qsp = alloc("p2qsp", [128, 16 * 512], BF16, p2)
            ksp = alloc("p2ksp", [128, 4 * 1152], BF16, p2)
            vsp = alloc("p2vsp", [128, 9 * 512], BF16, p2)
            prb2 = [alloc(f"p2prb{i}", [128, 9 * 512], BF16, p2) for i in range(2)]
            y2g = alloc("p2y2", [128, 16 * 512], BF16, p2)
            tmpf = alloc("p2tmpf", [128, 512], F32, p2)
            srow = alloc("p2srow", [1, 512], F32, p2)
            rrow = alloc("p2rrow", [1, 512], F32, p2)
            r2a = alloc("p2r2a", [1, 512], F32, p2)
            r2b = alloc("p2r2b", [1, 512], F32, p2)
            wom = [alloc(f"p2wo{i}", [128, 16 * 128], BF16, p2) for i in range(2)]
            xcc = [alloc(f"p2xc{i}", [128, 512], F32, p2) for i in range(2)]
            xmf = alloc("p2xmf", [128, 512], F32, p2)
            xsq = alloc("p2xsq", [128, 512], F32, p2)
            ps_sc = [psum(f"p2sc{i}", [128, 512], F32, p2) for i in range(2)]
            ps_av = psum("p2av", [128, 512], F32, p2)
            ps_s1 = psum("p2s1", [1, 512], F32, p2)
            ps_rb = psum("p2rb", [128, 512], F32, p2)
            ps_wo = psum("p2wo_", [128, 512], F32, p2)
            ps_st_m, ps_st_e = sb["p0m"], sb["p0e"]

            for g in range(4):
                qc0 = 512 * g
                for m in range(16):
                    Sd.op("sync", lambda e, m=m, qc0=qc0: e.dma_start(
                        out=qsp[:, 512 * m:512 * m + 512],
                        in_=QSPD[m, :, qc0:qc0 + 512]),
                        reads=[Bf("qsp_done")], writes=[Bf("p2qsp")], dma=True)
                for kc in range(4):
                    Sd.op("sync", lambda e, kc=kc: e.dma_start(
                        out=ksp[:, 1152 * kc:1152 * kc + 128], in_=KSPD[kc, :, 0:128]),
                        reads=[Bf("ksp_done")], writes=[Bf("p2ksp")], dma=True)
                    Sd.op("sync", lambda e, kc=kc, g=g: e.dma_start(
                        out=ksp[:, 1152 * kc + 128:1152 * kc + 1152],
                        in_=KSPD[kc, :, 128 * (4 * g + 1):128 * (4 * g + 9)]),
                        reads=[Bf("ksp_done")], writes=[Bf("p2ksp")], dma=True)
                for r in range(9):
                    blk = 0 if r == 0 else 4 * g + r
                    Sd.op("sync", lambda e, r=r, blk=blk: e.dma_start(
                        out=vsp[:, 512 * r:512 * r + 512], in_=VSPD[blk]),
                        reads=[Bf("vsp_done")], writes=[Bf("p2vsp")], dma=True)

                for h in range(16):
                    gv = h // 4
                    prb = prb2[h % 2]
                    prbB = Bf(f"p2prb{h % 2}")
                    for r in range(9):
                        pp = r % 2
                        Sd.op("pe", lambda e, h=h, gv=gv, r=r, pp=pp:
                              e.matmul(ps_sc[pp][:],
                                       ksp[:, 1152 * gv + 128 * r:1152 * gv + 128 * r + 128],
                                       qsp[:, 512 * h:512 * h + 512],
                                       start=True, stop=True),
                              reads=[Bf("p2ksp"), Bf("p2qsp")], writes=[Bf(f"p2sc{pp}")])
                        Sd.op("act", lambda e, r=r, pp=pp, prb=prb:
                              e.activation(prb[:, 512 * r:512 * r + 512], ps_sc[pp][:],
                                           AF.Exp),
                              reads=[Bf(f"p2sc{pp}")], writes=[prbB])
                        Sd.op("dve", lambda e, g=g, r=r, prb=prb:
                              e.tensor_mul(prb[:, 512 * r:512 * r + 512],
                                           prb[:, 512 * r:512 * r + 512],
                                           sb["c_mwin"][:, 4608 * g + 512 * r:4608 * g + 512 * r + 512]),
                              reads=[Bf("c_mwin")], writes=[prbB])
                        Sd.op("pe", lambda e, r=r, prb=prb:
                              e.matmul(ps_s1[:], sb["c_ones128"][:],
                                       prb[:, 512 * r:512 * r + 512],
                                       start=(r == 0), stop=(r == 8)),
                              reads=[prbB, Bf("c_ones128")], writes=[Bf("p2s1")])
                        Sd.op("pe", lambda e, gv=gv, r=r, prb=prb:
                              e.matmul(ps_av[:],
                                       vsp[:, 512 * r + 128 * gv:512 * r + 128 * gv + 128],
                                       prb[:, 512 * r:512 * r + 512],
                                       start=(r == 0), stop=(r == 8)),
                              reads=[Bf("p2vsp"), prbB], writes=[Bf("p2av")])
                    Sd.op("act", lambda e: e.copy(srow[:], ps_s1[:]),
                          reads=[Bf("p2s1")], writes=[Bf("p2srow")])
                    Sd.op("dve", lambda e: e.reciprocal(rrow[:], srow[:]),
                          reads=[Bf("p2srow")], writes=[Bf("p2rrow")])
                    Sd.op("pe", lambda e: e.matmul(ps_rb[:], sb["c_ones1f"][:],
                                                   rrow[:], start=True, stop=True),
                          reads=[Bf("p2rrow"), Bf("c_ones1f")], writes=[Bf("p2rb")])
                    Sd.op("act", lambda e: e.copy(tmpf[:], ps_av[:]),
                          reads=[Bf("p2av")], writes=[Bf("p2tmpf")])
                    Sd.op("dve", lambda e, h=h:
                          e.tensor_tensor(y2g[:, 512 * h:512 * h + 512], tmpf[:],
                                          ps_rb[:], ALU.mult),
                          reads=[Bf("p2tmpf"), Bf("p2rb")], writes=[Bf("p2y2")])
                    pump(2)
                if TAPS:
                    Sd.op("act", lambda e, g=g: e.dma_start(
                        out=Y2D[g], in_=y2g[:]),
                        reads=[Bf("p2y2")], writes=[Bf("y2_done")], dma=True)
                pump_until("wo")
                for m in range(16):
                    wsl = m % 2
                    pump(1)
                    Sd.op("sync", lambda e, m=m, wsl=wsl: e.dma_start(
                        out=wom[wsl][:].rearrange("p (n q) -> p n q", q=128),
                        in_=rb(WBF["wo"])[:, :, 128 * m:128 * m + 128]
                        .rearrange("n p q -> p n q")),
                        reads=[Bf("wbf_wo")], writes=[Bf(f"p2wo{wsl}")], dma=True)
                    for c in range(16):
                        Sd.op("pe", lambda e, c=c, wsl=wsl:
                              e.matmul(ps_wo[:], wom[wsl][:, 128 * c:128 * c + 128],
                                       y2g[:, 512 * c:512 * c + 512],
                                       start=(c == 0), stop=(c == 15)),
                              reads=[Bf(f"p2wo{wsl}"), Bf("p2y2")], writes=[Bf("p2wops")])
                    xsl = m % 2
                    Sd.op("sync", lambda e, m=m, qc0=qc0, xsl=xsl: e.dma_start(
                        out=xcc[xsl][:], in_=X[m, :, OWN0 + qc0:OWN0 + qc0 + 512]),
                        writes=[Bf(f"p2xc{xsl}")], dma=True)
                    Sd.op("dve", lambda e, m=m, xsl=xsl:
                          e.scalar_tensor_tensor(xmf[:], ps_wo[:],
                                                 sb["c_bo"][:, m:m + 1], xcc[xsl][:],
                                                 ALU.add, ALU.add),
                          reads=[Bf("p2wops"), Bf("c_bo"), Bf(f"p2xc{xsl}")],
                          writes=[Bf("p2xmf")])
                    Sd.op("act", lambda e, m=m, qc0=qc0: e.dma_start(
                        out=XMIDD[m, :, qc0:qc0 + 512], in_=xmf[:]),
                        reads=[Bf("p2xmf")], writes=[Bf("xmid_done")], dma=True)
                    Sd.op("pool", lambda e: e.tensor_mul(xsq[:], xmf[:], xmf[:]),
                          reads=[Bf("p2xmf")], writes=[Bf("p2xsq")])
                    Sd.op("pe", lambda e, m=m, ps_st_m=ps_st_m:
                          e.matmul(ps_st_m[:], sb["c_ones128f"][:], xmf[:],
                                   start=(m == 0), stop=(m == 15)),
                          reads=[Bf("p2xmf"), Bf("c_ones128f")], writes=[Bf("p0m")])
                    Sd.op("pe", lambda e, m=m, ps_st_e=ps_st_e:
                          e.matmul(ps_st_e[:], sb["c_ones128f"][:], xsq[:],
                                   start=(m == 0), stop=(m == 15)),
                          reads=[Bf("p2xsq"), Bf("c_ones128f")], writes=[Bf("p0e")])
                Sd.op("act", lambda e, ps_st_m=ps_st_m: e.copy(r2a[:], ps_st_m[:]),
                      reads=[Bf("p0m")], writes=[Bf("p2r2a")])
                Sd.op("act", lambda e, ps_st_e=ps_st_e: e.copy(r2b[:], ps_st_e[:]),
                      reads=[Bf("p0e")], writes=[Bf("p2r2b")])
                Sd.op("act", lambda e, qc0=qc0: e.dma_start(
                    out=ROWD[2:3, qc0:qc0 + 512], in_=r2a[:]),
                    reads=[Bf("p2r2a")], writes=[Bf("row_mf")], dma=True)
                Sd.op("act", lambda e, qc0=qc0: e.dma_start(
                    out=ROWD[3:4, qc0:qc0 + 512], in_=r2b[:]),
                    reads=[Bf("p2r2b")], writes=[Bf("row_ef")], dma=True)
            Sd.barrier()

        phase2()

        def phase34():
          with contextlib.ExitStack() as p34:
            qp_sb = alloc("qp_sb", [128, 8 * 1024], BF16, p34)
            qr_sb = alloc("qr_sb", [128, 8 * 1024], BF16, p34)
            rc1 = alloc("p3c1", [1, NQ], F32, p34)
            rac1 = alloc("p3ac1", [1, NQ], F32, p34)

            with contextlib.ExitStack() as p3:
                r3m = alloc("p3rm", [1, NQ], F32, p3)
                r3e = alloc("p3re", [1, NQ], F32, p3)
                r3t1 = alloc("p3t1", [1, NQ], F32, p3)
                r3sf = alloc("p3sf", [1, NQ], F32, p3)
                r3asf = alloc("p3asf", [1, NQ], F32, p3)
                r3t2 = r3sf
                Sd.op("sync", lambda e: e.dma_start(out=r3m[:], in_=ROWD[2:3, 0:NQ]),
                      reads=[Bf("row_mf")], writes=[Bf("p3rm")], dma=True)
                Sd.op("sync", lambda e: e.dma_start(out=r3e[:], in_=ROWD[3:4, 0:NQ]),
                      reads=[Bf("row_ef")], writes=[Bf("p3re")], dma=True)
                row_coefs(r3m, Bf("p3rm"), r3e, Bf("p3re"),
                          r3sf, Bf("p3sf"), r3asf, Bf("p3asf"),
                          r3t1, Bf("p3t1"), r3t2, Bf("p3sf"), NQ)
                Sd.op("dve", lambda e: e.tensor_scalar(r3m[:], r3t1[:], EPS, None, ALU.add),
                      reads=[Bf("p3t1")], writes=[Bf("p3rm")])
                Sd.op("dve", lambda e: e.reciprocal(r3m[:], r3m[:]),
                      reads=[Bf("p3rm")], writes=[Bf("p3rm")])
                Sd.op("dve", lambda e: e.tensor_mul(r3m[:], r3t1[:], r3m[:]),
                      reads=[Bf("p3t1")], writes=[Bf("p3rm")])
                Sd.op("act", lambda e: e.activation(r3m[:], r3m[:], AF.Sqrt, bias=sb["c_eps"][:]),
                      reads=[Bf("p3rm"), Bf("c_eps")], writes=[Bf("p3rm")])
                Sd.op("dve", lambda e: e.reciprocal(r3m[:], r3m[:]),
                      reads=[Bf("p3rm")], writes=[Bf("p3rm")])
                Sd.op("dve", lambda e: e.tensor_mul(rc1[:], r3sf[:], r3m[:]),
                      reads=[Bf("p3sf"), Bf("p3rm")], writes=[Bf("p3c1")])
                Sd.op("dve", lambda e: e.tensor_mul(rac1[:], r3asf[:], r3m[:]),
                      reads=[Bf("p3asf"), Bf("p3rm")], writes=[Bf("p3ac1")])
                Sd.op("act", lambda e: e.dma_start(out=ROWD[4:5, 0:NQ], in_=r3sf[:]),
                      reads=[Bf("p3sf")], writes=[Bf("row_sf")], dma=True)
                Sd.op("act", lambda e: e.dma_start(out=ROWD[5:6, 0:NQ], in_=r3asf[:]),
                      reads=[Bf("p3asf")], writes=[Bf("row_asf")], dma=True)
                ln1rf = load_const("c_ln1r", [1, 2048], F32, LN1R, p3)
                tcm = alloc("p3tcm", [128, 512], F32, p3)
                tsm = alloc("p3tsm", [128, 512], F32, p3)

                kul = alloc("p3ku", [128, 4 * 1024], BF16, p3)
                qul = alloc("p3qu", [128, 4 * 1024], BF16, p3)
                rql = alloc("p3rq", [128, 4 * 1024], BF16, p3)
                pump_until("ropeq")
                for nm, t in [("ku", kul), ("qu", qul), ("ropeq", rql)]:
                    for c in range(4):
                        Sd.op("sync", lambda e, nm=nm, t=t, c=c: e.dma_start(
                            out=t[:, 1024 * c:1024 * (c + 1)], in_=rb(WBF[nm])[c]),
                            reads=[Bf("wbf_" + nm)], writes=[Bf("p3w_" + nm)], dma=True)

                xmc = [alloc(f"p3xm{i}", [128, 512], F32, p3) for i in range(2)]
                rms1 = alloc("p3rms1", [128, 16 * 512], BF16, p3)
                kvdb = alloc("p3kvd", [128, 4 * 512], BF16, p3)
                qdb = alloc("p3qd", [128, 4 * 512], BF16, p3)
                stg = [alloc(f"p3stg{i}", [128, 512], BF16, p3) for i in range(2)]
                rawf = [alloc(f"p3rawf{i}", [128, 512], F32, p3) for i in range(2)]
                rawb = [alloc(f"p3rawb{i}", [128, 512], BF16, p3) for i in range(2)]
                t1f = [alloc(f"p3t1f{i}", [128, 512], F32, p3) for i in range(2)]
                t2f = [alloc(f"p3t2f{i}", [128, 512], F32, p3) for i in range(2)]
                wst = [alloc(f"p3wst{i}", [128, 2048], BF16, p3) for i in range(2)]
                ps_bc = psum("p3bc", [128, 1024], F32, p3)
                ps_a = psum("p3a_", [128, 512], F32, p3)
                ps_b = psum("p3b_", [128, 512], F32, p3)
                ps_rot = psum("p3rot", [128, 512], F32, p3)

                for g in range(4):
                    qc0 = 512 * g
                    Sd.op("sync", lambda e, qc0=qc0: e.dma_start(
                        out=tcm[:], in_=COSM[:, qc0:qc0 + 512]),
                        writes=[Bf("ctcm")], dma=True)
                    Sd.op("sync", lambda e, qc0=qc0: e.dma_start(
                        out=tsm[:], in_=SINM[:, qc0:qc0 + 512]),
                        writes=[Bf("ctsm")], dma=True)
                    Sd.op("pe", lambda e, qc0=qc0: e.matmul(
                        ps_bc[:, 0:512], sb["c_ones1f"][:], rc1[:, qc0:qc0 + 512],
                        start=True, stop=True),
                        reads=[Bf("p3c1"), Bf("c_ones1f")], writes=[Bf("p3bc")])
                    for c in range(16):
                        xsl = c % 2
                        Sd.op("sync", lambda e, c=c, qc0=qc0, xsl=xsl: e.dma_start(
                            out=xmc[xsl][:], in_=XMIDD[c, :, qc0:qc0 + 512]),
                            reads=[Bf("xmid_done")], writes=[Bf(f"p3xm{xsl}")], dma=True)
                        Sd.op("pe", lambda e, c=c, qc0=qc0: e.matmul(
                            ps_bc[:, 512:1024], ln1rf[:, 128 * c:128 * c + 128],
                            rac1[:, qc0:qc0 + 512], start=True, stop=True),
                            reads=[Bf("p3ac1"), Bf("c_ln1r")], writes=[Bf("p3bc2")])
                        Sd.op("dve", lambda e, xsl=xsl: e.tensor_tensor(
                            xmc[xsl][:], xmc[xsl][:], ps_bc[:, 0:512], ALU.mult),
                            reads=[Bf("p3bc")], writes=[Bf(f"p3xm{xsl}")])
                        Sd.op("dve", lambda e, c=c, xsl=xsl: e.scalar_tensor_tensor(
                            rms1[:, 512 * c:512 * c + 512], xmc[xsl][:],
                            sb["c_ln1p"][:, c:c + 1], ps_bc[:, 512:1024],
                            ALU.mult, ALU.add),
                            reads=[Bf(f"p3xm{xsl}"), Bf("c_ln1p"), Bf("p3bc2")],
                            writes=[Bf("p3rms1")])
                    projs = [("kvd", kvdb, "p3kvdb")]
                    if g < 2:
                        projs.append(("qd", qdb, "p3qdb"))
                    for nm, dstt, dstb in projs:
                        for mc in range(4):
                            wsl = mc % 2
                            Sd.op("sync", lambda e, nm=nm, mc=mc, wsl=wsl: e.dma_start(
                                out=wst[wsl][:].rearrange("p (n q) -> p n q", q=128),
                                in_=rb(WBF[nm])[:, :, 128 * mc:128 * mc + 128]
                                .rearrange("n p q -> p n q")),
                                reads=[Bf("wbf_" + nm)], writes=[Bf(f"p3wst{wsl}")], dma=True)
                            pp = mc % 2
                            psx = [ps_a, ps_b][pp]
                            for c in range(16):
                                Sd.op("pe", lambda e, c=c, wsl=wsl, psx=psx: e.matmul(
                                    psx[:], wst[wsl][:, 128 * c:128 * c + 128],
                                    rms1[:, 512 * c:512 * c + 512],
                                    start=(c == 0), stop=(c == 15)),
                                    reads=[Bf(f"p3wst{wsl}"), Bf("p3rms1")],
                                    writes=[Bf(f"p3ab{pp}")])
                            Sd.op("act", lambda e, dstt=dstt, mc=mc, psx=psx: e.copy(
                                dstt[:, 512 * mc:512 * mc + 512], psx[:]),
                                reads=[Bf(f"p3ab{pp}")], writes=[Bf(dstb)])
                    for pc in range(8):
                        pp = pc % 2
                        psx = [ps_a, ps_b][pp]
                        for kc in range(4):
                            Sd.op("pe", lambda e, pc=pc, kc=kc, psx=psx: e.matmul(
                                psx[:], kul[:, 1024 * kc + 128 * pc:1024 * kc + 128 * pc + 128],
                                kvdb[:, 512 * kc:512 * kc + 512],
                                start=(kc == 0), stop=(kc == 3)),
                                reads=[Bf("p3w_ku"), Bf("p3kvdb")], writes=[Bf(f"p3ab{pp}")])
                        Sd.op("act", lambda e, pp=pp, psx=psx: e.copy(stg[pp][:], psx[:]),
                              reads=[Bf(f"p3ab{pp}")], writes=[Bf(f"p3stg{pp}")])
                        Sd.op("act", lambda e, pc=pc, qc0=qc0, pp=pp: e.dma_start(
                            out=KPD[pc, :, qc0:qc0 + 512], in_=stg[pp][:]),
                            reads=[Bf(f"p3stg{pp}")], writes=[Bf("kp_done")], dma=True)
                        pump(1)
                    for pc in range(8):
                        wsl = pc % 2
                        pp = pc % 2
                        psx = [ps_a, ps_b][pp]
                        Sd.op("sync", lambda e, pc=pc, wsl=wsl: e.dma_start(
                            out=wst[wsl][:].rearrange("p (n q) -> p n q", q=128),
                            in_=rb(WBF["ropek"])[:, :, 128 * pc:128 * pc + 128]
                            .rearrange("n p q -> p n q")),
                            reads=[Bf("wbf_ropek")], writes=[Bf(f"p3wst{wsl}")], dma=True)
                        for c in range(16):
                            Sd.op("pe", lambda e, c=c, wsl=wsl, psx=psx: e.matmul(
                                psx[:], wst[wsl][:, 128 * c:128 * c + 128],
                                rms1[:, 512 * c:512 * c + 512],
                                start=(c == 0), stop=(c == 15)),
                                reads=[Bf(f"p3wst{wsl}"), Bf("p3rms1")],
                                writes=[Bf(f"p3ab{pp}")])
                        Sd.op("act", lambda e, pp=pp, psx=psx: e.copy(rawf[pp][:], psx[:]),
                              reads=[Bf(f"p3ab{pp}")], writes=[Bf(f"p3rawf{pp}")])
                        rope4(rawf[pp][:], Bf(f"p3rawf{pp}"),
                              rawb[pp][:], Bf(f"p3rawb{pp}"),
                              tcm[:], tsm[:],
                              stg[pp][:], Bf(f"p3stg{pp}"), 512,
                              "c_rmlat", ps_rot, Bf("p3rot"),
                              t1f[pp], Bf(f"p3t1f{pp}"), t2f[pp], Bf(f"p3t2f{pp}"),
                              Bf("ctcm"), Bf("ctsm"))
                        Sd.op("act", lambda e, pc=pc, qc0=qc0, pp=pp: e.dma_start(
                            out=KRD[pc, :, qc0:qc0 + 512], in_=stg[pp][:]),
                            reads=[Bf(f"p3stg{pp}")], writes=[Bf("kr_done")], dma=True)
                        pump(1)
                    for vh in range(4):
                        wsl = vh % 2
                        Sd.op("sync", lambda e, vh=vh, wsl=wsl: e.dma_start(
                            out=wst[wsl][:].rearrange("p (n q) -> p n q", q=512),
                            in_=rb(WBF["vu"])[:, :, 512 * vh:512 * vh + 512]
                            .rearrange("n p q -> p n q")),
                            reads=[Bf("wbf_vu")], writes=[Bf(f"p3wst{wsl}")], dma=True)
                        for bk in range(4):
                            pp = bk % 2
                            psx = [ps_a, ps_b][pp]
                            for kc in range(4):
                                Sd.op("pe", lambda e, bk=bk, kc=kc, wsl=wsl, psx=psx:
                                      e.matmul(psx[:],
                                               kvdb[:, 512 * kc + 128 * bk:512 * kc + 128 * bk + 128],
                                               wst[wsl][:, 512 * kc:512 * kc + 512],
                                               start=(kc == 0), stop=(kc == 3)),
                                      reads=[Bf("p3kvdb"), Bf(f"p3wst{wsl}")],
                                      writes=[Bf(f"p3ab{pp}")])
                            Sd.op("act", lambda e, pp=pp, psx=psx: e.copy(stg[pp][:], psx[:]),
                                  reads=[Bf(f"p3ab{pp}")], writes=[Bf(f"p3stg{pp}")])
                            Sd.op("act", lambda e, g=g, bk=bk, vh=vh, pp=pp: e.dma_start(
                                out=VMLAD[4 * g + bk, :, 512 * vh:512 * vh + 512],
                                in_=stg[pp][:]),
                                reads=[Bf(f"p3stg{pp}")], writes=[Bf("vmla_done")], dma=True)
                            pump(1)
                    if g < 2:
                        for pc in range(8):
                            pp = pc % 2
                            psx = [ps_a, ps_b][pp]
                            for kc in range(4):
                                Sd.op("pe", lambda e, pc=pc, kc=kc, psx=psx: e.matmul(
                                    psx[:], qul[:, 1024 * kc + 128 * pc:1024 * kc + 128 * pc + 128],
                                    qdb[:, 512 * kc:512 * kc + 512],
                                    start=(kc == 0), stop=(kc == 3)),
                                    reads=[Bf("p3w_qu"), Bf("p3qdb")], writes=[Bf(f"p3ab{pp}")])
                            Sd.op("dve", lambda e, pc=pc, qc0=qc0, psx=psx: e.tensor_copy(
                                qp_sb[:, 1024 * pc + qc0:1024 * pc + qc0 + 512], psx[:]),
                                reads=[Bf(f"p3ab{pp}")], writes=[Bf("qp_sb")])
                        for pc in range(8):
                            pp = pc % 2
                            psx = [ps_a, ps_b][pp]
                            for kc in range(4):
                                Sd.op("pe", lambda e, pc=pc, kc=kc, psx=psx: e.matmul(
                                    psx[:], rql[:, 1024 * kc + 128 * pc:1024 * kc + 128 * pc + 128],
                                    qdb[:, 512 * kc:512 * kc + 512],
                                    start=(kc == 0), stop=(kc == 3)),
                                    reads=[Bf("p3w_ropeq"), Bf("p3qdb")], writes=[Bf(f"p3ab{pp}")])
                            Sd.op("act", lambda e, pp=pp, psx=psx: e.copy(rawf[pp][:], psx[:]),
                                  reads=[Bf(f"p3ab{pp}")], writes=[Bf(f"p3rawf{pp}")])
                            rope4(rawf[pp][:], Bf(f"p3rawf{pp}"),
                                  rawb[pp][:], Bf(f"p3rawb{pp}"),
                                  tcm[:], tsm[:],
                                  stg[pp][:], Bf(f"p3stg{pp}"), 512,
                                  "c_rmlat", ps_rot, Bf("p3rot"),
                                  t1f[pp], Bf(f"p3t1f{pp}"), t2f[pp], Bf(f"p3t2f{pp}"),
                                  Bf("ctcm"), Bf("ctsm"))
                            Sd.op("dve", lambda e, pc=pc, qc0=qc0, pp=pp: e.tensor_copy(
                                qr_sb[:, 1024 * pc + qc0:1024 * pc + qc0 + 512], stg[pp][:]),
                                reads=[Bf(f"p3stg{pp}")], writes=[Bf("qr_sb")])
                Sd.barrier()

            with contextlib.ExitStack() as p4:
                load_packed("c_tri", 128, 8, NOWN, BF16,
                            TRIM.rearrange("j p q -> p j q"), p4)
                y2 = alloc("p4y2", [128, 16 * 1024], BF16, p4)
                r4sf = alloc("p4rsf", [1, NOWN], F32, p4)
                r4asf = alloc("p4rasf", [1, NOWN], F32, p4)
                Sd.op("sync", lambda e: e.dma_start(out=r4sf[:], in_=ROWD[4:5, 0:NOWN]),
                      reads=[Bf("row_sf")], writes=[Bf("p4rsf")], dma=True)
                Sd.op("sync", lambda e: e.dma_start(out=r4asf[:], in_=ROWD[5:6, 0:NOWN]),
                      reads=[Bf("row_asf")], writes=[Bf("p4rasf")], dma=True)

                with contextlib.ExitStack() as p4a:
                    kpr = alloc("p4kpr", [128, 4 * 2048], BF16, p4a)
                    qpr = alloc("p4qpr", [128, 4 * 1024], BF16, p4a)
                    vh_sb = alloc("p4v", [128, 16 * 1024], BF16, p4a)
                    prb2 = [alloc(f"p4prb{i}", [128, 1024], BF16, p4a) for i in range(2)]
                    tmpf = alloc("p4tmpf", [128, 1024], F32, p4a)
                    srow = alloc("p4srow", [1, 1024], F32, p4a)
                    rrow = alloc("p4rrow", [1, 1024], F32, p4a)
                    ps_sc = [psum(f"p4sc{i}", [128, 1024], F32, p4a) for i in range(2)]
                    ps_av = psum("p4av", [128, 1024], F32, p4a)
                    ps_s1h = [sb["p0m"], sb["p0e"]]

                    def load_kq_round(hh, r):
                        for a in range(4 * r, 4 * r + 4):
                            h = 8 * hh + a
                            pc, base = h // 2, 64 * (h % 2)
                            s = a % 4
                            Sd.op("sync", lambda e, s=s, pc=pc, base=base: e.dma_start(
                                out=kpr[0:64, 2048 * s:2048 * (s + 1)],
                                in_=KPD[pc, base:base + 64, :]),
                                reads=[Bf("kp_done")], writes=[Bf("p4kpr")], dma=True)
                            Sd.op("sync", lambda e, s=s, pc=pc, base=base: e.dma_start(
                                out=kpr[64:128, 2048 * s:2048 * (s + 1)],
                                in_=KRD[pc, base:base + 64, :]),
                                reads=[Bf("kr_done")], writes=[Bf("p4kpr")], dma=True)
                            Sd.op("sync", lambda e, s=s, pc=pc, base=base: e.dma_start(
                                out=qpr[0:64, 1024 * s:1024 * (s + 1)],
                                in_=qp_sb[base:base + 64, 1024 * pc:1024 * pc + 1024]),
                                reads=[Bf("qp_sb")], writes=[Bf("p4qpr")], dma=True)
                            Sd.op("sync", lambda e, s=s, pc=pc, base=base: e.dma_start(
                                out=qpr[64:128, 1024 * s:1024 * (s + 1)],
                                in_=qr_sb[base:base + 64, 1024 * pc:1024 * pc + 1024]),
                                reads=[Bf("qr_sb")], writes=[Bf("p4qpr")], dma=True)

                    for hh in range(2):
                        load_kq_round(hh, 0)
                        for j in range(16):
                            Sd.op("sync", lambda e, hh=hh, j=j: e.dma_start(
                                out=vh_sb[:, 1024 * j:1024 * (j + 1)],
                                in_=VMLAD[j, :, 1024 * hh:1024 * hh + 1024]),
                                reads=[Bf("vmla_done")], writes=[Bf("p4v")], dma=True)
                        for h in range(8 * hh, 8 * hh + 8):
                            a = h - 8 * hh
                            if a == 4:
                                load_kq_round(hh, 1)
                            s = a % 4
                            prb = prb2[h % 2]
                            prbB = Bf(f"p4prb{h % 2}")
                            for j in range(16):
                                pp = j % 2
                                for qh in range(2):
                                    Sd.op("pe", lambda e, s=s, j=j, qh=qh, pp=pp:
                                          e.matmul(ps_sc[pp][:, 512 * qh:512 * qh + 512],
                                                   kpr[:, 2048 * s + 128 * j:2048 * s + 128 * j + 128],
                                                   qpr[:, 1024 * s + 512 * qh:1024 * s + 512 * qh + 512],
                                                   start=True, stop=True),
                                          reads=[Bf("p4kpr"), Bf("p4qpr")], writes=[Bf(f"p4sc{pp}")])
                                Sd.op("act", lambda e, j=j, pp=pp, prb=prb:
                                      e.activation(prb[:], ps_sc[pp][:], AF.Exp,
                                                   bias=sb["c_bmla"][:, j:j + 1]),
                                      reads=[Bf(f"p4sc{pp}"), Bf("c_bmla")], writes=[prbB])
                                pump(1)
                                if j < 8:
                                    Sd.op("pool", lambda e, j=j, prb=prb:
                                          e.tensor_mul(prb[:], prb[:],
                                                       sb["c_tri"][:, NOWN * j:NOWN * (j + 1)]),
                                          reads=[Bf("c_tri")], writes=[prbB])
                                for qh in range(2):
                                    Sd.op("pe", lambda e, j=j, qh=qh, prb=prb:
                                          e.matmul(ps_s1h[qh][:],
                                                   sb["c_ones128"][:],
                                                   prb[:, 512 * qh:512 * qh + 512],
                                                   start=(j == 0), stop=(j == 15)),
                                          reads=[prbB, Bf("c_ones128")], writes=[Bf("p0m"), Bf("p0e")])
                                    Sd.op("pe", lambda e, h=h, hh=hh, j=j, qh=qh, prb=prb:
                                          e.matmul(ps_av[:, 512 * qh:512 * qh + 512],
                                                   vh_sb[:, 1024 * j + 128 * (h - 8 * hh):
                                                         1024 * j + 128 * (h - 8 * hh) + 128],
                                                   prb[:, 512 * qh:512 * qh + 512],
                                                   start=(j == 0), stop=(j == 15)),
                                          reads=[Bf("p4v"), prbB], writes=[Bf("p4av")])
                            Sd.op("act", lambda e: e.copy(srow[:, 0:512], ps_s1h[0][:]),
                                  reads=[Bf("p0m")], writes=[Bf("p4srow")])
                            Sd.op("act", lambda e: e.copy(srow[:, 512:1024], ps_s1h[1][:]),
                                  reads=[Bf("p0e")], writes=[Bf("p4srow")])
                            Sd.op("dve", lambda e: e.reciprocal(rrow[:], srow[:]),
                                  reads=[Bf("p4srow")], writes=[Bf("p4rrow")])
                            for qh in range(2):
                                Sd.op("pe", lambda e, qh=qh: e.matmul(
                                    ps_sc[0][:, 512 * qh:512 * qh + 512], sb["c_ones1f"][:],
                                    rrow[:, 512 * qh:512 * qh + 512], start=True, stop=True),
                                    reads=[Bf("p4rrow"), Bf("c_ones1f")], writes=[Bf("p4sc0")])
                            Sd.op("act", lambda e: e.copy(tmpf[:], ps_av[:]),
                                  reads=[Bf("p4av")], writes=[Bf("p4tmpf")])
                            Sd.op("dve", lambda e, h=h: e.tensor_tensor(
                                y2[:, 1024 * h:1024 * (h + 1)], tmpf[:], ps_sc[0][:], ALU.mult),
                                reads=[Bf("p4tmpf"), Bf("p4sc0")], writes=[Bf("p4y2")])
                    Sd.barrier()

                with contextlib.ExitStack() as p4b:
                    wom = [alloc(f"p4wo{i}", [128, 16 * 128], BF16, p4b) for i in range(2)]
                    xmc4 = [alloc(f"p4xm{i}", [128, 1024], F32, p4b) for i in range(2)]
                    hcc = alloc("p4hc", [128, 1024], F32, p4b)
                    scc = alloc("p4scc", [128, 1024], F32, p4b)
                    t2c = alloc("p4t2c", [128, 1024], F32, p4b)
                    ps_ow = psum("p4ow", [128, 1024], F32, p4b)
                    ps_sf = psum("p4sf_", [128, 1024], F32, p4b)
                    ps_asf = psum("p4asf_", [128, 1024], F32, p4b)
                    for qh in range(2):
                        Sd.op("pe", lambda e, qh=qh: e.matmul(
                            ps_sf[:, 512 * qh:512 * qh + 512], sb["c_ones1f"][:],
                            r4sf[:, 512 * qh:512 * qh + 512], start=True, stop=True),
                            reads=[Bf("p4rsf"), Bf("c_ones1f")], writes=[Bf("p4psf")])
                        Sd.op("pe", lambda e, qh=qh: e.matmul(
                            ps_asf[:, 512 * qh:512 * qh + 512], sb["c_ones1f"][:],
                            r4asf[:, 512 * qh:512 * qh + 512], start=True, stop=True),
                            reads=[Bf("p4rasf"), Bf("c_ones1f")], writes=[Bf("p4pasf")])
                    pump_until("ow")
                    for m in range(16):
                        wsl = m % 2
                        pump(1)
                        Sd.op("sync", lambda e, m=m, wsl=wsl: e.dma_start(
                            out=wom[wsl][:].rearrange("p (n q) -> p n q", q=128),
                            in_=rb(WBF["ow"])[:, :, 128 * m:128 * m + 128]
                            .rearrange("n p q -> p n q")),
                            reads=[Bf("wbf_ow")], writes=[Bf(f"p4wo{wsl}")], dma=True)
                        Sd.op("sync", lambda e, m=m, wsl=wsl: e.dma_start(
                            out=xmc4[wsl][:], in_=XMIDD[m, :, 0:1024]),
                            reads=[Bf("xmid_done")], writes=[Bf(f"p4xm{wsl}")], dma=True)
                        for c in range(16):
                            for qh in range(2):
                                Sd.op("pe", lambda e, c=c, qh=qh, wsl=wsl:
                                      e.matmul(ps_ow[:, 512 * qh:512 * qh + 512],
                                               wom[wsl][:, 128 * c:128 * c + 128],
                                               y2[:, 1024 * c + 512 * qh:1024 * c + 512 * qh + 512],
                                               start=(c == 0), stop=(c == 15)),
                                      reads=[Bf(f"p4wo{wsl}"), Bf("p4y2")], writes=[Bf("p4ow")])
                        Sd.op("dve", lambda e, wsl=wsl: e.tensor_tensor(
                            t2c[:], xmc4[wsl][:], ps_sf[:], ALU.mult),
                            reads=[Bf(f"p4xm{wsl}"), Bf("p4psf")], writes=[Bf("p4t2c")])
                        Sd.op("dve", lambda e: e.tensor_tensor(
                            t2c[:], t2c[:], ps_asf[:], ALU.add),
                            reads=[Bf("p4pasf")], writes=[Bf("p4t2c")])
                        Sd.op("dve", lambda e: e.tensor_tensor(
                            hcc[:], t2c[:], ps_ow[:], ALU.add),
                            reads=[Bf("p4t2c"), Bf("p4ow")], writes=[Bf("p4hc")])
                        Sd.op("pool", lambda e, wsl=wsl: e.tensor_add(
                            scc[:], hcc[:], xmc4[wsl][:]),
                            reads=[Bf("p4hc"), Bf(f"p4xm{wsl}")], writes=[Bf("p4scc")])
                        Sd.op("act", lambda e, m=m: e.dma_start(out=HDD[m], in_=hcc[:]),
                              reads=[Bf("p4hc")], writes=[Bf("h_done")], dma=True)
                        Sd.op("act", lambda e, m=m: e.dma_start(out=SDD[m], in_=scc[:]),
                              reads=[Bf("p4scc")], writes=[Bf("s_done")], dma=True)
                    Sd.barrier()

        if PHASES_MAX >= 3:
            phase34()

        def phase5():
          with contextlib.ExitStack() as p5:
            rms2 = alloc("p5rms2", [128, 16 * 1024], BF16, p5)
            hcs = [alloc(f"p5h{i}", [128, 1024], F32, p5) for i in range(2)]
            hsq = alloc("p5hsq", [128, 1024], F32, p5)
            r5a = alloc("p5ra", [1, NOWN], F32, p5)
            r5b = alloc("p5rb", [1, NOWN], F32, p5)
            ps_g = psum("p5g", [128, 1024], F32, p5)
            ps_u = psum("p5u", [128, 1024], F32, p5)
            ps_w = psum("p5w", [128, 1024], F32, p5)

            for c in range(16):
                sl = c % 2
                Sd.op("sync", lambda e, c=c, sl=sl: e.dma_start(
                    out=hcs[sl][:], in_=HDD[c]),
                    reads=[Bf("h_done")], writes=[Bf(f"p5h{sl}")], dma=True)
                Sd.op("pool", lambda e, sl=sl: e.tensor_mul(hsq[:], hcs[sl][:], hcs[sl][:]),
                      reads=[Bf(f"p5h{sl}")], writes=[Bf("p5hsq")])
                for qh in range(2):
                    pst = [sb["p0m"], sb["p0e"]][qh]
                    Sd.op("pe", lambda e, c=c, qh=qh, pst=pst: e.matmul(
                        pst[:], sb["c_ones128f"][:],
                        hsq[:, 512 * qh:512 * qh + 512],
                        start=(c == 0), stop=(c == 15)),
                        reads=[Bf("p5hsq"), Bf("c_ones128f")], writes=[Bf("p0m"), Bf("p0e")])
            Sd.op("act", lambda e: e.copy(r5a[:, 0:512], sb["p0m"][:]),
                  reads=[Bf("p0m")], writes=[Bf("p5ra")])
            Sd.op("act", lambda e: e.copy(r5a[:, 512:1024], sb["p0e"][:]),
                  reads=[Bf("p0e")], writes=[Bf("p5ra")])
            Sd.op("dve", lambda e: e.tensor_scalar_mul(r5a[:], r5a[:], 1.0 / E),
                  reads=[Bf("p5ra")], writes=[Bf("p5ra")])
            Sd.op("act", lambda e: e.activation(r5b[:], r5a[:], AF.Sqrt, bias=sb["c_eps"][:]),
                  reads=[Bf("p5ra"), Bf("c_eps")], writes=[Bf("p5rb")])
            Sd.op("dve", lambda e: e.reciprocal(r5b[:], r5b[:]),
                  reads=[Bf("p5rb")], writes=[Bf("p5rb")])
            for qh in range(2):
                Sd.op("pe", lambda e, qh=qh: e.matmul(
                    ps_w[:, 512 * qh:512 * qh + 512], sb["c_ones1f"][:],
                    r5b[:, 512 * qh:512 * qh + 512], start=True, stop=True),
                    reads=[Bf("p5rb"), Bf("c_ones1f")], writes=[Bf("p5w")])
            for c in range(16):
                sl = c % 2
                Sd.op("sync", lambda e, c=c, sl=sl: e.dma_start(
                    out=hcs[sl][:], in_=HDD[c]),
                    reads=[Bf("h_done")], writes=[Bf(f"p5h{sl}")], dma=True)
                Sd.op("dve", lambda e, sl=sl: e.tensor_tensor(
                    hcs[sl][:], hcs[sl][:], ps_w[:], ALU.mult),
                    reads=[Bf("p5w")], writes=[Bf(f"p5h{sl}")])
                Sd.op("dve", lambda e, c=c, sl=sl: e.tensor_scalar_mul(
                    rms2[:, 1024 * c:1024 * (c + 1)], hcs[sl][:],
                    sb["c_ln2p"][:, c:c + 1]),
                    reads=[Bf(f"p5h{sl}"), Bf("c_ln2p")], writes=[Bf("p5rms2")])
            Sd.barrier()

            routw = alloc("p5routw", [128, 16 * 7], BF16, p5)
            pump_until("routerw")
            for c in range(16):
                Sd.op("sync", lambda e, c=c: e.dma_start(
                    out=routw[:, 7 * c:7 * (c + 1)], in_=rb(WBF["routerw"])[c]),
                    reads=[Bf("wbf_routerw")], writes=[Bf("p5routw")], dma=True)
            ptm8 = alloc("p5ptm8", [128, 64], F32, p5)
            mx8 = alloc("p5mx8", [128, 64], F32, p5)
            idx8 = alloc("p5idx8", [128, 64], mybir.dt.uint32, p5)
            idxf = alloc("p5idxf", [128, 16], F32, p5)
            sel1 = alloc("p5sel1", [128, 8 * 7], F32, p5)
            sel2 = alloc("p5sel2", [128, 8 * 7], F32, p5)
            wden = alloc("p5wden", [128, 8], F32, p5)
            w1c = alloc("p5w1c", [128, 8], F32, p5)
            w2c = alloc("p5w2c", [128, 8], F32, p5)
            wtm = alloc("p5wtm", [128, 8 * 7], F32, p5)
            wpad = alloc("p5wpad", [128, 128], F32, p5)
            wrows = alloc("p5wrows", [8, 1024], F32, p5)
            wone = alloc("p5wone", [1, 1024], F32, p5)
            RBuf = Bf("p5ptmB")

            Sd.op("dve", lambda e: e.memset(ptm8[:], NEG), writes=[RBuf])
            Sd.op("dve", lambda e: e.memset(wpad[:], 0.0), writes=[Bf("p5wpad")])
            for t in range(8):
                for c in range(16):
                    Sd.op("pe", lambda e, t=t, c=c: e.matmul(
                        ps_u[:, 0:7], rms2[:, 1024 * c + 128 * t:1024 * c + 128 * t + 128],
                        routw[:, 7 * c:7 * (c + 1)], start=(c == 0), stop=False),
                        reads=[Bf("p5rms2"), Bf("p5routw")], writes=[Bf("p5u")])
                Sd.op("pe", lambda e: e.matmul(
                    ps_u[:, 0:7], sb["c_ones1"][:], sb["c_brout_bf"][:],
                    start=False, stop=True),
                    reads=[Bf("c_ones1"), Bf("c_brout_bf")], writes=[Bf("p5u")])
                Sd.op("act", lambda e, t=t: e.activation(
                    ptm8[:, 8 * t:8 * t + 7], ps_u[:, 0:7], AF.Sigmoid),
                    reads=[Bf("p5u")], writes=[RBuf])
            for t in range(8):
                Sd.op("dve", lambda e, t=t: e.max(
                    mx8[:, 8 * t:8 * (t + 1)], ptm8[:, 8 * t:8 * (t + 1)]),
                    reads=[RBuf], writes=[Bf("p5mx8")])
                Sd.op("dve", lambda e, t=t: e.max_index(
                    idx8[:, 8 * t:8 * (t + 1)], mx8[:, 8 * t:8 * (t + 1)],
                    ptm8[:, 8 * t:8 * (t + 1)]),
                    reads=[RBuf, Bf("p5mx8")], writes=[Bf("p5idx8")])
                Sd.op("dve", lambda e, t=t: e.tensor_copy(
                    idxf[:, 2 * t:2 * t + 2], idx8[:, 8 * t:8 * t + 2]),
                    reads=[Bf("p5idx8")], writes=[Bf("p5idxf")])
                Sd.op("dve", lambda e, t=t: e.tensor_single_scalar(
                    sel1[:, 7 * t:7 * t + 7], sb["c_iota7"][:],
                    idxf[:, 2 * t:2 * t + 1], ALU.is_equal),
                    reads=[Bf("p5idxf"), Bf("c_iota7")], writes=[Bf("p5sel1")])
                Sd.op("dve", lambda e, t=t: e.tensor_single_scalar(
                    sel2[:, 7 * t:7 * t + 7], sb["c_iota7"][:],
                    idxf[:, 2 * t + 1:2 * t + 2], ALU.is_equal),
                    reads=[Bf("p5idxf"), Bf("c_iota7")], writes=[Bf("p5sel2")])
                Sd.op("dve", lambda e, t=t: e.tensor_add(
                    wden[:, t:t + 1], mx8[:, 8 * t:8 * t + 1],
                    mx8[:, 8 * t + 1:8 * t + 2]),
                    reads=[Bf("p5mx8")], writes=[Bf("p5wden")])
            Sd.op("dve", lambda e: e.reciprocal(wden[:], wden[:]),
                  reads=[Bf("p5wden")], writes=[Bf("p5wden")])
            for t in range(8):
                Sd.op("dve", lambda e, t=t: e.tensor_mul(
                    w1c[:, t:t + 1], mx8[:, 8 * t:8 * t + 1], wden[:, t:t + 1]),
                    reads=[Bf("p5mx8"), Bf("p5wden")], writes=[Bf("p5w1c")])
                Sd.op("dve", lambda e, t=t: e.tensor_mul(
                    w2c[:, t:t + 1], mx8[:, 8 * t + 1:8 * t + 2], wden[:, t:t + 1]),
                    reads=[Bf("p5mx8"), Bf("p5wden")], writes=[Bf("p5w2c")])
            for t in range(8):
                Sd.op("dve", lambda e, t=t: e.tensor_scalar_mul(
                    wtm[:, 7 * t:7 * t + 7], sel1[:, 7 * t:7 * t + 7], w1c[:, t:t + 1]),
                    reads=[Bf("p5sel1"), Bf("p5w1c")], writes=[Bf("p5wtm")])
                Sd.op("dve", lambda e, t=t: e.scalar_tensor_tensor(
                    wtm[:, 7 * t:7 * t + 7], sel2[:, 7 * t:7 * t + 7], w2c[:, t:t + 1],
                    wtm[:, 7 * t:7 * t + 7], ALU.mult, ALU.add),
                    reads=[Bf("p5sel2"), Bf("p5w2c")], writes=[Bf("p5wtm")])
            for t in range(8):
                Sd.op("dve", lambda e, t=t: e.tensor_copy(
                    wpad[:, 0:7], wtm[:, 7 * t:7 * t + 7]),
                    reads=[Bf("p5wtm"), Bf("p5wpad")], writes=[Bf("p5wpad")])
                Sd.op("pe", lambda e: e.transpose(ps_g[:, 0:128], wpad[:], sb["c_idf"][:]),
                      reads=[Bf("p5wpad"), Bf("c_idf")], writes=[Bf("p5g")])
                Sd.op("act", lambda e, t=t: e.mul(
                    wrows[0:8, 128 * t:128 * t + 128], ps_g[0:8, 0:128], 0.125),
                    reads=[Bf("p5g")], writes=[Bf("p5wrows")])
            if TAPS:
                for ti, (tt, tb) in enumerate([(ptm8, RBuf), (mx8, Bf("p5mx8")),
                                               (sel1, Bf("p5sel1")), (sel2, Bf("p5sel2")),
                                               (idxf, Bf("p5idxf"))]):
                    Sd.op("act", lambda e, ti=ti, tt=tt: e.dma_start(
                        out=PTMD[:, 64 * ti:64 * ti + tt.shape[1]], in_=tt[:]),
                        reads=[tb], writes=[Bf("ptm_done")], dma=True)
            Sd.op("act", lambda e: e.dma_start(out=WROWD[:], in_=wrows[:]),
                  reads=[Bf("p5wrows")], writes=[Bf("wrow_done")], dma=True)

            hp = alloc("p5hp", [128, 32 * 1024], BF16, p5)
            silu_sb = alloc("p5silu", [128, 1024], F32, p5)
            wg = [alloc(f"p5wg{i}", [128, 16 * 128], BF16, p5) for i in range(2)]
            wu = [alloc(f"p5wu{i}", [128, 16 * 128], BF16, p5) for i in range(2)]
            wd = [alloc(f"p5wd{i}", [128, 32 * 128], BF16, p5) for i in range(2)]
            scs = [alloc(f"p5scs{i}", [128, 1024], F32, p5) for i in range(2)]
            oat = alloc("p5oat", [128, 1024], F32, p5)
            gav, uav, dav = rb(WBF["gate_all"]), rb(WBF["up_all"]), rb(WBF["down_all"])

            pump_until("up_all")
            for pa in range(2):
                for kk, k in enumerate(range(4 * pa, 4 * pa + 4)):
                    if k == 0:
                        wsrc, wsrcB = sb["c_shw"], Bf("c_shw")
                    else:
                        Sd.op("sync", lambda e, k=k: e.dma_start(
                            out=wone[:], in_=WROWD[k - 1:k, :]),
                            reads=[Bf("wrow_done")], writes=[Bf("p5wone")], dma=True)
                        wsrc, wsrcB = wone, Bf("p5wone")
                    for qh in range(2):
                        Sd.op("pe", lambda e, qh=qh, wsrc=wsrc: e.matmul(
                            ps_w[:, 512 * qh:512 * qh + 512], sb["c_ones1f"][:],
                            wsrc[:, 512 * qh:512 * qh + 512], start=True, stop=True),
                            reads=[wsrcB, Bf("c_ones1f")], writes=[Bf("p5w")])
                    for ic in range(8):
                        wsl = ic % 2
                        Sd.op("sync", lambda e, k=k, ic=ic, wsl=wsl: e.dma_start(
                            out=wg[wsl][:].rearrange("p (n q) -> p n q", q=128),
                            in_=gav[16 * k:16 * k + 16, :, 128 * ic:128 * ic + 128]
                            .rearrange("n p q -> p n q")),
                            reads=[Bf("wbf_gate_all")], writes=[Bf(f"p5wg{wsl}")], dma=True)
                        Sd.op("sync", lambda e, k=k, ic=ic, wsl=wsl: e.dma_start(
                            out=wu[wsl][:].rearrange("p (n q) -> p n q", q=128),
                            in_=uav[16 * k:16 * k + 16, :, 128 * ic:128 * ic + 128]
                            .rearrange("n p q -> p n q")),
                            reads=[Bf("wbf_up_all")], writes=[Bf(f"p5wu{wsl}")], dma=True)
                        for c in range(16):
                            for qh in range(2):
                                Sd.op("pe", lambda e, c=c, qh=qh, wsl=wsl: e.matmul(
                                    ps_g[:, 512 * qh:512 * qh + 512],
                                    wg[wsl][:, 128 * c:128 * c + 128],
                                    rms2[:, 1024 * c + 512 * qh:1024 * c + 512 * qh + 512],
                                    start=(c == 0), stop=(c == 15)),
                                    reads=[Bf(f"p5wg{wsl}"), Bf("p5rms2")], writes=[Bf("p5g")])
                                Sd.op("pe", lambda e, c=c, qh=qh, wsl=wsl: e.matmul(
                                    ps_u[:, 512 * qh:512 * qh + 512],
                                    wu[wsl][:, 128 * c:128 * c + 128],
                                    rms2[:, 1024 * c + 512 * qh:1024 * c + 512 * qh + 512],
                                    start=(c == 0), stop=(c == 15)),
                                    reads=[Bf(f"p5wu{wsl}"), Bf("p5rms2")], writes=[Bf("p5u")])
                        Sd.op("act", lambda e: e.activation(silu_sb[:], ps_g[:], AF.Silu),
                              reads=[Bf("p5g")], writes=[Bf("p5silu")])
                        Sd.op("dve", lambda e: e.tensor_tensor(
                            silu_sb[:], ps_u[:], silu_sb[:], ALU.mult),
                            reads=[Bf("p5u")], writes=[Bf("p5silu")])
                        Sd.op("dve", lambda e, kk=kk, ic=ic: e.tensor_tensor(
                            hp[:, 1024 * (8 * kk + ic):1024 * (8 * kk + ic + 1)],
                            silu_sb[:], ps_w[:], ALU.mult),
                            reads=[Bf("p5silu"), Bf("p5w")], writes=[Bf("p5hp")])
                        pump(3)
                pump_until("down_all")
                for dm in range(16):
                    wsl = dm % 2
                    Sd.op("sync", lambda e, pa=pa, dm=dm, wsl=wsl: e.dma_start(
                        out=wd[wsl][:].rearrange("p (n q) -> p n q", q=128),
                        in_=dav[32 * pa:32 * pa + 32, :, 128 * dm:128 * dm + 128]
                        .rearrange("n p q -> p n q")),
                        reads=[Bf("wbf_down_all")], writes=[Bf(f"p5wd{wsl}")], dma=True)
                    if pa == 0:
                        Sd.op("sync", lambda e, dm=dm, wsl=wsl: e.dma_start(
                            out=scs[wsl][:], in_=SDD[dm]),
                            reads=[Bf("s_done")], writes=[Bf(f"p5scs{wsl}")], dma=True)
                    else:
                        Sd.op("sync", lambda e, dm=dm, wsl=wsl: e.dma_start(
                            out=scs[wsl][:], in_=OAD[dm]),
                            reads=[Bf("oa_done")], writes=[Bf(f"p5scs{wsl}")], dma=True)
                    for kx in range(32):
                        for qh in range(2):
                            Sd.op("pe", lambda e, kx=kx, qh=qh, wsl=wsl: e.matmul(
                                ps_g[:, 512 * qh:512 * qh + 512],
                                wd[wsl][:, 128 * kx:128 * kx + 128],
                                hp[:, 1024 * kx + 512 * qh:1024 * kx + 512 * qh + 512],
                                start=(kx == 0), stop=(kx == 31)),
                                reads=[Bf(f"p5wd{wsl}"), Bf("p5hp")], writes=[Bf("p5g")])
                    Sd.op("dve", lambda e, wsl=wsl: e.scalar_tensor_tensor(
                        oat[:], ps_g[:], 1.0, scs[wsl][:], ALU.mult, ALU.add),
                        reads=[Bf("p5g"), Bf(f"p5scs{wsl}")], writes=[Bf("p5oat")])
                    if pa == 0:
                        Sd.op("act", lambda e, dm=dm: e.dma_start(
                            out=OAD[dm], in_=oat[:]),
                            reads=[Bf("p5oat")], writes=[Bf("oa_done")], dma=True)
                    else:
                        Sd.op("act", lambda e, dm=dm: e.dma_start(
                            out=OUT[dm], in_=oat[:]),
                            reads=[Bf("p5oat")], writes=[Bf("out_done")], dma=True)
            Sd.barrier()

        if PHASES_MAX >= 5:
            phase5()

        cum_c = {}
        for e in Sched.ENGS:
            vc, tc = {}, 0
            for i, rec in enumerate(Sd.ops[e]):
                if rec[3] is None and i in Sd.marked[e]:
                    tc += 1
                vc[i] = tc
            cum_c[e] = vc

        def emit(eng_name, eng):
            waited_c = {}
            waited_l = {}

            def do_wait(we, wi):
                dep_lane = Sd.ops[we][wi][3]
                if dep_lane is not None:
                    lane, use = dep_lane
                    v = 16 * (use + 1)
                    key = (we, lane)
                    if waited_l.get(key, 0) < v:
                        eng.wait_ge(sems_dl[we][lane], v)
                        waited_l[key] = v
                else:
                    v = cum_c[we][wi]
                    if v > 0 and waited_c.get(we, 0) < v:
                        eng.wait_ge(sems[we], v)
                        waited_c[we] = v

            for i, rec in enumerate(Sd.ops[eng_name]):
                thunk, waits, amt, lane_info = rec[0], rec[1], rec[2], rec[3]
                for key in sorted(waits, key=str):
                    do_wait(key[0], waits[key])
                if len(rec) > 4:
                    for (we, wi) in rec[4]:
                        do_wait(we, wi)
                if lane_info is not None:
                    lane, use = lane_info
                    if use > 0:
                        key = (eng_name, lane)
                        v = 16 * use
                        if waited_l.get(key, 0) < v:
                            eng.wait_ge(sems_dl[eng_name][lane], v)
                            waited_l[key] = v
                inst = eng.nop() if thunk is None else thunk(eng)
                if lane_info is not None:
                    inst.then_inc(sems_dl[eng_name][lane_info[0]], 16)
                elif i in Sd.marked[eng_name]:
                    inst.then_inc(sems[eng_name], 1)

        with nc.Block() as block:

            @block.tensor
            def _(eng):
                emit("pe", eng)

            @block.scalar
            def _(eng):
                emit("act", eng)

            @block.vector
            def _(eng):
                emit("dve", eng)

            @block.gpsimd
            def _(eng):
                emit("pool", eng)

            @block.sync
            def _(eng):
                emit("sync", eng)

    n_ops = sum(len(Sd.ops[e]) for e in Sched.ENGS)
    n_marks = sum(len(Sd.marked[e]) for e in Sched.ENGS)
    return nc, n_ops, n_marks


_PROG_CACHE = {}


def _get_program():
    if "nc" not in _PROG_CACHE:
        nc, n_ops, n_marks = build_program()
        _PROG_CACHE["nc"] = nc
    return _PROG_CACHE["nc"]


def kernel(**inputs):
    in_maps = _stage_inputs(inputs)
    nc = _get_program()
    res = run_bass_kernel_spmd(nc, in_maps, core_ids=list(range(8)))
    out = np.zeros((B, S, E), np.float32)
    for c in range(8):
        b, h = c // 2, c % 2
        o = np.asarray(res.results[c]["out"])
        out[b, 1024 * h:1024 * h + 1024, :] = o.reshape(E, NOWN).T
    return out


# version 11
# speedup vs baseline: 1.0279x; 1.0279x over previous
import contextlib
import numpy as np
import ml_dtypes

import concourse.bass as bass
import concourse.mybir as mybir
from concourse.bass_utils import run_bass_kernel_spmd

F32 = mybir.dt.float32
BF16 = mybir.dt.bfloat16
AF = mybir.ActivationFunctionType
ALU = mybir.AluOpType
AX = mybir.AxisListType
BF = ml_dtypes.bfloat16

B, S, E = 4, 2048, 2048
WINDOW, NMETA = 512, 16
HDIM = 128
NH = 16
LAT = 512
RD = 64
IEXP = 1024
NEXP = 8
EPS = 1e-5
SCALE = float(1.0 / np.sqrt(128.0))

RLOC = 2688
NBLK = 21
HALO0, OWN0, REST0 = 129, 640, 1664
NQ = 2048
NOWN = 1024
NEG = -1e30

TAPS = False
PHASES_MAX = 9


class Buf:
    __slots__ = ("ws", "r", "name")

    def __init__(self, name=""):
        self.ws = []
        self.r = []
        self.name = name


NLANE = 8


class Sched:
    ENGS = ("pe", "act", "dve", "pool", "sync")

    def __init__(self):
        self.ops = {e: [] for e in self.ENGS}
        self.marked = {e: set() for e in self.ENGS}
        self.dma_seq = {e: 0 for e in self.ENGS}
        self.lane_use = {e: [0] * NLANE for e in self.ENGS}
        self.lane_last = {e: [None] * NLANE for e in self.ENGS}
        self.last_compute = {e: None for e in self.ENGS}

    def op(self, eng, thunk, reads=(), writes=(), dma=False):
        idx = len(self.ops[eng])
        deps = []
        for b in reads:
            deps.extend(b.ws)
        for b in writes:
            if b.r:
                deps.extend(b.r)
            else:
                deps.extend(b.ws)
        waits = {}
        for (de, di) in deps:
            dep_lane = self.ops[de][di][3]
            if de == eng:
                dep_dma = dep_lane is not None
                if dep_dma and dma:
                    continue
                if (not dep_dma) and (not dma) and eng == "pe":
                    continue
            key = (de, "c") if dep_lane is None else (de, dep_lane[0])
            if waits.get(key, -1) < di:
                waits[key] = di
        for (de, _k), di in waits.items():
            if self.ops[de][di][3] is None:
                self.marked[de].add(di)
        for b in reads:
            b.r.append((eng, idx))
        for b in writes:
            if b.r:
                b.ws = [(eng, idx)]
                b.r = []
            else:
                b.ws.append((eng, idx))
        lane_info = None
        if dma:
            lane = self.dma_seq[eng] % NLANE
            self.dma_seq[eng] += 1
            use = self.lane_use[eng][lane]
            self.lane_use[eng][lane] += 1
            lane_info = (lane, use)
            self.lane_last[eng][lane] = idx
        else:
            self.last_compute[eng] = idx
        self.ops[eng].append([thunk, waits, 16 if dma else 1, lane_info])

    def barrier(self):
        targets = {}
        for oe in self.ENGS:
            t = []
            if self.last_compute[oe] is not None:
                t.append(self.last_compute[oe])
                self.marked[oe].add(self.last_compute[oe])
            for li in self.lane_last[oe]:
                if li is not None:
                    t.append(li)
            targets[oe] = t
        for e in self.ENGS:
            waits = {}
            ext = []
            for oe in self.ENGS:
                if oe == e:
                    continue
                for ti in targets[oe]:
                    ext.append((oe, ti))
            self.ops[e].append([None, waits, 1, None, ext])
            self.last_compute[e] = len(self.ops[e]) - 1

    def finalize(self, nc, sems, engmap):
        cum = {}
        for e in self.ENGS:
            vals, tot = {}, 0
            for i, (_t, _w, amt) in enumerate(self.ops[e]):
                if i in self.marked[e]:
                    tot += amt
                vals[i] = tot
            cum[e] = vals
        n_w = sum(len(w) for e in self.ENGS for (_t, w, _a) in self.ops[e])
        n_o = sum(len(self.ops[e]) for e in self.ENGS)

        def body(eng_name):
            eng = engmap[eng_name]
            waited = {}
            for i, (thunk, waits, amt) in enumerate(self.ops[eng_name]):
                for we in sorted(waits):
                    v = cum[we][waits[we]]
                    if waited.get(we, 0) < v:
                        eng.wait_ge(sems[we], v)
                        waited[we] = v
                if thunk is None:
                    inst = eng.nop()
                else:
                    inst = thunk(eng)
                if i in self.marked[eng_name]:
                    inst.then_inc(sems[eng_name], amt)
        return body, n_o, n_w


def _posmap(h):
    pos = np.full(RLOC, -1, np.int64)
    pos[0:16] = np.arange(16)
    if h == 1:
        pos[HALO0:HALO0 + 511] = np.arange(513, 1024)
    pos[OWN0:OWN0 + 1024] = np.arange(1024 * h, 1024 * h + 1024)
    pos[REST0:REST0 + 1024] = np.arange(1024 * (1 - h), 1024 * (1 - h) + 1024)
    return pos


def _rope_sparse(pos):
    inv = 10000.0 ** (-np.arange(0, HDIM, 2, dtype=np.float64) / HDIM)
    p = np.where(pos < 0, 0, pos).astype(np.float64)
    ang = p[None, :] * inv[:, None]
    cos = np.repeat(np.cos(ang), 2, axis=0)
    sin = np.repeat(np.sin(ang), 2, axis=0)
    return cos.astype(np.float32), sin.astype(np.float32)


def _rope_mla(pos):
    inv = 10000.0 ** (-np.arange(0, RD, 2, dtype=np.float64) / RD)
    p = np.where(pos < 0, 0, pos).astype(np.float64)
    ang = p[None, :] * inv[:, None]
    c = np.concatenate([np.cos(ang), np.cos(ang)], 0)
    s = np.concatenate([np.sin(ang), np.sin(ang)], 0)
    return (np.concatenate([c, c], 0).astype(np.float32),
            np.concatenate([s, s], 0).astype(np.float32))


def _host_constants(h):
    pos = _posmap(h)
    valid = pos >= 0
    qpos = pos[OWN0:]
    d = {}
    d["cosq_sp"], d["sinq_sp"] = _rope_sparse(qpos)
    d["cosk_sp"], d["sink_sp"] = _rope_sparse(pos)
    d["cos_mla"], d["sin_mla"] = _rope_mla(qpos)

    Rsp = np.zeros((HDIM, HDIM), np.float32)
    for i in range(HDIM // 2):
        Rsp[2 * i, 2 * i + 1] = -1.0
        Rsp[2 * i + 1, 2 * i] = 1.0
    d["RspT"] = Rsp.T.astype(BF).copy()
    R64 = np.zeros((RD, RD), np.float32)
    for i in range(RD // 2):
        R64[i, i + RD // 2] = -1.0
        R64[i + RD // 2, i] = 1.0
    Rm = np.zeros((HDIM, HDIM), np.float32)
    Rm[:RD, :RD] = R64
    Rm[RD:, RD:] = R64
    d["RmlaT"] = Rm.T.astype(BF).copy()

    mask_win = np.zeros((4, 9, 128, 512), BF)
    for g in range(4):
        qp = qpos[512 * g: 512 * g + 512][None, :].astype(np.int64)
        kp0 = pos[0:128][:, None]
        mask_win[g, 0, 0:16] = ((kp0[0:16] <= qp) & (qp - kp0[0:16] > 511)).astype(BF)
        for r in range(1, 9):
            kb = 4 * g + r
            kp = pos[128 * kb:128 * kb + 128][:, None]
            vl = valid[128 * kb:128 * kb + 128][:, None]
            mask_win[g, r] = (vl & (kp <= qp) & (qp - kp <= 511)).astype(BF)
    d["mask_win"] = mask_win

    bias_mla = np.zeros((128, 16), np.float32)
    for j in range(16):
        kp = qpos[128 * j: 128 * j + 128]
        le = kp[:, None] <= qpos[None, 0:1024]
        if le.all():
            bias_mla[:, j] = 0.0
        elif not le.any():
            bias_mla[:, j] = NEG
    d["bias_mla"] = bias_mla
    tri = np.zeros((8, 128, NOWN), BF)
    for j in range(8):
        kk = np.arange(128)[:, None] + 128 * j
        qq = np.arange(NOWN)[None, :]
        tri[j] = (kk <= qq).astype(BF)
    d["tri_mla"] = tri

    d["ident_f32"] = np.eye(128, dtype=np.float32)
    d["iota7"] = np.tile(np.arange(7, dtype=np.float32), (128, 1))
    return d


def _stage_inputs(inputs):
    x = np.asarray(inputs["L0_attn_x"])
    w = {k: np.asarray(v) for k, v in inputs.items()}
    shared = {
        "wq": w["L0_attn_Wq"], "wkv": w["L0_attn_Wkv"], "wo": w["L0_attn_Wo"],
        "kvd": w["L0_ffn_kv_d_w"], "qd": w["L0_ffn_q_d_w"],
        "ku": w["L0_ffn_k_u_w"], "vu": w["L0_ffn_v_u_w"], "qu": w["L0_ffn_q_u_w"],
        "ropek": w["L0_ffn_rope_k_w"], "ropeq": w["L0_ffn_rope_q_w"],
        "ow": w["L0_ffn_o_w"],
        "routerw": np.ascontiguousarray(w["L0_ffn_router_w"]),
        "routerb": np.ascontiguousarray(w["L0_ffn_router_b"].reshape(1, 7)),
        "gate_all": np.ascontiguousarray(np.concatenate(
            [w["L0_ffn_sh_gate"][None], w["L0_ffn_r_gate"]], 0).reshape(NEXP * E, IEXP)),
        "up_all": np.ascontiguousarray(np.concatenate(
            [w["L0_ffn_sh_up"][None], w["L0_ffn_r_up"]], 0).reshape(NEXP * E, IEXP)),
        "down_all": np.ascontiguousarray(np.concatenate(
            [w["L0_ffn_sh_down"][None], w["L0_ffn_r_down"]], 0).reshape(NEXP * IEXP, E)),
    }

    def pc(v):
        return np.ascontiguousarray(v.reshape(16, 128).T)
    shared["bq_pc"] = pc(w["L0_attn_bq"])
    shared["bo_pc"] = pc(w["L0_attn_bo"])
    shared["bkvk_pc"] = np.ascontiguousarray(w["L0_attn_bkv"][0:512].reshape(4, 128).T)
    shared["bkvv_row"] = np.ascontiguousarray(w["L0_attn_bkv"][512:1024].reshape(1, 512))
    shared["ln1w_pc"] = pc(w["L0_ffn_ln1_w"])
    shared["ln2w_pc"] = pc(w["L0_ffn_ln2_w"])
    shared["ln1w_row"] = np.ascontiguousarray(w["L0_ffn_ln1_w"].reshape(1, 2048))

    cvar = [_host_constants(0), _host_constants(1)]
    in_maps = []
    for c in range(8):
        b, h = c // 2, c % 2
        xb = x[b]
        xloc = np.zeros((RLOC, E), np.float32)
        xloc[0:16] = xb[0:16]
        if h == 1:
            xloc[HALO0:HALO0 + 511] = xb[513:1024]
        xloc[OWN0:OWN0 + 1024] = xb[1024 * h: 1024 * h + 1024]
        xloc[REST0:REST0 + 1024] = xb[1024 * (1 - h): 1024 * (1 - h) + 1024]
        m = dict(shared)
        m["xloc"] = np.ascontiguousarray(xloc.T.reshape(16, 128, RLOC))
        m.update(cvar[h])
        in_maps.append(m)
    return in_maps


def build_program():
    nc = bass.Bass("TRN2", target_bir_lowering=False, debug=False)
    Sd = Sched()
    bufs = {}

    def Bf(name):
        if name not in bufs:
            bufs[name] = Buf(name)
        return bufs[name]

    def din(name, shape, dt=F32):
        return nc.dram_tensor(name, shape, dt, kind="ExternalInput").ap()

    X = din("xloc", [16, 128, RLOC])
    WRAW = {nm: din(nm, shape) for nm, shape in [
        ("wq", [E, E]), ("wkv", [E, 1024]), ("wo", [E, E]),
        ("kvd", [E, LAT]), ("qd", [E, LAT]), ("ku", [LAT, 1024]),
        ("vu", [LAT, E]), ("qu", [LAT, 1024]), ("ropek", [E, 1024]),
        ("ropeq", [LAT, 1024]), ("ow", [E, E]), ("routerw", [E, 7]),
        ("gate_all", [NEXP * E, IEXP]), ("up_all", [NEXP * E, IEXP]),
        ("down_all", [NEXP * IEXP, E]),
    ]}
    BQ = din("bq_pc", [128, 16])
    BO = din("bo_pc", [128, 16])
    BKVK = din("bkvk_pc", [128, 4])
    BKVV = din("bkvv_row", [1, 512])
    LN1P = din("ln1w_pc", [128, 16])
    LN2P = din("ln2w_pc", [128, 16])
    LN1R = din("ln1w_row", [1, 2048])
    BROUT = din("routerb", [1, 7])
    COSQ = din("cosq_sp", [128, NQ])
    SINQ = din("sinq_sp", [128, NQ])
    COSK = din("cosk_sp", [128, RLOC])
    SINK = din("sink_sp", [128, RLOC])
    COSM = din("cos_mla", [128, NQ])
    SINM = din("sin_mla", [128, NQ])
    RSPT = din("RspT", [128, 128], BF16)
    RMLAT = din("RmlaT", [128, 128], BF16)
    MWIN = din("mask_win", [4, 9, 128, 512], BF16)
    BMLA = din("bias_mla", [128, 16])
    TRIM = din("tri_mla", [8, 128, NOWN], BF16)
    IDF = din("ident_f32", [128, 128])
    IOTA7 = din("iota7", [128, 7])

    OUT = nc.dram_tensor("out", [16, 128, NOWN], F32, kind="ExternalOutput").ap()

    ikind = "ExternalOutput" if TAPS else "Internal"

    def dint(name, shape, dt=BF16, tap=False):
        return nc.dram_tensor(name, shape, dt,
                              kind=(ikind if tap else "Internal")).ap()

    WBF = {}
    for nm, shape in [
        ("wq", [E, E]), ("wkv", [E, 1024]), ("wo", [E, E]), ("kvd", [E, LAT]),
        ("qd", [E, LAT]), ("ku", [LAT, 1024]), ("vu", [LAT, E]),
        ("qu", [LAT, 1024]), ("ropek", [E, 1024]), ("ropeq", [LAT, 1024]),
        ("ow", [E, E]), ("routerw", [E, 7]),
        ("gate_all", [NEXP * E, IEXP]), ("up_all", [NEXP * E, IEXP]),
        ("down_all", [NEXP * IEXP, E]),
    ]:
        WBF[nm] = dint("wbf_" + nm, shape)

    QSPD = dint("qsp_d", [16, 128, NQ], tap=True)
    KSPD = dint("ksp_d", [4, 128, RLOC], tap=True)
    VSPD = dint("vsp_d", [NBLK, 128, 512], tap=True)
    XMIDD = dint("xmid_d", [16, 128, NQ], F32, tap=True)
    KPD = dint("kp_d", [8, 128, NQ], tap=True)
    KRD = dint("kr_d", [8, 128, NQ], tap=True)
    VMLAD = dint("vmla_d", [16, 128, E], tap=True)
    HDD = dint("h_d", [16, 128, NOWN], F32, tap=True)
    SDD = dint("s_d", [16, 128, NOWN], F32, tap=True)
    OAD = dint("oa_d", [16, 128, NOWN], F32, tap=True)
    Y2D = dint("y2_d", [4, 128, 16 * 512], tap=True)
    PTMD = dint("ptm_d", [128, 5 * 64], F32, tap=True)
    WROWD = dint("wrow_d", [8, 1024], F32, tap=True)
    ROWD = dint("row_d", [8, RLOC], F32, tap=True)

    def rb(ap):
        return ap.rearrange("(n p) c -> n p c", p=128)

    es = contextlib.ExitStack()
    with es:
        sems = {e: es.enter_context(nc.semaphore("sem_" + e)) for e in Sched.ENGS}
        sems_dl = {e: [es.enter_context(nc.semaphore(f"semd_{e}{j}"))
                       for j in range(NLANE)]
                   for e in ("sync", "act")}
        sb = {}

        def alloc(name, shape, dt=F32, stack=None):
            t = (stack or es).enter_context(nc.sbuf_tensor(name, shape, dt))
            sb[name] = t
            return t

        def psum(name, shape, dt=F32, stack=None):
            t = (stack or es).enter_context(nc.psum_tensor(name, shape, dt))
            sb[name] = t
            return t

        def load_packed(name, nparts, n, q, dt, src3, stack=None):
            t = alloc(name, [nparts, n * q], dt, stack)
            Sd.op("sync", lambda e, t=t, src3=src3, n=n, q=q: e.dma_start(
                out=t[:].rearrange("p (n q) -> p n q", q=q), in_=src3),
                writes=[Bf(name)], dma=True)
            return t

        def load_const(name, shape, dt, src, stack=None):
            t = alloc(name, shape, dt, stack)
            Sd.op("sync", lambda e, t=t, src=src: e.dma_start(out=t[:], in_=src),
                  writes=[Bf(name)], dma=True)
            return t

        load_const("c_bq", [128, 16], F32, BQ)
        Sd.op("dve", lambda e: e.tensor_scalar_mul(sb["c_bq"][:], sb["c_bq"][:], SCALE),
              reads=[Bf("c_bq")], writes=[Bf("c_bq")])
        load_const("c_bo", [128, 16], F32, BO)
        load_const("c_bkvk", [128, 4], F32, BKVK)
        load_const("c_ln1p", [128, 16], F32, LN1P)
        load_const("c_ln2p", [128, 16], F32, LN2P)
        load_const("c_bmla", [128, 16], F32, BMLA)
        load_const("c_rspt", [128, 128], BF16, RSPT)
        load_const("c_rmlat", [128, 128], BF16, RMLAT)
        load_const("c_idf", [128, 128], F32, IDF)
        load_const("c_iota7", [128, 7], F32, IOTA7)
        load_const("c_bkvv", [1, 512], F32, BKVV)
        load_const("c_brout", [1, 7], F32, BROUT)
        for nm, src in [("c_bkvv_bf", "c_bkvv"), ("c_brout_bf", "c_brout")]:
            t = alloc(nm, list(sb[src].shape), BF16)
            Sd.op("dve", lambda e, t=t, s=sb[src]: e.tensor_copy(t[:], s[:]),
                  reads=[Bf(src)], writes=[Bf(nm)])
        for nm, val, shape, dt in [("c_ones1", 1.0, [1, 128], BF16),
                                   ("c_ones1f", 1.0, [1, 128], F32),
                                   ("c_ones128", 1.0, [128, 1], BF16),
                                   ("c_ones128f", 1.0, [128, 1], F32),
                                   ("c_eps", EPS, [1, 1], F32),
                                   ("c_shw", 0.125, [1, NOWN], F32)]:
            t = alloc(nm, shape, dt)
            Sd.op("dve", lambda e, t=t, v=val: e.memset(t[:], v), writes=[Bf(nm)])

        def row_coefs(rm, rmB, re_, reB, rc, rcB, ra, raB, t1, t1B, t2, t2B, width):
            Sd.op("dve", lambda e: e.tensor_scalar_mul(rm[:, 0:width], rm[:, 0:width], 1.0 / E),
                  reads=[rmB], writes=[rmB])
            Sd.op("dve", lambda e: e.tensor_scalar_mul(re_[:, 0:width], re_[:, 0:width], 1.0 / E),
                  reads=[reB], writes=[reB])
            Sd.op("dve", lambda e: e.tensor_mul(t1[:, 0:width], rm[:, 0:width], rm[:, 0:width]),
                  reads=[rmB], writes=[t1B])
            Sd.op("dve", lambda e: e.tensor_sub(t1[:, 0:width], re_[:, 0:width], t1[:, 0:width]),
                  reads=[reB], writes=[t1B])
            Sd.op("act", lambda e: e.activation(t2[:, 0:width], t1[:, 0:width], AF.Sqrt, bias=sb["c_eps"][:]),
                  reads=[t1B, Bf("c_eps")], writes=[t2B])
            Sd.op("dve", lambda e: e.reciprocal(rc[:, 0:width], t2[:, 0:width]),
                  reads=[t2B], writes=[rcB])
            Sd.op("dve", lambda e: e.tensor_mul(ra[:, 0:width], rm[:, 0:width], rc[:, 0:width]),
                  reads=[rmB, rcB], writes=[raB])
            Sd.op("dve", lambda e: e.tensor_scalar_mul(ra[:, 0:width], ra[:, 0:width], -1.0),
                  reads=[raB], writes=[raB])

        def rope4(rawf_ap, rawfbuf, rawb_ap, rawbbuf, cos_ap, sin_ap,
                  out_ap, outbuf, w_, rot_lhs, ps_rot, psrotbuf,
                  t1, t1buf, t2, t2buf, trigbuf, trigbuf2=None):
            Sd.op("dve", lambda e: e.tensor_copy(rawb_ap, rawf_ap),
                  reads=[rawfbuf], writes=[rawbbuf])
            Sd.op("pe", lambda e: e.matmul(ps_rot[:, 0:w_], sb[rot_lhs][:],
                                           rawb_ap, start=True, stop=True),
                  reads=[rawbbuf, Bf(rot_lhs)], writes=[psrotbuf])
            Sd.op("dve", lambda e: e.tensor_mul(t1[:, 0:w_], rawf_ap, cos_ap),
                  reads=[rawfbuf, trigbuf], writes=[t1buf])
            Sd.op("dve", lambda e: e.scalar_tensor_tensor(
                t2[:, 0:w_], ps_rot[:, 0:w_], 1.0, sin_ap, ALU.mult, ALU.mult),
                reads=[psrotbuf, trigbuf2 or trigbuf], writes=[t2buf])
            Sd.op("pool", lambda e: e.tensor_add(out_ap, t1[:, 0:w_], t2[:, 0:w_]),
                  reads=[t1buf, t2buf], writes=[outbuf])

        p0s = contextlib.ExitStack()

        wfp = [alloc(f"p0wf{i}", [128, 1024], F32) for i in range(2)]
        wbp = [alloc(f"p0wb{i}", [128, 1024], BF16) for i in range(2)]
        pump_tiles = []
        pump_end = {}
        for nm in ["wkv", "wq", "wo", "kvd", "ku", "vu", "ropek", "qd",
                   "qu", "ropeq", "ow", "routerw", "gate_all", "up_all",
                   "down_all"]:
            wsrc = WRAW[nm]
            rows, cols = wsrc.tensor.shape
            scaled = nm in ("qu", "ropeq", "wq")
            sv, dv = rb(wsrc), rb(WBF[nm])
            for rblk in range(rows // 128):
                for c0_ in range(0, cols, 1024):
                    cw = min(1024, cols - c0_)
                    pump_tiles.append((nm, sv, dv, rblk, c0_, cw, scaled))
            pump_end[nm] = len(pump_tiles)
        pump_pos = [0]
        pump_pend = [None]

        def _pump_one():
            ti = pump_pos[0]
            nm, sv, dv, rblk, c0_, cw, scaled = pump_tiles[ti]
            pump_pos[0] = ti + 1
            sl = ti % 2
            Sd.op("sync", lambda e, sl=sl, sv=sv, rblk=rblk, c0_=c0_, cw=cw:
                  e.dma_start(out=wfp[sl][:, 0:cw],
                              in_=sv[rblk][:, c0_:c0_ + cw]),
                  writes=[Bf(f"p0wf{sl}")], dma=True)
            if scaled:
                Sd.op("dve", lambda e, sl=sl, cw=cw:
                      e.tensor_scalar_mul(wbp[sl][:, 0:cw], wfp[sl][:, 0:cw], SCALE),
                      reads=[Bf(f"p0wf{sl}")], writes=[Bf(f"p0wb{sl}")])
            elif ti % 2 == 0:
                Sd.op("dve", lambda e, sl=sl, cw=cw:
                      e.tensor_copy(wbp[sl][:, 0:cw], wfp[sl][:, 0:cw]),
                      reads=[Bf(f"p0wf{sl}")], writes=[Bf(f"p0wb{sl}")])
            else:
                Sd.op("act", lambda e, sl=sl, cw=cw:
                      e.copy(wbp[sl][:, 0:cw], wfp[sl][:, 0:cw]),
                      reads=[Bf(f"p0wf{sl}")], writes=[Bf(f"p0wb{sl}")])
            if pump_pend[0] is not None:
                pump_pend[0]()
            def _store(sl=sl, nm=nm, dv=dv, rblk=rblk, c0_=c0_, cw=cw):
                Sd.op("act", lambda e, sl=sl, dv=dv, rblk=rblk, c0_=c0_, cw=cw:
                      e.dma_start(out=dv[rblk][:, c0_:c0_ + cw],
                                  in_=wbp[sl][:, 0:cw]),
                      reads=[Bf(f"p0wb{sl}")], writes=[Bf("wbf_" + nm)], dma=True)
            pump_pend[0] = _store

        def pump(k):
            for _ in range(k):
                if pump_pos[0] < len(pump_tiles):
                    _pump_one()

        def pump_until(nm):
            while pump_pos[0] < pump_end[nm]:
                _pump_one()
            if pump_pend[0] is not None:
                pump_pend[0]()
                pump_pend[0] = None

        def phase0():
            xs = [alloc(f"p0x{i}", [128, 512], F32, p0s) for i in range(2)]
            xq = [alloc(f"p0q{i}", [128, 512], F32, p0s) for i in range(2)]
            r_m = alloc("p0rm", [1, RLOC], F32, p0s)
            r_e = alloc("p0re", [1, RLOC], F32, p0s)
            r_t1 = alloc("p0t1", [1, RLOC], F32, p0s)
            r_t2 = alloc("p0t2", [1, RLOC], F32, p0s)
            ps_m = psum("p0m", [1, 512], F32)
            ps_e = psum("p0e", [1, 512], F32)
            st = 0
            for col0 in range(0, RLOC, 512):
                w_ = min(512, RLOC - col0)
                for c in range(16):
                    sl = st % 2
                    st += 1
                    Sd.op("sync", lambda e, sl=sl, c=c, col0=col0, w_=w_:
                          e.dma_start(out=xs[sl][:, 0:w_], in_=X[c, :, col0:col0 + w_]),
                          writes=[Bf(f"p0xx{sl}")], dma=True)
                    Sd.op("pool", lambda e, sl=sl, w_=w_:
                          e.tensor_mul(xq[sl][:, 0:w_], xs[sl][:, 0:w_], xs[sl][:, 0:w_]),
                          reads=[Bf(f"p0xx{sl}")], writes=[Bf(f"p0qq{sl}")])
                    Sd.op("pe", lambda e, sl=sl, c=c, w_=w_:
                          e.matmul(ps_m[:, 0:w_], sb["c_ones128f"][:], xs[sl][:, 0:w_],
                                   start=(c == 0), stop=(c == 15)),
                          reads=[Bf(f"p0xx{sl}"), Bf("c_ones128f")], writes=[Bf("p0m")])
                    Sd.op("pe", lambda e, sl=sl, c=c, w_=w_:
                          e.matmul(ps_e[:, 0:w_], sb["c_ones128f"][:], xq[sl][:, 0:w_],
                                   start=(c == 0), stop=(c == 15)),
                          reads=[Bf(f"p0qq{sl}"), Bf("c_ones128f")], writes=[Bf("p0e")])
                    if st % 2 == 0:
                        pump(1)
                Sd.op("act", lambda e, col0=col0, w_=w_:
                      e.copy(r_m[:, col0:col0 + w_], ps_m[:, 0:w_]),
                      reads=[Bf("p0m")], writes=[Bf("p0rm")])
                Sd.op("act", lambda e, col0=col0, w_=w_:
                      e.copy(r_e[:, col0:col0 + w_], ps_e[:, 0:w_]),
                      reads=[Bf("p0e")], writes=[Bf("p0re")])
            row_coefs(r_m, Bf("p0rm"), r_e, Bf("p0re"),
                      r_t2, Bf("p0t2"), r_e, Bf("p0re"),
                      r_t1, Bf("p0t1"), r_t2, Bf("p0t2"), RLOC)
            Sd.op("act", lambda e: e.dma_start(out=ROWD[0:1, :], in_=r_t2[:]),
                  reads=[Bf("p0t2")], writes=[Bf("row_c0")], dma=True)
            Sd.op("act", lambda e: e.dma_start(out=ROWD[1:2, :], in_=r_e[:]),
                  reads=[Bf("p0re")], writes=[Bf("row_a0")], dma=True)

        phase0()

        def phase1():
          with contextlib.ExitStack() as p1:
            tck = alloc("p1tck", [128, 640], F32, p1)
            tsk = alloc("p1tsk", [128, 640], F32, p1)
            tcq = alloc("p1tcq", [128, 512], F32, p1)
            tsq = alloc("p1tsq", [128, 512], F32, p1)
            rc0 = alloc("p1c0", [1, RLOC], F32, p1)
            ra0 = alloc("p1a0", [1, RLOC], F32, p1)
            Sd.op("sync", lambda e: e.dma_start(out=rc0[:], in_=ROWD[0:1, :]),
                  reads=[Bf("row_c0")], writes=[Bf("p1c0")], dma=True)
            Sd.op("sync", lambda e: e.dma_start(out=ra0[:], in_=ROWD[1:2, :]),
                  reads=[Bf("row_a0")], writes=[Bf("p1a0")], dma=True)

            wkvl = alloc("p1wkv", [128, 16 * 1024], BF16, p1)
            pump_until("wkv")
            for c in range(16):
                Sd.op("sync", lambda e, c=c: e.dma_start(
                    out=wkvl[:, 1024 * c:1024 * (c + 1)], in_=rb(WBF["wkv"])[c]),
                    reads=[Bf("wbf_wkv")], writes=[Bf("p1wkv")], dma=True)

            ln0 = [alloc("p1ln0", [128, 16 * 640], BF16, p1),
                   alloc("p1ln1", [128, 16 * 512], BF16, p1)]
            xs = [alloc(f"p1x{i}", [128, 640], F32, p1) for i in range(3)]
            wqt = [alloc(f"p1wqt{i}", [128, 256], BF16, p1) for i in range(2)]
            ps_bc = psum("p1bc", [128, 1024], F32, p1)
            ps_q = [psum(f"p1q{i}", [128, 512], F32, p1) for i in range(2)]
            ps_a = psum("p1a_", [128, 512], F32, p1)
            ps_b = psum("p1b_", [128, 512], F32, p1)
            scf = alloc("p1scf", [128, 512], F32, p1)
            rawf = [alloc(f"p1rawf{i}", [128, 512], F32, p1) for i in range(2)]
            rawb = [alloc(f"p1rawb{i}", [128, 512], BF16, p1) for i in range(2)]
            t1f = [alloc(f"p1t1f{i}", [128, 512], F32, p1) for i in range(2)]
            t2f = [alloc(f"p1t2f{i}", [128, 512], F32, p1) for i in range(2)]
            stg = [alloc(f"p1stg{i}", [128, 512], BF16, p1) for i in range(2)]

            for G in range(5):
                blocks = list(range(0, 5)) if G == 0 else list(range(4 * G + 1, 4 * G + 5))
                col0 = 128 * blocks[0]
                w_ = 128 * len(blocks)
                lsl = G % 2
                lstr = 640 if lsl == 0 else 512
                Sd.op("sync", lambda e, col0=col0, w_=w_: e.dma_start(
                    out=tck[:, 0:w_], in_=COSK[:, col0:col0 + w_]),
                    writes=[Bf("ctck")], dma=True)
                Sd.op("sync", lambda e, col0=col0, w_=w_: e.dma_start(
                    out=tsk[:, 0:w_], in_=SINK[:, col0:col0 + w_]),
                    writes=[Bf("ctsk")], dma=True)
                if G >= 1:
                    g_ = G - 1
                    Sd.op("sync", lambda e, g_=g_: e.dma_start(
                        out=tcq[:], in_=COSQ[:, 512 * g_:512 * g_ + 512]),
                        writes=[Bf("ctcq")], dma=True)
                    Sd.op("sync", lambda e, g_=g_: e.dma_start(
                        out=tsq[:], in_=SINQ[:, 512 * g_:512 * g_ + 512]),
                        writes=[Bf("ctsq")], dma=True)
                subs = [(0, 512), (512, 128)] if G == 0 else [(0, 512)]
                for (s0, sw) in subs:
                    Sd.op("pe", lambda e, col0=col0, s0=s0, sw=sw:
                          e.matmul(ps_bc[:, 0:sw], sb["c_ones1f"][:],
                                   rc0[:, col0 + s0:col0 + s0 + sw],
                                   start=True, stop=True),
                          reads=[Bf("p1c0"), Bf("c_ones1f")], writes=[Bf("p1bc")])
                    Sd.op("pe", lambda e, col0=col0, s0=s0, sw=sw:
                          e.matmul(ps_bc[:, 512:512 + sw], sb["c_ones1f"][:],
                                   ra0[:, col0 + s0:col0 + s0 + sw],
                                   start=True, stop=True),
                          reads=[Bf("p1a0"), Bf("c_ones1f")], writes=[Bf("p1bc")])
                    for c in range(16):
                        xsl = (G * 16 + c) % 3
                        Sd.op("sync", lambda e, c=c, xsl=xsl, col0=col0, s0=s0, sw=sw:
                              e.dma_start(out=xs[xsl][:, 0:sw],
                                          in_=X[c, :, col0 + s0:col0 + s0 + sw]),
                              writes=[Bf(f"p1x{xsl}")], dma=True)
                        Sd.op("dve", lambda e, xsl=xsl, sw=sw:
                              e.tensor_tensor(scf[:, 0:sw], xs[xsl][:, 0:sw],
                                              ps_bc[:, 0:sw], ALU.mult),
                              reads=[Bf(f"p1x{xsl}"), Bf("p1bc")], writes=[Bf("p1scf")])
                        Sd.op("dve", lambda e, lsl=lsl, lstr=lstr, c=c, s0=s0, sw=sw:
                              e.tensor_tensor(ln0[lsl][:, lstr * c + s0:lstr * c + s0 + sw],
                                              scf[:, 0:sw], ps_bc[:, 512:512 + sw], ALU.add),
                              reads=[Bf("p1scf"), Bf("p1bc")], writes=[Bf(f"p1ln{lsl}")])
                        pump(1)
                for kc in range(4):
                    for si, (s0, sw) in enumerate(subs):
                        pp = (kc + si) % 2
                        psk = [ps_a, ps_b][pp]
                        pskb = Bf(f"p1ab{pp}")
                        for c in range(16):
                            Sd.op("pe", lambda e, kc=kc, c=c, lsl=lsl, lstr=lstr, s0=s0, sw=sw, psk=psk:
                                  e.matmul(psk[:, 0:sw],
                                           wkvl[:, 1024 * c + 128 * kc:1024 * c + 128 * kc + 128],
                                           ln0[lsl][:, lstr * c + s0:lstr * c + s0 + sw],
                                           start=(c == 0), stop=(c == 15)),
                                  reads=[Bf("p1wkv"), Bf(f"p1ln{lsl}")], writes=[pskb])
                        Sd.op("act", lambda e, kc=kc, sw=sw, psk=psk, pp=pp:
                              e.add(rawf[pp][:, 0:sw], psk[:, 0:sw],
                                    sb["c_bkvk"][:, kc:kc + 1]),
                              reads=[pskb, Bf("c_bkvk")], writes=[Bf(f"p1rawf{pp}")])
                        rope4(rawf[pp][:, 0:sw], Bf(f"p1rawf{pp}"),
                              rawb[pp][:, 0:sw], Bf(f"p1rawb{pp}"),
                              tck[:, s0:s0 + sw], tsk[:, s0:s0 + sw],
                              stg[pp][:, 0:sw], Bf(f"p1stg{pp}"), sw,
                              "c_rspt", ps_bc, Bf("p1bc"),
                              t1f[pp], Bf(f"p1t1f{pp}"), t2f[pp], Bf(f"p1t2f{pp}"),
                              Bf("ctck"), Bf("ctsk"))
                        Sd.op("act", lambda e, kc=kc, col0=col0, s0=s0, sw=sw, pp=pp:
                              e.dma_start(out=KSPD[kc, :, col0 + s0:col0 + s0 + sw],
                                          in_=stg[pp][:, 0:sw]),
                              reads=[Bf(f"p1stg{pp}")], writes=[Bf("ksp_done")], dma=True)
                        pump(1)
                for bi, blk in enumerate(blocks):
                    pp = bi % 2
                    psv = [ps_a, ps_b][pp]
                    psvb = Bf(f"p1ab{pp}")
                    for c in range(16):
                        Sd.op("pe", lambda e, c=c, bi=bi, lsl=lsl, lstr=lstr, psv=psv:
                              e.matmul(psv[:],
                                       ln0[lsl][:, lstr * c + 128 * bi:lstr * c + 128 * bi + 128],
                                       wkvl[:, 1024 * c + 512:1024 * c + 1024],
                                       start=(c == 0), stop=False),
                              reads=[Bf(f"p1ln{lsl}"), Bf("p1wkv")], writes=[psvb])
                    Sd.op("pe", lambda e, psv=psv:
                          e.matmul(psv[:], sb["c_ones1"][:], sb["c_bkvv_bf"][:],
                                   start=False, stop=True),
                          reads=[Bf("c_ones1"), Bf("c_bkvv_bf")], writes=[psvb])
                    Sd.op("act", lambda e, pp=pp, psv=psv: e.copy(stg[pp][:], psv[:]),
                          reads=[psvb], writes=[Bf(f"p1stg{pp}")])
                    Sd.op("act", lambda e, blk=blk, pp=pp:
                          e.dma_start(out=VSPD[blk], in_=stg[pp][:]),
                          reads=[Bf(f"p1stg{pp}")], writes=[Bf("vsp_done")], dma=True)
                    pump(1)
                if G >= 1:
                    g = G - 1
                    pump_until("wq")
                    for mg in range(8):
                        for c in range(16):
                            wsl = c % 2
                            Sd.op("sync", lambda e, mg=mg, c=c, wsl=wsl: e.dma_start(
                                out=wqt[wsl][:],
                                in_=rb(WBF["wq"])[c][:, 256 * mg:256 * mg + 256]),
                                reads=[Bf("wbf_wq")], writes=[Bf(f"p1wqt{wsl}")], dma=True)
                            for mi in range(2):
                                Sd.op("pe", lambda e, mi=mi, c=c, lsl=lsl, lstr=lstr, wsl=wsl:
                                      e.matmul(ps_q[mi][:],
                                               wqt[wsl][:, 128 * mi:128 * mi + 128],
                                               ln0[lsl][:, lstr * c:lstr * c + 512],
                                               start=(c == 0), stop=(c == 15)),
                                      reads=[Bf(f"p1wqt{wsl}"), Bf(f"p1ln{lsl}")],
                                      writes=[Bf(f"p1q{mi}")])
                        for mi in range(2):
                            m = 2 * mg + mi
                            pp = mi % 2
                            Sd.op("act", lambda e, m=m, mi=mi, pp=pp:
                                  e.add(rawf[pp][:], ps_q[mi][:], sb["c_bq"][:, m:m + 1]),
                                  reads=[Bf(f"p1q{mi}"), Bf("c_bq")], writes=[Bf(f"p1rawf{pp}")])
                            rope4(rawf[pp][:], Bf(f"p1rawf{pp}"),
                                  rawb[pp][:], Bf(f"p1rawb{pp}"),
                                  tcq[:], tsq[:],
                                  stg[pp][:], Bf(f"p1stg{pp}"), 512,
                                  "c_rspt", ps_bc, Bf("p1bc"),
                                  t1f[pp], Bf(f"p1t1f{pp}"), t2f[pp], Bf(f"p1t2f{pp}"),
                                  Bf("ctcq"), Bf("ctsq"))
                            Sd.op("act", lambda e, m=m, g=g, pp=pp:
                                  e.dma_start(out=QSPD[m, :, 512 * g:512 * g + 512],
                                              in_=stg[pp][:]),
                                  reads=[Bf(f"p1stg{pp}")], writes=[Bf("qsp_done")], dma=True)
                        pump(1)
            Sd.barrier()

        phase1()
        p0s.close()

        def phase2():
          with contextlib.ExitStack() as p2:
            load_packed("c_mwin", 128, 36, 512, BF16,
                        MWIN.rearrange("g r p q -> p (g r) q"), p2)
            qsp = alloc("p2qsp", [128, 16 * 512], BF16, p2)
            ksp = alloc("p2ksp", [128, 4 * 1152], BF16, p2)
            vsp = alloc("p2vsp", [128, 9 * 512], BF16, p2)
            prb2 = [alloc(f"p2prb{i}", [128, 9 * 512], BF16, p2) for i in range(2)]
            y2g = alloc("p2y2", [128, 16 * 512], BF16, p2)
            tmpf = alloc("p2tmpf", [128, 512], F32, p2)
            srow = alloc("p2srow", [1, 512], F32, p2)
            rrow = alloc("p2rrow", [1, 512], F32, p2)
            r2a = alloc("p2r2a", [1, 512], F32, p2)
            r2b = alloc("p2r2b", [1, 512], F32, p2)
            wom = [alloc(f"p2wo{i}", [128, 16 * 128], BF16, p2) for i in range(2)]
            xcc = [alloc(f"p2xc{i}", [128, 512], F32, p2) for i in range(2)]
            xmf = alloc("p2xmf", [128, 512], F32, p2)
            xsq = alloc("p2xsq", [128, 512], F32, p2)
            ps_sc = [psum(f"p2sc{i}", [128, 512], F32, p2) for i in range(2)]
            ps_av = psum("p2av", [128, 512], F32, p2)
            ps_s1 = psum("p2s1", [1, 512], F32, p2)
            ps_rb = psum("p2rb", [128, 512], F32, p2)
            ps_wo = psum("p2wo_", [128, 512], F32, p2)
            ps_st_m, ps_st_e = sb["p0m"], sb["p0e"]

            for g in range(4):
                qc0 = 512 * g
                for m in range(16):
                    Sd.op("sync", lambda e, m=m, qc0=qc0: e.dma_start(
                        out=qsp[:, 512 * m:512 * m + 512],
                        in_=QSPD[m, :, qc0:qc0 + 512]),
                        reads=[Bf("qsp_done")], writes=[Bf("p2qsp")], dma=True)
                for kc in range(4):
                    Sd.op("sync", lambda e, kc=kc: e.dma_start(
                        out=ksp[:, 1152 * kc:1152 * kc + 128], in_=KSPD[kc, :, 0:128]),
                        reads=[Bf("ksp_done")], writes=[Bf("p2ksp")], dma=True)
                    Sd.op("sync", lambda e, kc=kc, g=g: e.dma_start(
                        out=ksp[:, 1152 * kc + 128:1152 * kc + 1152],
                        in_=KSPD[kc, :, 128 * (4 * g + 1):128 * (4 * g + 9)]),
                        reads=[Bf("ksp_done")], writes=[Bf("p2ksp")], dma=True)
                for r in range(9):
                    blk = 0 if r == 0 else 4 * g + r
                    Sd.op("sync", lambda e, r=r, blk=blk: e.dma_start(
                        out=vsp[:, 512 * r:512 * r + 512], in_=VSPD[blk]),
                        reads=[Bf("vsp_done")], writes=[Bf("p2vsp")], dma=True)

                for h in range(16):
                    gv = h // 4
                    prb = prb2[h % 2]
                    prbB = Bf(f"p2prb{h % 2}")
                    for r in range(9):
                        pp = r % 2
                        Sd.op("pe", lambda e, h=h, gv=gv, r=r, pp=pp:
                              e.matmul(ps_sc[pp][:],
                                       ksp[:, 1152 * gv + 128 * r:1152 * gv + 128 * r + 128],
                                       qsp[:, 512 * h:512 * h + 512],
                                       start=True, stop=True),
                              reads=[Bf("p2ksp"), Bf("p2qsp")], writes=[Bf(f"p2sc{pp}")])
                        Sd.op("act", lambda e, r=r, pp=pp, prb=prb:
                              e.activation(prb[:, 512 * r:512 * r + 512], ps_sc[pp][:],
                                           AF.Exp),
                              reads=[Bf(f"p2sc{pp}")], writes=[prbB])
                    Sd.op("pool", lambda e, g=g, prb=prb:
                          e.tensor_mul(prb[:], prb[:],
                                       sb["c_mwin"][:, 4608 * g:4608 * (g + 1)]),
                          reads=[Bf("c_mwin")], writes=[prbB])
                    for r in range(9):
                        Sd.op("pe", lambda e, r=r, prb=prb:
                              e.matmul(ps_s1[:], sb["c_ones128"][:],
                                       prb[:, 512 * r:512 * r + 512],
                                       start=(r == 0), stop=(r == 8)),
                              reads=[prbB, Bf("c_ones128")], writes=[Bf("p2s1")])
                        Sd.op("pe", lambda e, gv=gv, r=r, prb=prb:
                              e.matmul(ps_av[:],
                                       vsp[:, 512 * r + 128 * gv:512 * r + 128 * gv + 128],
                                       prb[:, 512 * r:512 * r + 512],
                                       start=(r == 0), stop=(r == 8)),
                              reads=[Bf("p2vsp"), prbB], writes=[Bf("p2av")])
                    Sd.op("act", lambda e: e.copy(srow[:], ps_s1[:]),
                          reads=[Bf("p2s1")], writes=[Bf("p2srow")])
                    Sd.op("dve", lambda e: e.reciprocal(rrow[:], srow[:]),
                          reads=[Bf("p2srow")], writes=[Bf("p2rrow")])
                    Sd.op("pe", lambda e: e.matmul(ps_rb[:], sb["c_ones1f"][:],
                                                   rrow[:], start=True, stop=True),
                          reads=[Bf("p2rrow"), Bf("c_ones1f")], writes=[Bf("p2rb")])
                    Sd.op("act", lambda e: e.copy(tmpf[:], ps_av[:]),
                          reads=[Bf("p2av")], writes=[Bf("p2tmpf")])
                    Sd.op("dve", lambda e, h=h:
                          e.tensor_tensor(y2g[:, 512 * h:512 * h + 512], tmpf[:],
                                          ps_rb[:], ALU.mult),
                          reads=[Bf("p2tmpf"), Bf("p2rb")], writes=[Bf("p2y2")])
                    pump(3)
                if TAPS:
                    Sd.op("act", lambda e, g=g: e.dma_start(
                        out=Y2D[g], in_=y2g[:]),
                        reads=[Bf("p2y2")], writes=[Bf("y2_done")], dma=True)
                pump_until("wo")
                for m in range(16):
                    wsl = m % 2
                    pump(1)
                    Sd.op("sync", lambda e, m=m, wsl=wsl: e.dma_start(
                        out=wom[wsl][:].rearrange("p (n q) -> p n q", q=128),
                        in_=rb(WBF["wo"])[:, :, 128 * m:128 * m + 128]
                        .rearrange("n p q -> p n q")),
                        reads=[Bf("wbf_wo")], writes=[Bf(f"p2wo{wsl}")], dma=True)
                    for c in range(16):
                        Sd.op("pe", lambda e, c=c, wsl=wsl:
                              e.matmul(ps_wo[:], wom[wsl][:, 128 * c:128 * c + 128],
                                       y2g[:, 512 * c:512 * c + 512],
                                       start=(c == 0), stop=(c == 15)),
                              reads=[Bf(f"p2wo{wsl}"), Bf("p2y2")], writes=[Bf("p2wops")])
                    xsl = m % 2
                    Sd.op("sync", lambda e, m=m, qc0=qc0, xsl=xsl: e.dma_start(
                        out=xcc[xsl][:], in_=X[m, :, OWN0 + qc0:OWN0 + qc0 + 512]),
                        writes=[Bf(f"p2xc{xsl}")], dma=True)
                    Sd.op("dve", lambda e, m=m, xsl=xsl:
                          e.scalar_tensor_tensor(xmf[:], ps_wo[:],
                                                 sb["c_bo"][:, m:m + 1], xcc[xsl][:],
                                                 ALU.add, ALU.add),
                          reads=[Bf("p2wops"), Bf("c_bo"), Bf(f"p2xc{xsl}")],
                          writes=[Bf("p2xmf")])
                    Sd.op("act", lambda e, m=m, qc0=qc0: e.dma_start(
                        out=XMIDD[m, :, qc0:qc0 + 512], in_=xmf[:]),
                        reads=[Bf("p2xmf")], writes=[Bf("xmid_done")], dma=True)
                    Sd.op("pool", lambda e: e.tensor_mul(xsq[:], xmf[:], xmf[:]),
                          reads=[Bf("p2xmf")], writes=[Bf("p2xsq")])
                    Sd.op("pe", lambda e, m=m, ps_st_m=ps_st_m:
                          e.matmul(ps_st_m[:], sb["c_ones128f"][:], xmf[:],
                                   start=(m == 0), stop=(m == 15)),
                          reads=[Bf("p2xmf"), Bf("c_ones128f")], writes=[Bf("p0m")])
                    Sd.op("pe", lambda e, m=m, ps_st_e=ps_st_e:
                          e.matmul(ps_st_e[:], sb["c_ones128f"][:], xsq[:],
                                   start=(m == 0), stop=(m == 15)),
                          reads=[Bf("p2xsq"), Bf("c_ones128f")], writes=[Bf("p0e")])
                Sd.op("act", lambda e, ps_st_m=ps_st_m: e.copy(r2a[:], ps_st_m[:]),
                      reads=[Bf("p0m")], writes=[Bf("p2r2a")])
                Sd.op("act", lambda e, ps_st_e=ps_st_e: e.copy(r2b[:], ps_st_e[:]),
                      reads=[Bf("p0e")], writes=[Bf("p2r2b")])
                Sd.op("act", lambda e, qc0=qc0: e.dma_start(
                    out=ROWD[2:3, qc0:qc0 + 512], in_=r2a[:]),
                    reads=[Bf("p2r2a")], writes=[Bf("row_mf")], dma=True)
                Sd.op("act", lambda e, qc0=qc0: e.dma_start(
                    out=ROWD[3:4, qc0:qc0 + 512], in_=r2b[:]),
                    reads=[Bf("p2r2b")], writes=[Bf("row_ef")], dma=True)
            Sd.barrier()

        phase2()

        def phase34():
          with contextlib.ExitStack() as p34:
            qp_sb = alloc("qp_sb", [128, 8 * 1024], BF16, p34)
            qr_sb = alloc("qr_sb", [128, 8 * 1024], BF16, p34)
            rc1 = alloc("p3c1", [1, NQ], F32, p34)
            rac1 = alloc("p3ac1", [1, NQ], F32, p34)

            with contextlib.ExitStack() as p3:
                r3m = alloc("p3rm", [1, NQ], F32, p3)
                r3e = alloc("p3re", [1, NQ], F32, p3)
                r3t1 = alloc("p3t1", [1, NQ], F32, p3)
                r3sf = alloc("p3sf", [1, NQ], F32, p3)
                r3asf = alloc("p3asf", [1, NQ], F32, p3)
                r3t2 = r3sf
                Sd.op("sync", lambda e: e.dma_start(out=r3m[:], in_=ROWD[2:3, 0:NQ]),
                      reads=[Bf("row_mf")], writes=[Bf("p3rm")], dma=True)
                Sd.op("sync", lambda e: e.dma_start(out=r3e[:], in_=ROWD[3:4, 0:NQ]),
                      reads=[Bf("row_ef")], writes=[Bf("p3re")], dma=True)
                row_coefs(r3m, Bf("p3rm"), r3e, Bf("p3re"),
                          r3sf, Bf("p3sf"), r3asf, Bf("p3asf"),
                          r3t1, Bf("p3t1"), r3t2, Bf("p3sf"), NQ)
                Sd.op("dve", lambda e: e.tensor_scalar(r3m[:], r3t1[:], EPS, None, ALU.add),
                      reads=[Bf("p3t1")], writes=[Bf("p3rm")])
                Sd.op("dve", lambda e: e.reciprocal(r3m[:], r3m[:]),
                      reads=[Bf("p3rm")], writes=[Bf("p3rm")])
                Sd.op("dve", lambda e: e.tensor_mul(r3m[:], r3t1[:], r3m[:]),
                      reads=[Bf("p3t1")], writes=[Bf("p3rm")])
                Sd.op("act", lambda e: e.activation(r3m[:], r3m[:], AF.Sqrt, bias=sb["c_eps"][:]),
                      reads=[Bf("p3rm"), Bf("c_eps")], writes=[Bf("p3rm")])
                Sd.op("dve", lambda e: e.reciprocal(r3m[:], r3m[:]),
                      reads=[Bf("p3rm")], writes=[Bf("p3rm")])
                Sd.op("dve", lambda e: e.tensor_mul(rc1[:], r3sf[:], r3m[:]),
                      reads=[Bf("p3sf"), Bf("p3rm")], writes=[Bf("p3c1")])
                Sd.op("dve", lambda e: e.tensor_mul(rac1[:], r3asf[:], r3m[:]),
                      reads=[Bf("p3asf"), Bf("p3rm")], writes=[Bf("p3ac1")])
                Sd.op("act", lambda e: e.dma_start(out=ROWD[4:5, 0:NQ], in_=r3sf[:]),
                      reads=[Bf("p3sf")], writes=[Bf("row_sf")], dma=True)
                Sd.op("act", lambda e: e.dma_start(out=ROWD[5:6, 0:NQ], in_=r3asf[:]),
                      reads=[Bf("p3asf")], writes=[Bf("row_asf")], dma=True)
                ln1rf = load_const("c_ln1r", [1, 2048], F32, LN1R, p3)
                tcm = alloc("p3tcm", [128, 512], F32, p3)
                tsm = alloc("p3tsm", [128, 512], F32, p3)

                kul = alloc("p3ku", [128, 4 * 1024], BF16, p3)
                qul = alloc("p3qu", [128, 4 * 1024], BF16, p3)
                rql = alloc("p3rq", [128, 4 * 1024], BF16, p3)
                pump_until("ropeq")
                for nm, t in [("ku", kul), ("qu", qul), ("ropeq", rql)]:
                    for c in range(4):
                        Sd.op("sync", lambda e, nm=nm, t=t, c=c: e.dma_start(
                            out=t[:, 1024 * c:1024 * (c + 1)], in_=rb(WBF[nm])[c]),
                            reads=[Bf("wbf_" + nm)], writes=[Bf("p3w_" + nm)], dma=True)

                xmc = [alloc(f"p3xm{i}", [128, 512], F32, p3) for i in range(2)]
                rms1 = alloc("p3rms1", [128, 16 * 512], BF16, p3)
                kvdb = alloc("p3kvd", [128, 4 * 512], BF16, p3)
                qdb = alloc("p3qd", [128, 4 * 512], BF16, p3)
                stg = [alloc(f"p3stg{i}", [128, 512], BF16, p3) for i in range(2)]
                rawf = [alloc(f"p3rawf{i}", [128, 512], F32, p3) for i in range(2)]
                rawb = [alloc(f"p3rawb{i}", [128, 512], BF16, p3) for i in range(2)]
                t1f = [alloc(f"p3t1f{i}", [128, 512], F32, p3) for i in range(2)]
                t2f = [alloc(f"p3t2f{i}", [128, 512], F32, p3) for i in range(2)]
                wst = [alloc(f"p3wst{i}", [128, 2048], BF16, p3) for i in range(2)]
                ps_bc = psum("p3bc", [128, 1024], F32, p3)
                ps_a = psum("p3a_", [128, 512], F32, p3)
                ps_b = psum("p3b_", [128, 512], F32, p3)
                ps_rot = psum("p3rot", [128, 512], F32, p3)

                for g in range(4):
                    qc0 = 512 * g
                    Sd.op("sync", lambda e, qc0=qc0: e.dma_start(
                        out=tcm[:], in_=COSM[:, qc0:qc0 + 512]),
                        writes=[Bf("ctcm")], dma=True)
                    Sd.op("sync", lambda e, qc0=qc0: e.dma_start(
                        out=tsm[:], in_=SINM[:, qc0:qc0 + 512]),
                        writes=[Bf("ctsm")], dma=True)
                    Sd.op("pe", lambda e, qc0=qc0: e.matmul(
                        ps_bc[:, 0:512], sb["c_ones1f"][:], rc1[:, qc0:qc0 + 512],
                        start=True, stop=True),
                        reads=[Bf("p3c1"), Bf("c_ones1f")], writes=[Bf("p3bc")])
                    for c in range(16):
                        xsl = c % 2
                        Sd.op("sync", lambda e, c=c, qc0=qc0, xsl=xsl: e.dma_start(
                            out=xmc[xsl][:], in_=XMIDD[c, :, qc0:qc0 + 512]),
                            reads=[Bf("xmid_done")], writes=[Bf(f"p3xm{xsl}")], dma=True)
                        Sd.op("pe", lambda e, c=c, qc0=qc0: e.matmul(
                            ps_bc[:, 512:1024], ln1rf[:, 128 * c:128 * c + 128],
                            rac1[:, qc0:qc0 + 512], start=True, stop=True),
                            reads=[Bf("p3ac1"), Bf("c_ln1r")], writes=[Bf("p3bc2")])
                        Sd.op("dve", lambda e, xsl=xsl: e.tensor_tensor(
                            xmc[xsl][:], xmc[xsl][:], ps_bc[:, 0:512], ALU.mult),
                            reads=[Bf("p3bc")], writes=[Bf(f"p3xm{xsl}")])
                        Sd.op("dve", lambda e, c=c, xsl=xsl: e.scalar_tensor_tensor(
                            rms1[:, 512 * c:512 * c + 512], xmc[xsl][:],
                            sb["c_ln1p"][:, c:c + 1], ps_bc[:, 512:1024],
                            ALU.mult, ALU.add),
                            reads=[Bf(f"p3xm{xsl}"), Bf("c_ln1p"), Bf("p3bc2")],
                            writes=[Bf("p3rms1")])
                    projs = [("kvd", kvdb, "p3kvdb")]
                    if g < 2:
                        projs.append(("qd", qdb, "p3qdb"))
                    for nm, dstt, dstb in projs:
                        for mc in range(4):
                            wsl = mc % 2
                            Sd.op("sync", lambda e, nm=nm, mc=mc, wsl=wsl: e.dma_start(
                                out=wst[wsl][:].rearrange("p (n q) -> p n q", q=128),
                                in_=rb(WBF[nm])[:, :, 128 * mc:128 * mc + 128]
                                .rearrange("n p q -> p n q")),
                                reads=[Bf("wbf_" + nm)], writes=[Bf(f"p3wst{wsl}")], dma=True)
                            pp = mc % 2
                            psx = [ps_a, ps_b][pp]
                            for c in range(16):
                                Sd.op("pe", lambda e, c=c, wsl=wsl, psx=psx: e.matmul(
                                    psx[:], wst[wsl][:, 128 * c:128 * c + 128],
                                    rms1[:, 512 * c:512 * c + 512],
                                    start=(c == 0), stop=(c == 15)),
                                    reads=[Bf(f"p3wst{wsl}"), Bf("p3rms1")],
                                    writes=[Bf(f"p3ab{pp}")])
                            Sd.op("act", lambda e, dstt=dstt, mc=mc, psx=psx: e.copy(
                                dstt[:, 512 * mc:512 * mc + 512], psx[:]),
                                reads=[Bf(f"p3ab{pp}")], writes=[Bf(dstb)])
                    for pc in range(8):
                        pp = pc % 2
                        psx = [ps_a, ps_b][pp]
                        for kc in range(4):
                            Sd.op("pe", lambda e, pc=pc, kc=kc, psx=psx: e.matmul(
                                psx[:], kul[:, 1024 * kc + 128 * pc:1024 * kc + 128 * pc + 128],
                                kvdb[:, 512 * kc:512 * kc + 512],
                                start=(kc == 0), stop=(kc == 3)),
                                reads=[Bf("p3w_ku"), Bf("p3kvdb")], writes=[Bf(f"p3ab{pp}")])
                        Sd.op("act", lambda e, pp=pp, psx=psx: e.copy(stg[pp][:], psx[:]),
                              reads=[Bf(f"p3ab{pp}")], writes=[Bf(f"p3stg{pp}")])
                        Sd.op("act", lambda e, pc=pc, qc0=qc0, pp=pp: e.dma_start(
                            out=KPD[pc, :, qc0:qc0 + 512], in_=stg[pp][:]),
                            reads=[Bf(f"p3stg{pp}")], writes=[Bf("kp_done")], dma=True)
                        pump(1)
                    for pc in range(8):
                        wsl = pc % 2
                        pp = pc % 2
                        psx = [ps_a, ps_b][pp]
                        Sd.op("sync", lambda e, pc=pc, wsl=wsl: e.dma_start(
                            out=wst[wsl][:].rearrange("p (n q) -> p n q", q=128),
                            in_=rb(WBF["ropek"])[:, :, 128 * pc:128 * pc + 128]
                            .rearrange("n p q -> p n q")),
                            reads=[Bf("wbf_ropek")], writes=[Bf(f"p3wst{wsl}")], dma=True)
                        for c in range(16):
                            Sd.op("pe", lambda e, c=c, wsl=wsl, psx=psx: e.matmul(
                                psx[:], wst[wsl][:, 128 * c:128 * c + 128],
                                rms1[:, 512 * c:512 * c + 512],
                                start=(c == 0), stop=(c == 15)),
                                reads=[Bf(f"p3wst{wsl}"), Bf("p3rms1")],
                                writes=[Bf(f"p3ab{pp}")])
                        Sd.op("act", lambda e, pp=pp, psx=psx: e.copy(rawf[pp][:], psx[:]),
                              reads=[Bf(f"p3ab{pp}")], writes=[Bf(f"p3rawf{pp}")])
                        rope4(rawf[pp][:], Bf(f"p3rawf{pp}"),
                              rawb[pp][:], Bf(f"p3rawb{pp}"),
                              tcm[:], tsm[:],
                              stg[pp][:], Bf(f"p3stg{pp}"), 512,
                              "c_rmlat", ps_rot, Bf("p3rot"),
                              t1f[pp], Bf(f"p3t1f{pp}"), t2f[pp], Bf(f"p3t2f{pp}"),
                              Bf("ctcm"), Bf("ctsm"))
                        Sd.op("act", lambda e, pc=pc, qc0=qc0, pp=pp: e.dma_start(
                            out=KRD[pc, :, qc0:qc0 + 512], in_=stg[pp][:]),
                            reads=[Bf(f"p3stg{pp}")], writes=[Bf("kr_done")], dma=True)
                        pump(1)
                    for vh in range(4):
                        wsl = vh % 2
                        Sd.op("sync", lambda e, vh=vh, wsl=wsl: e.dma_start(
                            out=wst[wsl][:].rearrange("p (n q) -> p n q", q=512),
                            in_=rb(WBF["vu"])[:, :, 512 * vh:512 * vh + 512]
                            .rearrange("n p q -> p n q")),
                            reads=[Bf("wbf_vu")], writes=[Bf(f"p3wst{wsl}")], dma=True)
                        for bk in range(4):
                            pp = bk % 2
                            psx = [ps_a, ps_b][pp]
                            for kc in range(4):
                                Sd.op("pe", lambda e, bk=bk, kc=kc, wsl=wsl, psx=psx:
                                      e.matmul(psx[:],
                                               kvdb[:, 512 * kc + 128 * bk:512 * kc + 128 * bk + 128],
                                               wst[wsl][:, 512 * kc:512 * kc + 512],
                                               start=(kc == 0), stop=(kc == 3)),
                                      reads=[Bf("p3kvdb"), Bf(f"p3wst{wsl}")],
                                      writes=[Bf(f"p3ab{pp}")])
                            Sd.op("act", lambda e, pp=pp, psx=psx: e.copy(stg[pp][:], psx[:]),
                                  reads=[Bf(f"p3ab{pp}")], writes=[Bf(f"p3stg{pp}")])
                            Sd.op("act", lambda e, g=g, bk=bk, vh=vh, pp=pp: e.dma_start(
                                out=VMLAD[4 * g + bk, :, 512 * vh:512 * vh + 512],
                                in_=stg[pp][:]),
                                reads=[Bf(f"p3stg{pp}")], writes=[Bf("vmla_done")], dma=True)
                            pump(1)
                    if g < 2:
                        for pc in range(8):
                            pp = pc % 2
                            psx = [ps_a, ps_b][pp]
                            for kc in range(4):
                                Sd.op("pe", lambda e, pc=pc, kc=kc, psx=psx: e.matmul(
                                    psx[:], qul[:, 1024 * kc + 128 * pc:1024 * kc + 128 * pc + 128],
                                    qdb[:, 512 * kc:512 * kc + 512],
                                    start=(kc == 0), stop=(kc == 3)),
                                    reads=[Bf("p3w_qu"), Bf("p3qdb")], writes=[Bf(f"p3ab{pp}")])
                            Sd.op("dve", lambda e, pc=pc, qc0=qc0, psx=psx: e.tensor_copy(
                                qp_sb[:, 1024 * pc + qc0:1024 * pc + qc0 + 512], psx[:]),
                                reads=[Bf(f"p3ab{pp}")], writes=[Bf("qp_sb")])
                        for pc in range(8):
                            pp = pc % 2
                            psx = [ps_a, ps_b][pp]
                            for kc in range(4):
                                Sd.op("pe", lambda e, pc=pc, kc=kc, psx=psx: e.matmul(
                                    psx[:], rql[:, 1024 * kc + 128 * pc:1024 * kc + 128 * pc + 128],
                                    qdb[:, 512 * kc:512 * kc + 512],
                                    start=(kc == 0), stop=(kc == 3)),
                                    reads=[Bf("p3w_ropeq"), Bf("p3qdb")], writes=[Bf(f"p3ab{pp}")])
                            Sd.op("act", lambda e, pp=pp, psx=psx: e.copy(rawf[pp][:], psx[:]),
                                  reads=[Bf(f"p3ab{pp}")], writes=[Bf(f"p3rawf{pp}")])
                            rope4(rawf[pp][:], Bf(f"p3rawf{pp}"),
                                  rawb[pp][:], Bf(f"p3rawb{pp}"),
                                  tcm[:], tsm[:],
                                  stg[pp][:], Bf(f"p3stg{pp}"), 512,
                                  "c_rmlat", ps_rot, Bf("p3rot"),
                                  t1f[pp], Bf(f"p3t1f{pp}"), t2f[pp], Bf(f"p3t2f{pp}"),
                                  Bf("ctcm"), Bf("ctsm"))
                            Sd.op("dve", lambda e, pc=pc, qc0=qc0, pp=pp: e.tensor_copy(
                                qr_sb[:, 1024 * pc + qc0:1024 * pc + qc0 + 512], stg[pp][:]),
                                reads=[Bf(f"p3stg{pp}")], writes=[Bf("qr_sb")])
                Sd.barrier()

            with contextlib.ExitStack() as p4:
                load_packed("c_tri", 128, 8, NOWN, BF16,
                            TRIM.rearrange("j p q -> p j q"), p4)
                y2 = alloc("p4y2", [128, 16 * 1024], BF16, p4)
                r4sf = alloc("p4rsf", [1, NOWN], F32, p4)
                r4asf = alloc("p4rasf", [1, NOWN], F32, p4)
                Sd.op("sync", lambda e: e.dma_start(out=r4sf[:], in_=ROWD[4:5, 0:NOWN]),
                      reads=[Bf("row_sf")], writes=[Bf("p4rsf")], dma=True)
                Sd.op("sync", lambda e: e.dma_start(out=r4asf[:], in_=ROWD[5:6, 0:NOWN]),
                      reads=[Bf("row_asf")], writes=[Bf("p4rasf")], dma=True)

                with contextlib.ExitStack() as p4a:
                    kpr = alloc("p4kpr", [128, 4 * 2048], BF16, p4a)
                    qpr = alloc("p4qpr", [128, 4 * 1024], BF16, p4a)
                    vh_sb = alloc("p4v", [128, 16 * 1024], BF16, p4a)
                    prb2 = [alloc(f"p4prb{i}", [128, 1024], BF16, p4a) for i in range(2)]
                    tmpf = alloc("p4tmpf", [128, 1024], F32, p4a)
                    srow = alloc("p4srow", [1, 1024], F32, p4a)
                    rrow = alloc("p4rrow", [1, 1024], F32, p4a)
                    ps_sc = [psum(f"p4sc{i}", [128, 1024], F32, p4a) for i in range(2)]
                    ps_av = psum("p4av", [128, 1024], F32, p4a)
                    ps_s1h = [sb["p0m"], sb["p0e"]]

                    def load_kq_round(hh, r):
                        for a in range(4 * r, 4 * r + 4):
                            h = 8 * hh + a
                            pc, base = h // 2, 64 * (h % 2)
                            s = a % 4
                            Sd.op("sync", lambda e, s=s, pc=pc, base=base: e.dma_start(
                                out=kpr[0:64, 2048 * s:2048 * (s + 1)],
                                in_=KPD[pc, base:base + 64, :]),
                                reads=[Bf("kp_done")], writes=[Bf("p4kpr")], dma=True)
                            Sd.op("sync", lambda e, s=s, pc=pc, base=base: e.dma_start(
                                out=kpr[64:128, 2048 * s:2048 * (s + 1)],
                                in_=KRD[pc, base:base + 64, :]),
                                reads=[Bf("kr_done")], writes=[Bf("p4kpr")], dma=True)
                            Sd.op("sync", lambda e, s=s, pc=pc, base=base: e.dma_start(
                                out=qpr[0:64, 1024 * s:1024 * (s + 1)],
                                in_=qp_sb[base:base + 64, 1024 * pc:1024 * pc + 1024]),
                                reads=[Bf("qp_sb")], writes=[Bf("p4qpr")], dma=True)
                            Sd.op("sync", lambda e, s=s, pc=pc, base=base: e.dma_start(
                                out=qpr[64:128, 1024 * s:1024 * (s + 1)],
                                in_=qr_sb[base:base + 64, 1024 * pc:1024 * pc + 1024]),
                                reads=[Bf("qr_sb")], writes=[Bf("p4qpr")], dma=True)

                    for hh in range(2):
                        load_kq_round(hh, 0)
                        for j in range(16):
                            Sd.op("sync", lambda e, hh=hh, j=j: e.dma_start(
                                out=vh_sb[:, 1024 * j:1024 * (j + 1)],
                                in_=VMLAD[j, :, 1024 * hh:1024 * hh + 1024]),
                                reads=[Bf("vmla_done")], writes=[Bf("p4v")], dma=True)
                        for h in range(8 * hh, 8 * hh + 8):
                            a = h - 8 * hh
                            if a == 4:
                                load_kq_round(hh, 1)
                            s = a % 4
                            prb = prb2[h % 2]
                            prbB = Bf(f"p4prb{h % 2}")
                            for j in range(16):
                                pp = j % 2
                                for qh in range(2):
                                    Sd.op("pe", lambda e, s=s, j=j, qh=qh, pp=pp:
                                          e.matmul(ps_sc[pp][:, 512 * qh:512 * qh + 512],
                                                   kpr[:, 2048 * s + 128 * j:2048 * s + 128 * j + 128],
                                                   qpr[:, 1024 * s + 512 * qh:1024 * s + 512 * qh + 512],
                                                   start=True, stop=True),
                                          reads=[Bf("p4kpr"), Bf("p4qpr")], writes=[Bf(f"p4sc{pp}")])
                                Sd.op("act", lambda e, j=j, pp=pp, prb=prb:
                                      e.activation(prb[:], ps_sc[pp][:], AF.Exp,
                                                   bias=sb["c_bmla"][:, j:j + 1]),
                                      reads=[Bf(f"p4sc{pp}"), Bf("c_bmla")], writes=[prbB])
                                pump(1)
                                if j < 8:
                                    Sd.op("pool", lambda e, j=j, prb=prb:
                                          e.tensor_mul(prb[:], prb[:],
                                                       sb["c_tri"][:, NOWN * j:NOWN * (j + 1)]),
                                          reads=[Bf("c_tri")], writes=[prbB])
                                for qh in range(2):
                                    Sd.op("pe", lambda e, j=j, qh=qh, prb=prb:
                                          e.matmul(ps_s1h[qh][:],
                                                   sb["c_ones128"][:],
                                                   prb[:, 512 * qh:512 * qh + 512],
                                                   start=(j == 0), stop=(j == 15)),
                                          reads=[prbB, Bf("c_ones128")], writes=[Bf("p0m"), Bf("p0e")])
                                    Sd.op("pe", lambda e, h=h, hh=hh, j=j, qh=qh, prb=prb:
                                          e.matmul(ps_av[:, 512 * qh:512 * qh + 512],
                                                   vh_sb[:, 1024 * j + 128 * (h - 8 * hh):
                                                         1024 * j + 128 * (h - 8 * hh) + 128],
                                                   prb[:, 512 * qh:512 * qh + 512],
                                                   start=(j == 0), stop=(j == 15)),
                                          reads=[Bf("p4v"), prbB], writes=[Bf("p4av")])
                            Sd.op("act", lambda e: e.copy(srow[:, 0:512], ps_s1h[0][:]),
                                  reads=[Bf("p0m")], writes=[Bf("p4srow")])
                            Sd.op("act", lambda e: e.copy(srow[:, 512:1024], ps_s1h[1][:]),
                                  reads=[Bf("p0e")], writes=[Bf("p4srow")])
                            Sd.op("dve", lambda e: e.reciprocal(rrow[:], srow[:]),
                                  reads=[Bf("p4srow")], writes=[Bf("p4rrow")])
                            for qh in range(2):
                                Sd.op("pe", lambda e, qh=qh: e.matmul(
                                    ps_sc[0][:, 512 * qh:512 * qh + 512], sb["c_ones1f"][:],
                                    rrow[:, 512 * qh:512 * qh + 512], start=True, stop=True),
                                    reads=[Bf("p4rrow"), Bf("c_ones1f")], writes=[Bf("p4sc0")])
                            Sd.op("act", lambda e: e.copy(tmpf[:], ps_av[:]),
                                  reads=[Bf("p4av")], writes=[Bf("p4tmpf")])
                            Sd.op("dve", lambda e, h=h: e.tensor_tensor(
                                y2[:, 1024 * h:1024 * (h + 1)], tmpf[:], ps_sc[0][:], ALU.mult),
                                reads=[Bf("p4tmpf"), Bf("p4sc0")], writes=[Bf("p4y2")])
                    Sd.barrier()

                with contextlib.ExitStack() as p4b:
                    wom = [alloc(f"p4wo{i}", [128, 16 * 128], BF16, p4b) for i in range(2)]
                    xmc4 = [alloc(f"p4xm{i}", [128, 1024], F32, p4b) for i in range(2)]
                    hcc = alloc("p4hc", [128, 1024], F32, p4b)
                    scc = alloc("p4scc", [128, 1024], F32, p4b)
                    t2c = alloc("p4t2c", [128, 1024], F32, p4b)
                    ps_ow = psum("p4ow", [128, 1024], F32, p4b)
                    ps_sf = psum("p4sf_", [128, 1024], F32, p4b)
                    ps_asf = psum("p4asf_", [128, 1024], F32, p4b)
                    for qh in range(2):
                        Sd.op("pe", lambda e, qh=qh: e.matmul(
                            ps_sf[:, 512 * qh:512 * qh + 512], sb["c_ones1f"][:],
                            r4sf[:, 512 * qh:512 * qh + 512], start=True, stop=True),
                            reads=[Bf("p4rsf"), Bf("c_ones1f")], writes=[Bf("p4psf")])
                        Sd.op("pe", lambda e, qh=qh: e.matmul(
                            ps_asf[:, 512 * qh:512 * qh + 512], sb["c_ones1f"][:],
                            r4asf[:, 512 * qh:512 * qh + 512], start=True, stop=True),
                            reads=[Bf("p4rasf"), Bf("c_ones1f")], writes=[Bf("p4pasf")])
                    pump_until("ow")
                    for m in range(16):
                        wsl = m % 2
                        pump(1)
                        Sd.op("sync", lambda e, m=m, wsl=wsl: e.dma_start(
                            out=wom[wsl][:].rearrange("p (n q) -> p n q", q=128),
                            in_=rb(WBF["ow"])[:, :, 128 * m:128 * m + 128]
                            .rearrange("n p q -> p n q")),
                            reads=[Bf("wbf_ow")], writes=[Bf(f"p4wo{wsl}")], dma=True)
                        Sd.op("sync", lambda e, m=m, wsl=wsl: e.dma_start(
                            out=xmc4[wsl][:], in_=XMIDD[m, :, 0:1024]),
                            reads=[Bf("xmid_done")], writes=[Bf(f"p4xm{wsl}")], dma=True)
                        for c in range(16):
                            for qh in range(2):
                                Sd.op("pe", lambda e, c=c, qh=qh, wsl=wsl:
                                      e.matmul(ps_ow[:, 512 * qh:512 * qh + 512],
                                               wom[wsl][:, 128 * c:128 * c + 128],
                                               y2[:, 1024 * c + 512 * qh:1024 * c + 512 * qh + 512],
                                               start=(c == 0), stop=(c == 15)),
                                      reads=[Bf(f"p4wo{wsl}"), Bf("p4y2")], writes=[Bf("p4ow")])
                        Sd.op("dve", lambda e, wsl=wsl: e.tensor_tensor(
                            t2c[:], xmc4[wsl][:], ps_sf[:], ALU.mult),
                            reads=[Bf(f"p4xm{wsl}"), Bf("p4psf")], writes=[Bf("p4t2c")])
                        Sd.op("dve", lambda e: e.tensor_tensor(
                            t2c[:], t2c[:], ps_asf[:], ALU.add),
                            reads=[Bf("p4pasf")], writes=[Bf("p4t2c")])
                        Sd.op("dve", lambda e: e.tensor_tensor(
                            hcc[:], t2c[:], ps_ow[:], ALU.add),
                            reads=[Bf("p4t2c"), Bf("p4ow")], writes=[Bf("p4hc")])
                        Sd.op("pool", lambda e, wsl=wsl: e.tensor_add(
                            scc[:], hcc[:], xmc4[wsl][:]),
                            reads=[Bf("p4hc"), Bf(f"p4xm{wsl}")], writes=[Bf("p4scc")])
                        Sd.op("act", lambda e, m=m: e.dma_start(out=HDD[m], in_=hcc[:]),
                              reads=[Bf("p4hc")], writes=[Bf("h_done")], dma=True)
                        Sd.op("act", lambda e, m=m: e.dma_start(out=SDD[m], in_=scc[:]),
                              reads=[Bf("p4scc")], writes=[Bf("s_done")], dma=True)
                    Sd.barrier()

        if PHASES_MAX >= 3:
            phase34()

        def phase5():
          with contextlib.ExitStack() as p5:
            rms2 = alloc("p5rms2", [128, 16 * 1024], BF16, p5)
            hcs = [alloc(f"p5h{i}", [128, 1024], F32, p5) for i in range(2)]
            hsq = alloc("p5hsq", [128, 1024], F32, p5)
            r5a = alloc("p5ra", [1, NOWN], F32, p5)
            r5b = alloc("p5rb", [1, NOWN], F32, p5)
            ps_g = psum("p5g", [128, 1024], F32, p5)
            ps_u = psum("p5u", [128, 1024], F32, p5)
            ps_w = psum("p5w", [128, 1024], F32, p5)

            for c in range(16):
                sl = c % 2
                Sd.op("sync", lambda e, c=c, sl=sl: e.dma_start(
                    out=hcs[sl][:], in_=HDD[c]),
                    reads=[Bf("h_done")], writes=[Bf(f"p5h{sl}")], dma=True)
                Sd.op("pool", lambda e, sl=sl: e.tensor_mul(hsq[:], hcs[sl][:], hcs[sl][:]),
                      reads=[Bf(f"p5h{sl}")], writes=[Bf("p5hsq")])
                for qh in range(2):
                    pst = [sb["p0m"], sb["p0e"]][qh]
                    Sd.op("pe", lambda e, c=c, qh=qh, pst=pst: e.matmul(
                        pst[:], sb["c_ones128f"][:],
                        hsq[:, 512 * qh:512 * qh + 512],
                        start=(c == 0), stop=(c == 15)),
                        reads=[Bf("p5hsq"), Bf("c_ones128f")], writes=[Bf("p0m"), Bf("p0e")])
            Sd.op("act", lambda e: e.copy(r5a[:, 0:512], sb["p0m"][:]),
                  reads=[Bf("p0m")], writes=[Bf("p5ra")])
            Sd.op("act", lambda e: e.copy(r5a[:, 512:1024], sb["p0e"][:]),
                  reads=[Bf("p0e")], writes=[Bf("p5ra")])
            Sd.op("dve", lambda e: e.tensor_scalar_mul(r5a[:], r5a[:], 1.0 / E),
                  reads=[Bf("p5ra")], writes=[Bf("p5ra")])
            Sd.op("act", lambda e: e.activation(r5b[:], r5a[:], AF.Sqrt, bias=sb["c_eps"][:]),
                  reads=[Bf("p5ra"), Bf("c_eps")], writes=[Bf("p5rb")])
            Sd.op("dve", lambda e: e.reciprocal(r5b[:], r5b[:]),
                  reads=[Bf("p5rb")], writes=[Bf("p5rb")])
            for qh in range(2):
                Sd.op("pe", lambda e, qh=qh: e.matmul(
                    ps_w[:, 512 * qh:512 * qh + 512], sb["c_ones1f"][:],
                    r5b[:, 512 * qh:512 * qh + 512], start=True, stop=True),
                    reads=[Bf("p5rb"), Bf("c_ones1f")], writes=[Bf("p5w")])
            for c in range(16):
                sl = c % 2
                Sd.op("sync", lambda e, c=c, sl=sl: e.dma_start(
                    out=hcs[sl][:], in_=HDD[c]),
                    reads=[Bf("h_done")], writes=[Bf(f"p5h{sl}")], dma=True)
                Sd.op("dve", lambda e, sl=sl: e.tensor_tensor(
                    hcs[sl][:], hcs[sl][:], ps_w[:], ALU.mult),
                    reads=[Bf("p5w")], writes=[Bf(f"p5h{sl}")])
                Sd.op("dve", lambda e, c=c, sl=sl: e.tensor_scalar_mul(
                    rms2[:, 1024 * c:1024 * (c + 1)], hcs[sl][:],
                    sb["c_ln2p"][:, c:c + 1]),
                    reads=[Bf(f"p5h{sl}"), Bf("c_ln2p")], writes=[Bf("p5rms2")])
            Sd.barrier()

            routw = alloc("p5routw", [128, 16 * 7], BF16, p5)
            pump_until("routerw")
            for c in range(16):
                Sd.op("sync", lambda e, c=c: e.dma_start(
                    out=routw[:, 7 * c:7 * (c + 1)], in_=rb(WBF["routerw"])[c]),
                    reads=[Bf("wbf_routerw")], writes=[Bf("p5routw")], dma=True)
            ptm8 = alloc("p5ptm8", [128, 64], F32, p5)
            mx8 = alloc("p5mx8", [128, 64], F32, p5)
            idx8 = alloc("p5idx8", [128, 64], mybir.dt.uint32, p5)
            idxf = alloc("p5idxf", [128, 16], F32, p5)
            sel1 = alloc("p5sel1", [128, 8 * 7], F32, p5)
            sel2 = alloc("p5sel2", [128, 8 * 7], F32, p5)
            wden = alloc("p5wden", [128, 8], F32, p5)
            w1c = alloc("p5w1c", [128, 8], F32, p5)
            w2c = alloc("p5w2c", [128, 8], F32, p5)
            wtm = alloc("p5wtm", [128, 8 * 7], F32, p5)
            wpad = alloc("p5wpad", [128, 128], F32, p5)
            wrows = alloc("p5wrows", [8, 1024], F32, p5)
            wone = alloc("p5wone", [1, 1024], F32, p5)
            RBuf = Bf("p5ptmB")

            Sd.op("dve", lambda e: e.memset(ptm8[:], NEG), writes=[RBuf])
            Sd.op("dve", lambda e: e.memset(wpad[:], 0.0), writes=[Bf("p5wpad")])
            for t in range(8):
                for c in range(16):
                    Sd.op("pe", lambda e, t=t, c=c: e.matmul(
                        ps_u[:, 0:7], rms2[:, 1024 * c + 128 * t:1024 * c + 128 * t + 128],
                        routw[:, 7 * c:7 * (c + 1)], start=(c == 0), stop=False),
                        reads=[Bf("p5rms2"), Bf("p5routw")], writes=[Bf("p5u")])
                Sd.op("pe", lambda e: e.matmul(
                    ps_u[:, 0:7], sb["c_ones1"][:], sb["c_brout_bf"][:],
                    start=False, stop=True),
                    reads=[Bf("c_ones1"), Bf("c_brout_bf")], writes=[Bf("p5u")])
                Sd.op("act", lambda e, t=t: e.activation(
                    ptm8[:, 8 * t:8 * t + 7], ps_u[:, 0:7], AF.Sigmoid),
                    reads=[Bf("p5u")], writes=[RBuf])
            for t in range(8):
                Sd.op("dve", lambda e, t=t: e.max(
                    mx8[:, 8 * t:8 * (t + 1)], ptm8[:, 8 * t:8 * (t + 1)]),
                    reads=[RBuf], writes=[Bf("p5mx8")])
                Sd.op("dve", lambda e, t=t: e.max_index(
                    idx8[:, 8 * t:8 * (t + 1)], mx8[:, 8 * t:8 * (t + 1)],
                    ptm8[:, 8 * t:8 * (t + 1)]),
                    reads=[RBuf, Bf("p5mx8")], writes=[Bf("p5idx8")])
                Sd.op("dve", lambda e, t=t: e.tensor_copy(
                    idxf[:, 2 * t:2 * t + 2], idx8[:, 8 * t:8 * t + 2]),
                    reads=[Bf("p5idx8")], writes=[Bf("p5idxf")])
                Sd.op("dve", lambda e, t=t: e.tensor_single_scalar(
                    sel1[:, 7 * t:7 * t + 7], sb["c_iota7"][:],
                    idxf[:, 2 * t:2 * t + 1], ALU.is_equal),
                    reads=[Bf("p5idxf"), Bf("c_iota7")], writes=[Bf("p5sel1")])
                Sd.op("dve", lambda e, t=t: e.tensor_single_scalar(
                    sel2[:, 7 * t:7 * t + 7], sb["c_iota7"][:],
                    idxf[:, 2 * t + 1:2 * t + 2], ALU.is_equal),
                    reads=[Bf("p5idxf"), Bf("c_iota7")], writes=[Bf("p5sel2")])
                Sd.op("dve", lambda e, t=t: e.tensor_add(
                    wden[:, t:t + 1], mx8[:, 8 * t:8 * t + 1],
                    mx8[:, 8 * t + 1:8 * t + 2]),
                    reads=[Bf("p5mx8")], writes=[Bf("p5wden")])
            Sd.op("dve", lambda e: e.reciprocal(wden[:], wden[:]),
                  reads=[Bf("p5wden")], writes=[Bf("p5wden")])
            for t in range(8):
                Sd.op("dve", lambda e, t=t: e.tensor_mul(
                    w1c[:, t:t + 1], mx8[:, 8 * t:8 * t + 1], wden[:, t:t + 1]),
                    reads=[Bf("p5mx8"), Bf("p5wden")], writes=[Bf("p5w1c")])
                Sd.op("dve", lambda e, t=t: e.tensor_mul(
                    w2c[:, t:t + 1], mx8[:, 8 * t + 1:8 * t + 2], wden[:, t:t + 1]),
                    reads=[Bf("p5mx8"), Bf("p5wden")], writes=[Bf("p5w2c")])
            for t in range(8):
                Sd.op("dve", lambda e, t=t: e.tensor_scalar_mul(
                    wtm[:, 7 * t:7 * t + 7], sel1[:, 7 * t:7 * t + 7], w1c[:, t:t + 1]),
                    reads=[Bf("p5sel1"), Bf("p5w1c")], writes=[Bf("p5wtm")])
                Sd.op("dve", lambda e, t=t: e.scalar_tensor_tensor(
                    wtm[:, 7 * t:7 * t + 7], sel2[:, 7 * t:7 * t + 7], w2c[:, t:t + 1],
                    wtm[:, 7 * t:7 * t + 7], ALU.mult, ALU.add),
                    reads=[Bf("p5sel2"), Bf("p5w2c")], writes=[Bf("p5wtm")])
            for t in range(8):
                Sd.op("dve", lambda e, t=t: e.tensor_copy(
                    wpad[:, 0:7], wtm[:, 7 * t:7 * t + 7]),
                    reads=[Bf("p5wtm"), Bf("p5wpad")], writes=[Bf("p5wpad")])
                Sd.op("pe", lambda e: e.transpose(ps_g[:, 0:128], wpad[:], sb["c_idf"][:]),
                      reads=[Bf("p5wpad"), Bf("c_idf")], writes=[Bf("p5g")])
                Sd.op("act", lambda e, t=t: e.mul(
                    wrows[0:8, 128 * t:128 * t + 128], ps_g[0:8, 0:128], 0.125),
                    reads=[Bf("p5g")], writes=[Bf("p5wrows")])
            if TAPS:
                for ti, (tt, tb) in enumerate([(ptm8, RBuf), (mx8, Bf("p5mx8")),
                                               (sel1, Bf("p5sel1")), (sel2, Bf("p5sel2")),
                                               (idxf, Bf("p5idxf"))]):
                    Sd.op("act", lambda e, ti=ti, tt=tt: e.dma_start(
                        out=PTMD[:, 64 * ti:64 * ti + tt.shape[1]], in_=tt[:]),
                        reads=[tb], writes=[Bf("ptm_done")], dma=True)
            Sd.op("act", lambda e: e.dma_start(out=WROWD[:], in_=wrows[:]),
                  reads=[Bf("p5wrows")], writes=[Bf("wrow_done")], dma=True)

            hp = alloc("p5hp", [128, 32 * 1024], BF16, p5)
            silu_sb = alloc("p5silu", [128, 1024], F32, p5)
            wg = [alloc(f"p5wg{i}", [128, 16 * 128], BF16, p5) for i in range(2)]
            wu = [alloc(f"p5wu{i}", [128, 16 * 128], BF16, p5) for i in range(2)]
            wd = [alloc(f"p5wd{i}", [128, 32 * 128], BF16, p5) for i in range(2)]
            scs = [alloc(f"p5scs{i}", [128, 1024], F32, p5) for i in range(2)]
            oat = alloc("p5oat", [128, 1024], F32, p5)
            gav, uav, dav = rb(WBF["gate_all"]), rb(WBF["up_all"]), rb(WBF["down_all"])

            pump_until("up_all")
            for pa in range(2):
                for kk, k in enumerate(range(4 * pa, 4 * pa + 4)):
                    if k == 0:
                        wsrc, wsrcB = sb["c_shw"], Bf("c_shw")
                    else:
                        Sd.op("sync", lambda e, k=k: e.dma_start(
                            out=wone[:], in_=WROWD[k - 1:k, :]),
                            reads=[Bf("wrow_done")], writes=[Bf("p5wone")], dma=True)
                        wsrc, wsrcB = wone, Bf("p5wone")
                    for qh in range(2):
                        Sd.op("pe", lambda e, qh=qh, wsrc=wsrc: e.matmul(
                            ps_w[:, 512 * qh:512 * qh + 512], sb["c_ones1f"][:],
                            wsrc[:, 512 * qh:512 * qh + 512], start=True, stop=True),
                            reads=[wsrcB, Bf("c_ones1f")], writes=[Bf("p5w")])
                    for ic in range(8):
                        wsl = ic % 2
                        Sd.op("sync", lambda e, k=k, ic=ic, wsl=wsl: e.dma_start(
                            out=wg[wsl][:].rearrange("p (n q) -> p n q", q=128),
                            in_=gav[16 * k:16 * k + 16, :, 128 * ic:128 * ic + 128]
                            .rearrange("n p q -> p n q")),
                            reads=[Bf("wbf_gate_all")], writes=[Bf(f"p5wg{wsl}")], dma=True)
                        Sd.op("sync", lambda e, k=k, ic=ic, wsl=wsl: e.dma_start(
                            out=wu[wsl][:].rearrange("p (n q) -> p n q", q=128),
                            in_=uav[16 * k:16 * k + 16, :, 128 * ic:128 * ic + 128]
                            .rearrange("n p q -> p n q")),
                            reads=[Bf("wbf_up_all")], writes=[Bf(f"p5wu{wsl}")], dma=True)
                        for c in range(16):
                            for qh in range(2):
                                Sd.op("pe", lambda e, c=c, qh=qh, wsl=wsl: e.matmul(
                                    ps_g[:, 512 * qh:512 * qh + 512],
                                    wg[wsl][:, 128 * c:128 * c + 128],
                                    rms2[:, 1024 * c + 512 * qh:1024 * c + 512 * qh + 512],
                                    start=(c == 0), stop=(c == 15)),
                                    reads=[Bf(f"p5wg{wsl}"), Bf("p5rms2")], writes=[Bf("p5g")])
                                Sd.op("pe", lambda e, c=c, qh=qh, wsl=wsl: e.matmul(
                                    ps_u[:, 512 * qh:512 * qh + 512],
                                    wu[wsl][:, 128 * c:128 * c + 128],
                                    rms2[:, 1024 * c + 512 * qh:1024 * c + 512 * qh + 512],
                                    start=(c == 0), stop=(c == 15)),
                                    reads=[Bf(f"p5wu{wsl}"), Bf("p5rms2")], writes=[Bf("p5u")])
                        Sd.op("act", lambda e: e.activation(silu_sb[:], ps_g[:], AF.Silu),
                              reads=[Bf("p5g")], writes=[Bf("p5silu")])
                        Sd.op("dve", lambda e: e.tensor_tensor(
                            silu_sb[:], ps_u[:], silu_sb[:], ALU.mult),
                            reads=[Bf("p5u")], writes=[Bf("p5silu")])
                        Sd.op("dve", lambda e, kk=kk, ic=ic: e.tensor_tensor(
                            hp[:, 1024 * (8 * kk + ic):1024 * (8 * kk + ic + 1)],
                            silu_sb[:], ps_w[:], ALU.mult),
                            reads=[Bf("p5silu"), Bf("p5w")], writes=[Bf("p5hp")])
                        pump(3)
                pump_until("down_all")
                for dm in range(16):
                    wsl = dm % 2
                    Sd.op("sync", lambda e, pa=pa, dm=dm, wsl=wsl: e.dma_start(
                        out=wd[wsl][:].rearrange("p (n q) -> p n q", q=128),
                        in_=dav[32 * pa:32 * pa + 32, :, 128 * dm:128 * dm + 128]
                        .rearrange("n p q -> p n q")),
                        reads=[Bf("wbf_down_all")], writes=[Bf(f"p5wd{wsl}")], dma=True)
                    if pa == 0:
                        Sd.op("sync", lambda e, dm=dm, wsl=wsl: e.dma_start(
                            out=scs[wsl][:], in_=SDD[dm]),
                            reads=[Bf("s_done")], writes=[Bf(f"p5scs{wsl}")], dma=True)
                    else:
                        Sd.op("sync", lambda e, dm=dm, wsl=wsl: e.dma_start(
                            out=scs[wsl][:], in_=OAD[dm]),
                            reads=[Bf("oa_done")], writes=[Bf(f"p5scs{wsl}")], dma=True)
                    for kx in range(32):
                        for qh in range(2):
                            Sd.op("pe", lambda e, kx=kx, qh=qh, wsl=wsl: e.matmul(
                                ps_g[:, 512 * qh:512 * qh + 512],
                                wd[wsl][:, 128 * kx:128 * kx + 128],
                                hp[:, 1024 * kx + 512 * qh:1024 * kx + 512 * qh + 512],
                                start=(kx == 0), stop=(kx == 31)),
                                reads=[Bf(f"p5wd{wsl}"), Bf("p5hp")], writes=[Bf("p5g")])
                    Sd.op("dve", lambda e, wsl=wsl: e.scalar_tensor_tensor(
                        oat[:], ps_g[:], 1.0, scs[wsl][:], ALU.mult, ALU.add),
                        reads=[Bf("p5g"), Bf(f"p5scs{wsl}")], writes=[Bf("p5oat")])
                    if pa == 0:
                        Sd.op("act", lambda e, dm=dm: e.dma_start(
                            out=OAD[dm], in_=oat[:]),
                            reads=[Bf("p5oat")], writes=[Bf("oa_done")], dma=True)
                    else:
                        Sd.op("act", lambda e, dm=dm: e.dma_start(
                            out=OUT[dm], in_=oat[:]),
                            reads=[Bf("p5oat")], writes=[Bf("out_done")], dma=True)
            Sd.barrier()

        if PHASES_MAX >= 5:
            phase5()

        cum_c = {}
        for e in Sched.ENGS:
            vc, tc = {}, 0
            for i, rec in enumerate(Sd.ops[e]):
                if rec[3] is None and i in Sd.marked[e]:
                    tc += 1
                vc[i] = tc
            cum_c[e] = vc

        def emit(eng_name, eng):
            waited_c = {}
            waited_l = {}

            def do_wait(we, wi):
                dep_lane = Sd.ops[we][wi][3]
                if dep_lane is not None:
                    lane, use = dep_lane
                    v = 16 * (use + 1)
                    key = (we, lane)
                    if waited_l.get(key, 0) < v:
                        eng.wait_ge(sems_dl[we][lane], v)
                        waited_l[key] = v
                else:
                    v = cum_c[we][wi]
                    if v > 0 and waited_c.get(we, 0) < v:
                        eng.wait_ge(sems[we], v)
                        waited_c[we] = v

            for i, rec in enumerate(Sd.ops[eng_name]):
                thunk, waits, amt, lane_info = rec[0], rec[1], rec[2], rec[3]
                for key in sorted(waits, key=str):
                    do_wait(key[0], waits[key])
                if len(rec) > 4:
                    for (we, wi) in rec[4]:
                        do_wait(we, wi)
                if lane_info is not None:
                    lane, use = lane_info
                    if use > 0:
                        key = (eng_name, lane)
                        v = 16 * use
                        if waited_l.get(key, 0) < v:
                            eng.wait_ge(sems_dl[eng_name][lane], v)
                            waited_l[key] = v
                inst = eng.nop() if thunk is None else thunk(eng)
                if lane_info is not None:
                    inst.then_inc(sems_dl[eng_name][lane_info[0]], 16)
                elif i in Sd.marked[eng_name]:
                    inst.then_inc(sems[eng_name], 1)

        with nc.Block() as block:

            @block.tensor
            def _(eng):
                emit("pe", eng)

            @block.scalar
            def _(eng):
                emit("act", eng)

            @block.vector
            def _(eng):
                emit("dve", eng)

            @block.gpsimd
            def _(eng):
                emit("pool", eng)

            @block.sync
            def _(eng):
                emit("sync", eng)

    n_ops = sum(len(Sd.ops[e]) for e in Sched.ENGS)
    n_marks = sum(len(Sd.marked[e]) for e in Sched.ENGS)
    return nc, n_ops, n_marks


_PROG_CACHE = {}


def _get_program():
    if "nc" not in _PROG_CACHE:
        nc, n_ops, n_marks = build_program()
        _PROG_CACHE["nc"] = nc
    return _PROG_CACHE["nc"]


def kernel(**inputs):
    in_maps = _stage_inputs(inputs)
    nc = _get_program()
    res = run_bass_kernel_spmd(nc, in_maps, core_ids=list(range(8)))
    out = np.zeros((B, S, E), np.float32)
    for c in range(8):
        b, h = c // 2, c % 2
        o = np.asarray(res.results[c]["out"])
        out[b, 1024 * h:1024 * h + 1024, :] = o.reshape(E, NOWN).T
    return out


# version 12
# speedup vs baseline: 1.1145x; 1.0842x over previous
import contextlib
import numpy as np
import ml_dtypes

import concourse.bass as bass
import concourse.mybir as mybir
from concourse.bass_utils import run_bass_kernel_spmd

F32 = mybir.dt.float32
BF16 = mybir.dt.bfloat16
AF = mybir.ActivationFunctionType
ALU = mybir.AluOpType
AX = mybir.AxisListType
BF = ml_dtypes.bfloat16

B, S, E = 4, 2048, 2048
WINDOW, NMETA = 512, 16
HDIM = 128
NH = 16
LAT = 512
RD = 64
IEXP = 1024
NEXP = 8
EPS = 1e-5
SCALE = float(1.0 / np.sqrt(128.0))

RLOC = 2688
NBLK = 21
HALO0, OWN0, REST0 = 129, 640, 1664
NQ = 2048
NOWN = 1024
NEG = -1e30

TAPS = False
PHASES_MAX = 9


class Buf:
    __slots__ = ("ws", "r", "name")

    def __init__(self, name=""):
        self.ws = []
        self.r = []
        self.name = name


NLANE = 8


class Sched:
    ENGS = ("pe", "act", "dve", "pool", "sync")

    def __init__(self):
        self.ops = {e: [] for e in self.ENGS}
        self.marked = {e: set() for e in self.ENGS}
        self.dma_seq = {e: 0 for e in self.ENGS}
        self.lane_use = {e: [0] * NLANE for e in self.ENGS}
        self.lane_last = {e: [None] * NLANE for e in self.ENGS}
        self.last_compute = {e: None for e in self.ENGS}

    def op(self, eng, thunk, reads=(), writes=(), dma=False):
        idx = len(self.ops[eng])
        deps = []
        for b in reads:
            deps.extend(b.ws)
        for b in writes:
            if b.r:
                deps.extend(b.r)
            else:
                deps.extend(b.ws)
        waits = {}
        for (de, di) in deps:
            dep_lane = self.ops[de][di][3]
            if de == eng:
                dep_dma = dep_lane is not None
                if dep_dma and dma:
                    continue
                if (not dep_dma) and (not dma) and eng == "pe":
                    continue
            key = (de, "c") if dep_lane is None else (de, dep_lane[0])
            if waits.get(key, -1) < di:
                waits[key] = di
        for (de, _k), di in waits.items():
            if self.ops[de][di][3] is None:
                self.marked[de].add(di)
        for b in reads:
            b.r.append((eng, idx))
        for b in writes:
            if b.r:
                b.ws = [(eng, idx)]
                b.r = []
            else:
                b.ws.append((eng, idx))
        lane_info = None
        if dma:
            lane = self.dma_seq[eng] % NLANE
            self.dma_seq[eng] += 1
            use = self.lane_use[eng][lane]
            self.lane_use[eng][lane] += 1
            lane_info = (lane, use)
            self.lane_last[eng][lane] = idx
        else:
            self.last_compute[eng] = idx
        self.ops[eng].append([thunk, waits, 16 if dma else 1, lane_info])

    def barrier(self):
        targets = {}
        for oe in self.ENGS:
            t = []
            if self.last_compute[oe] is not None:
                t.append(self.last_compute[oe])
                self.marked[oe].add(self.last_compute[oe])
            for li in self.lane_last[oe]:
                if li is not None:
                    t.append(li)
            targets[oe] = t
        for e in self.ENGS:
            waits = {}
            ext = []
            for oe in self.ENGS:
                if oe == e:
                    continue
                for ti in targets[oe]:
                    ext.append((oe, ti))
            self.ops[e].append([None, waits, 1, None, ext])
            self.last_compute[e] = len(self.ops[e]) - 1

    def finalize(self, nc, sems, engmap):
        cum = {}
        for e in self.ENGS:
            vals, tot = {}, 0
            for i, (_t, _w, amt) in enumerate(self.ops[e]):
                if i in self.marked[e]:
                    tot += amt
                vals[i] = tot
            cum[e] = vals
        n_w = sum(len(w) for e in self.ENGS for (_t, w, _a) in self.ops[e])
        n_o = sum(len(self.ops[e]) for e in self.ENGS)

        def body(eng_name):
            eng = engmap[eng_name]
            waited = {}
            for i, (thunk, waits, amt) in enumerate(self.ops[eng_name]):
                for we in sorted(waits):
                    v = cum[we][waits[we]]
                    if waited.get(we, 0) < v:
                        eng.wait_ge(sems[we], v)
                        waited[we] = v
                if thunk is None:
                    inst = eng.nop()
                else:
                    inst = thunk(eng)
                if i in self.marked[eng_name]:
                    inst.then_inc(sems[eng_name], amt)
        return body, n_o, n_w


def _posmap(h):
    pos = np.full(RLOC, -1, np.int64)
    pos[0:16] = np.arange(16)
    if h == 1:
        pos[HALO0:HALO0 + 511] = np.arange(513, 1024)
    pos[OWN0:OWN0 + 1024] = np.arange(1024 * h, 1024 * h + 1024)
    pos[REST0:REST0 + 1024] = np.arange(1024 * (1 - h), 1024 * (1 - h) + 1024)
    return pos


def _rope_sparse(pos):
    inv = 10000.0 ** (-np.arange(0, HDIM, 2, dtype=np.float64) / HDIM)
    p = np.where(pos < 0, 0, pos).astype(np.float64)
    ang = p[None, :] * inv[:, None]
    cos = np.repeat(np.cos(ang), 2, axis=0)
    sin = np.repeat(np.sin(ang), 2, axis=0)
    return cos.astype(np.float32), sin.astype(np.float32)


def _rope_mla(pos):
    inv = 10000.0 ** (-np.arange(0, RD, 2, dtype=np.float64) / RD)
    p = np.where(pos < 0, 0, pos).astype(np.float64)
    ang = p[None, :] * inv[:, None]
    c = np.concatenate([np.cos(ang), np.cos(ang)], 0)
    s = np.concatenate([np.sin(ang), np.sin(ang)], 0)
    return (np.concatenate([c, c], 0).astype(np.float32),
            np.concatenate([s, s], 0).astype(np.float32))


def _host_constants(h):
    pos = _posmap(h)
    valid = pos >= 0
    qpos = pos[OWN0:]
    d = {}
    d["cosq_sp"], d["sinq_sp"] = _rope_sparse(qpos)
    d["cosk_sp"], d["sink_sp"] = _rope_sparse(pos)
    d["cos_mla"], d["sin_mla"] = _rope_mla(qpos)

    Rsp = np.zeros((HDIM, HDIM), np.float32)
    for i in range(HDIM // 2):
        Rsp[2 * i, 2 * i + 1] = -1.0
        Rsp[2 * i + 1, 2 * i] = 1.0
    d["RspT"] = Rsp.T.astype(BF).copy()
    R64 = np.zeros((RD, RD), np.float32)
    for i in range(RD // 2):
        R64[i, i + RD // 2] = -1.0
        R64[i + RD // 2, i] = 1.0
    Rm = np.zeros((HDIM, HDIM), np.float32)
    Rm[:RD, :RD] = R64
    Rm[RD:, RD:] = R64
    d["RmlaT"] = Rm.T.astype(BF).copy()

    mask_win = np.zeros((4, 9, 128, 512), BF)
    for g in range(4):
        qp = qpos[512 * g: 512 * g + 512][None, :].astype(np.int64)
        kp0 = pos[0:128][:, None]
        mask_win[g, 0, 0:16] = ((kp0[0:16] <= qp) & (qp - kp0[0:16] > 511)).astype(BF)
        for r in range(1, 9):
            kb = 4 * g + r
            kp = pos[128 * kb:128 * kb + 128][:, None]
            vl = valid[128 * kb:128 * kb + 128][:, None]
            mask_win[g, r] = (vl & (kp <= qp) & (qp - kp <= 511)).astype(BF)
    d["mask_win"] = mask_win

    bias_mla = np.zeros((128, 16), np.float32)
    for j in range(16):
        kp = qpos[128 * j: 128 * j + 128]
        le = kp[:, None] <= qpos[None, 0:1024]
        if le.all():
            bias_mla[:, j] = 0.0
        elif not le.any():
            bias_mla[:, j] = NEG
    d["bias_mla"] = bias_mla
    tri = np.zeros((8, 128, NOWN), BF)
    for j in range(8):
        kk = np.arange(128)[:, None] + 128 * j
        qq = np.arange(NOWN)[None, :]
        tri[j] = (kk <= qq).astype(BF)
    d["tri_mla"] = tri

    d["ident_f32"] = np.eye(128, dtype=np.float32)
    d["iota7"] = np.tile(np.arange(7, dtype=np.float32), (128, 1))
    return d


def _stage_inputs(inputs):
    x = np.asarray(inputs["L0_attn_x"])
    w = {k: np.asarray(v) for k, v in inputs.items()}
    shared = {
        "wq": w["L0_attn_Wq"], "wkv": w["L0_attn_Wkv"], "wo": w["L0_attn_Wo"],
        "kvd": w["L0_ffn_kv_d_w"], "qd": w["L0_ffn_q_d_w"],
        "ku": w["L0_ffn_k_u_w"], "vu": w["L0_ffn_v_u_w"], "qu": w["L0_ffn_q_u_w"],
        "ropek": w["L0_ffn_rope_k_w"], "ropeq": w["L0_ffn_rope_q_w"],
        "ow": w["L0_ffn_o_w"],
        "routerw": np.ascontiguousarray(w["L0_ffn_router_w"]),
        "routerb": np.ascontiguousarray(w["L0_ffn_router_b"].reshape(1, 7)),
        "gate_all": np.ascontiguousarray(np.concatenate(
            [w["L0_ffn_sh_gate"][None], w["L0_ffn_r_gate"]], 0).reshape(NEXP * E, IEXP)),
        "up_all": np.ascontiguousarray(np.concatenate(
            [w["L0_ffn_sh_up"][None], w["L0_ffn_r_up"]], 0).reshape(NEXP * E, IEXP)),
        "down_all": np.ascontiguousarray(np.concatenate(
            [w["L0_ffn_sh_down"][None], w["L0_ffn_r_down"]], 0).reshape(NEXP * IEXP, E)),
    }

    def pc(v):
        return np.ascontiguousarray(v.reshape(16, 128).T)
    shared["bq_pc"] = pc(w["L0_attn_bq"])
    shared["bo_pc"] = pc(w["L0_attn_bo"])
    shared["bkvk_pc"] = np.ascontiguousarray(w["L0_attn_bkv"][0:512].reshape(4, 128).T)
    shared["bkvv_row"] = np.ascontiguousarray(w["L0_attn_bkv"][512:1024].reshape(1, 512))
    shared["ln1w_pc"] = pc(w["L0_ffn_ln1_w"])
    shared["ln2w_pc"] = pc(w["L0_ffn_ln2_w"])
    shared["ln1w_row"] = np.ascontiguousarray(w["L0_ffn_ln1_w"].reshape(1, 2048))

    cvar = [_host_constants(0), _host_constants(1)]
    in_maps = []
    for c in range(8):
        b, h = c // 2, c % 2
        xb = x[b]
        xloc = np.zeros((RLOC, E), np.float32)
        xloc[0:16] = xb[0:16]
        if h == 1:
            xloc[HALO0:HALO0 + 511] = xb[513:1024]
        xloc[OWN0:OWN0 + 1024] = xb[1024 * h: 1024 * h + 1024]
        xloc[REST0:REST0 + 1024] = xb[1024 * (1 - h): 1024 * (1 - h) + 1024]
        m = dict(shared)
        m["xloc"] = np.ascontiguousarray(xloc.T.reshape(16, 128, RLOC))
        m.update(cvar[h])
        in_maps.append(m)
    return in_maps


def build_program():
    nc = bass.Bass("TRN2", target_bir_lowering=False, debug=False)
    Sd = Sched()
    bufs = {}

    def Bf(name):
        if name not in bufs:
            bufs[name] = Buf(name)
        return bufs[name]

    def din(name, shape, dt=F32):
        return nc.dram_tensor(name, shape, dt, kind="ExternalInput").ap()

    X = din("xloc", [16, 128, RLOC])
    WRAW = {nm: din(nm, shape) for nm, shape in [
        ("wq", [E, E]), ("wkv", [E, 1024]), ("wo", [E, E]),
        ("kvd", [E, LAT]), ("qd", [E, LAT]), ("ku", [LAT, 1024]),
        ("vu", [LAT, E]), ("qu", [LAT, 1024]), ("ropek", [E, 1024]),
        ("ropeq", [LAT, 1024]), ("ow", [E, E]), ("routerw", [E, 7]),
        ("gate_all", [NEXP * E, IEXP]), ("up_all", [NEXP * E, IEXP]),
        ("down_all", [NEXP * IEXP, E]),
    ]}
    BQ = din("bq_pc", [128, 16])
    BO = din("bo_pc", [128, 16])
    BKVK = din("bkvk_pc", [128, 4])
    BKVV = din("bkvv_row", [1, 512])
    LN1P = din("ln1w_pc", [128, 16])
    LN2P = din("ln2w_pc", [128, 16])
    LN1R = din("ln1w_row", [1, 2048])
    BROUT = din("routerb", [1, 7])
    COSQ = din("cosq_sp", [128, NQ])
    SINQ = din("sinq_sp", [128, NQ])
    COSK = din("cosk_sp", [128, RLOC])
    SINK = din("sink_sp", [128, RLOC])
    COSM = din("cos_mla", [128, NQ])
    SINM = din("sin_mla", [128, NQ])
    RSPT = din("RspT", [128, 128], BF16)
    RMLAT = din("RmlaT", [128, 128], BF16)
    MWIN = din("mask_win", [4, 9, 128, 512], BF16)
    BMLA = din("bias_mla", [128, 16])
    TRIM = din("tri_mla", [8, 128, NOWN], BF16)
    IDF = din("ident_f32", [128, 128])
    IOTA7 = din("iota7", [128, 7])

    OUT = nc.dram_tensor("out", [16, 128, NOWN], F32, kind="ExternalOutput").ap()

    ikind = "ExternalOutput" if TAPS else "Internal"

    def dint(name, shape, dt=BF16, tap=False):
        return nc.dram_tensor(name, shape, dt,
                              kind=(ikind if tap else "Internal")).ap()

    WBF = {}
    for nm, shape in [
        ("wq", [E, E]), ("wkv", [E, 1024]), ("wo", [E, E]), ("kvd", [E, LAT]),
        ("qd", [E, LAT]), ("ku", [LAT, 1024]), ("vu", [LAT, E]),
        ("qu", [LAT, 1024]), ("ropek", [E, 1024]), ("ropeq", [LAT, 1024]),
        ("ow", [E, E]), ("routerw", [E, 7]),
        ("gate_all", [NEXP * E, IEXP]), ("up_all", [NEXP * E, IEXP]),
        ("down_all", [NEXP * IEXP, E]),
    ]:
        WBF[nm] = dint("wbf_" + nm, shape)

    QSPD = dint("qsp_d", [16, 128, NQ], tap=True)
    KSPD = dint("ksp_d", [4, 128, RLOC], tap=True)
    VSPD = dint("vsp_d", [NBLK, 128, 512], tap=True)
    XMIDD = dint("xmid_d", [16, 128, NQ], F32, tap=True)
    KPD = dint("kp_d", [8, 128, NQ], tap=True)
    KRD = dint("kr_d", [8, 128, NQ], tap=True)
    VMLAD = dint("vmla_d", [16, 128, E], tap=True)
    HDD = dint("h_d", [16, 128, NOWN], F32, tap=True)
    SDD = dint("s_d", [16, 128, NOWN], F32, tap=True)
    OAD = dint("oa_d", [16, 128, NOWN], F32, tap=True)
    Y2D = dint("y2_d", [4, 128, 16 * 512], tap=True)
    PTMD = dint("ptm_d", [128, 5 * 64], F32, tap=True)
    WROWD = dint("wrow_d", [8, 1024], F32, tap=True)
    ROWD = dint("row_d", [8, RLOC], F32, tap=True)

    def rb(ap):
        return ap.rearrange("(n p) c -> n p c", p=128)

    es = contextlib.ExitStack()
    with es:
        sems = {e: es.enter_context(nc.semaphore("sem_" + e)) for e in Sched.ENGS}
        sems_dl = {e: [es.enter_context(nc.semaphore(f"semd_{e}{j}"))
                       for j in range(NLANE)]
                   for e in ("sync", "act")}
        sb = {}

        def alloc(name, shape, dt=F32, stack=None):
            t = (stack or es).enter_context(nc.sbuf_tensor(name, shape, dt))
            sb[name] = t
            return t

        def psum(name, shape, dt=F32, stack=None):
            t = (stack or es).enter_context(nc.psum_tensor(name, shape, dt))
            sb[name] = t
            return t

        def load_packed(name, nparts, n, q, dt, src3, stack=None):
            t = alloc(name, [nparts, n * q], dt, stack)
            Sd.op("sync", lambda e, t=t, src3=src3, n=n, q=q: e.dma_start(
                out=t[:].rearrange("p (n q) -> p n q", q=q), in_=src3),
                writes=[Bf(name)], dma=True)
            return t

        def load_const(name, shape, dt, src, stack=None):
            t = alloc(name, shape, dt, stack)
            Sd.op("sync", lambda e, t=t, src=src: e.dma_start(out=t[:], in_=src),
                  writes=[Bf(name)], dma=True)
            return t

        load_const("c_bq", [128, 16], F32, BQ)
        Sd.op("dve", lambda e: e.tensor_scalar_mul(sb["c_bq"][:], sb["c_bq"][:], SCALE),
              reads=[Bf("c_bq")], writes=[Bf("c_bq")])
        load_const("c_bo", [128, 16], F32, BO)
        load_const("c_bkvk", [128, 4], F32, BKVK)
        load_const("c_ln1p", [128, 16], F32, LN1P)
        load_const("c_ln2p", [128, 16], F32, LN2P)
        load_const("c_bmla", [128, 16], F32, BMLA)
        load_const("c_rspt", [128, 128], BF16, RSPT)
        load_const("c_rmlat", [128, 128], BF16, RMLAT)
        load_const("c_idf", [128, 128], F32, IDF)
        load_const("c_iota7", [128, 7], F32, IOTA7)
        load_const("c_bkvv", [1, 512], F32, BKVV)
        load_const("c_brout", [1, 7], F32, BROUT)
        for nm, src in [("c_bkvv_bf", "c_bkvv"), ("c_brout_bf", "c_brout")]:
            t = alloc(nm, list(sb[src].shape), BF16)
            Sd.op("dve", lambda e, t=t, s=sb[src]: e.tensor_copy(t[:], s[:]),
                  reads=[Bf(src)], writes=[Bf(nm)])
        for nm, val, shape, dt in [("c_ones1", 1.0, [1, 128], BF16),
                                   ("c_ones1f", 1.0, [1, 128], F32),
                                   ("c_ones128", 1.0, [128, 1], BF16),
                                   ("c_ones128f", 1.0, [128, 1], F32),
                                   ("c_eps", EPS, [1, 1], F32),
                                   ("c_shw", 0.125, [1, NOWN], F32)]:
            t = alloc(nm, shape, dt)
            Sd.op("dve", lambda e, t=t, v=val: e.memset(t[:], v), writes=[Bf(nm)])

        def row_coefs(rm, rmB, re_, reB, rc, rcB, ra, raB, t1, t1B, t2, t2B, width):
            Sd.op("dve", lambda e: e.tensor_scalar_mul(rm[:, 0:width], rm[:, 0:width], 1.0 / E),
                  reads=[rmB], writes=[rmB])
            Sd.op("dve", lambda e: e.tensor_scalar_mul(re_[:, 0:width], re_[:, 0:width], 1.0 / E),
                  reads=[reB], writes=[reB])
            Sd.op("dve", lambda e: e.tensor_mul(t1[:, 0:width], rm[:, 0:width], rm[:, 0:width]),
                  reads=[rmB], writes=[t1B])
            Sd.op("dve", lambda e: e.tensor_sub(t1[:, 0:width], re_[:, 0:width], t1[:, 0:width]),
                  reads=[reB], writes=[t1B])
            Sd.op("act", lambda e: e.activation(t2[:, 0:width], t1[:, 0:width], AF.Sqrt, bias=sb["c_eps"][:]),
                  reads=[t1B, Bf("c_eps")], writes=[t2B])
            Sd.op("dve", lambda e: e.reciprocal(rc[:, 0:width], t2[:, 0:width]),
                  reads=[t2B], writes=[rcB])
            Sd.op("dve", lambda e: e.tensor_mul(ra[:, 0:width], rm[:, 0:width], rc[:, 0:width]),
                  reads=[rmB, rcB], writes=[raB])
            Sd.op("dve", lambda e: e.tensor_scalar_mul(ra[:, 0:width], ra[:, 0:width], -1.0),
                  reads=[raB], writes=[raB])

        def rope4(rawf_ap, rawfbuf, rawb_ap, rawbbuf, cos_ap, sin_ap,
                  out_ap, outbuf, w_, rot_lhs, ps_rot, psrotbuf,
                  t1, t1buf, t2, t2buf, trigbuf, trigbuf2=None):
            Sd.op("dve", lambda e: e.tensor_copy(rawb_ap, rawf_ap),
                  reads=[rawfbuf], writes=[rawbbuf])
            Sd.op("pe", lambda e: e.matmul(ps_rot[:, 0:w_], sb[rot_lhs][:],
                                           rawb_ap, start=True, stop=True),
                  reads=[rawbbuf, Bf(rot_lhs)], writes=[psrotbuf])
            Sd.op("dve", lambda e: e.tensor_mul(t1[:, 0:w_], rawf_ap, cos_ap),
                  reads=[rawfbuf, trigbuf], writes=[t1buf])
            Sd.op("dve", lambda e: e.scalar_tensor_tensor(
                t2[:, 0:w_], ps_rot[:, 0:w_], 1.0, sin_ap, ALU.mult, ALU.mult),
                reads=[psrotbuf, trigbuf2 or trigbuf], writes=[t2buf])
            Sd.op("pool", lambda e: e.tensor_add(out_ap, t1[:, 0:w_], t2[:, 0:w_]),
                  reads=[t1buf, t2buf], writes=[outbuf])

        p0s = contextlib.ExitStack()

        wfp = [alloc(f"p0wf{i}", [128, 1024], F32) for i in range(2)]
        wbp = [alloc(f"p0wb{i}", [128, 1024], BF16) for i in range(2)]
        pump_tiles = []
        pump_end = {}
        for nm in ["wkv", "wq", "wo", "kvd", "ku", "vu", "ropek", "qd",
                   "qu", "ropeq", "ow", "routerw", "gate_all", "up_all",
                   "down_all"]:
            wsrc = WRAW[nm]
            rows, cols = wsrc.tensor.shape
            scaled = nm in ("qu", "ropeq", "wq")
            sv, dv = rb(wsrc), rb(WBF[nm])
            for rblk in range(rows // 128):
                for c0_ in range(0, cols, 1024):
                    cw = min(1024, cols - c0_)
                    pump_tiles.append((nm, sv, dv, rblk, c0_, cw, scaled))
            pump_end[nm] = len(pump_tiles)
        pump_pos = [0]
        pump_pend = [None]

        def _pump_one():
            ti = pump_pos[0]
            nm, sv, dv, rblk, c0_, cw, scaled = pump_tiles[ti]
            pump_pos[0] = ti + 1
            sl = ti % 2
            Sd.op("sync", lambda e, sl=sl, sv=sv, rblk=rblk, c0_=c0_, cw=cw:
                  e.dma_start(out=wfp[sl][:, 0:cw],
                              in_=sv[rblk][:, c0_:c0_ + cw]),
                  writes=[Bf(f"p0wf{sl}")], dma=True)
            if scaled:
                Sd.op("dve", lambda e, sl=sl, cw=cw:
                      e.tensor_scalar_mul(wbp[sl][:, 0:cw], wfp[sl][:, 0:cw], SCALE),
                      reads=[Bf(f"p0wf{sl}")], writes=[Bf(f"p0wb{sl}")])
            elif ti % 2 == 0:
                Sd.op("dve", lambda e, sl=sl, cw=cw:
                      e.tensor_copy(wbp[sl][:, 0:cw], wfp[sl][:, 0:cw]),
                      reads=[Bf(f"p0wf{sl}")], writes=[Bf(f"p0wb{sl}")])
            else:
                Sd.op("act", lambda e, sl=sl, cw=cw:
                      e.copy(wbp[sl][:, 0:cw], wfp[sl][:, 0:cw]),
                      reads=[Bf(f"p0wf{sl}")], writes=[Bf(f"p0wb{sl}")])
            if pump_pend[0] is not None:
                pump_pend[0]()
            def _store(sl=sl, nm=nm, dv=dv, rblk=rblk, c0_=c0_, cw=cw):
                Sd.op("act", lambda e, sl=sl, dv=dv, rblk=rblk, c0_=c0_, cw=cw:
                      e.dma_start(out=dv[rblk][:, c0_:c0_ + cw],
                                  in_=wbp[sl][:, 0:cw]),
                      reads=[Bf(f"p0wb{sl}")], writes=[Bf("wbf_" + nm)], dma=True)
            pump_pend[0] = _store

        def pump(k):
            for _ in range(k):
                if pump_pos[0] < len(pump_tiles):
                    _pump_one()

        def pump_until(nm):
            while pump_pos[0] < pump_end[nm]:
                _pump_one()
            if pump_pend[0] is not None:
                pump_pend[0]()
                pump_pend[0] = None

        def phase0():
            xs = [alloc(f"p0x{i}", [128, 512], F32, p0s) for i in range(2)]
            xq = [alloc(f"p0q{i}", [128, 512], F32, p0s) for i in range(2)]
            r_m = alloc("p0rm", [1, RLOC], F32, p0s)
            r_e = alloc("p0re", [1, RLOC], F32, p0s)
            r_t1 = alloc("p0t1", [1, RLOC], F32, p0s)
            r_t2 = alloc("p0t2", [1, RLOC], F32, p0s)
            ps_m = psum("p0m", [1, 512], F32)
            ps_e = psum("p0e", [1, 512], F32)
            st = 0
            for col0 in range(0, RLOC, 512):
                w_ = min(512, RLOC - col0)
                for c in range(16):
                    sl = st % 2
                    st += 1
                    Sd.op("sync", lambda e, sl=sl, c=c, col0=col0, w_=w_:
                          e.dma_start(out=xs[sl][:, 0:w_], in_=X[c, :, col0:col0 + w_]),
                          writes=[Bf(f"p0xx{sl}")], dma=True)
                    Sd.op("pool", lambda e, sl=sl, w_=w_:
                          e.tensor_mul(xq[sl][:, 0:w_], xs[sl][:, 0:w_], xs[sl][:, 0:w_]),
                          reads=[Bf(f"p0xx{sl}")], writes=[Bf(f"p0qq{sl}")])
                    Sd.op("pe", lambda e, sl=sl, c=c, w_=w_:
                          e.matmul(ps_m[:, 0:w_], sb["c_ones128f"][:], xs[sl][:, 0:w_],
                                   start=(c == 0), stop=(c == 15)),
                          reads=[Bf(f"p0xx{sl}"), Bf("c_ones128f")], writes=[Bf("p0m")])
                    Sd.op("pe", lambda e, sl=sl, c=c, w_=w_:
                          e.matmul(ps_e[:, 0:w_], sb["c_ones128f"][:], xq[sl][:, 0:w_],
                                   start=(c == 0), stop=(c == 15)),
                          reads=[Bf(f"p0qq{sl}"), Bf("c_ones128f")], writes=[Bf("p0e")])
                    pump(1)
                Sd.op("act", lambda e, col0=col0, w_=w_:
                      e.copy(r_m[:, col0:col0 + w_], ps_m[:, 0:w_]),
                      reads=[Bf("p0m")], writes=[Bf("p0rm")])
                Sd.op("act", lambda e, col0=col0, w_=w_:
                      e.copy(r_e[:, col0:col0 + w_], ps_e[:, 0:w_]),
                      reads=[Bf("p0e")], writes=[Bf("p0re")])
            row_coefs(r_m, Bf("p0rm"), r_e, Bf("p0re"),
                      r_t2, Bf("p0t2"), r_e, Bf("p0re"),
                      r_t1, Bf("p0t1"), r_t2, Bf("p0t2"), RLOC)
            Sd.op("act", lambda e: e.dma_start(out=ROWD[0:1, :], in_=r_t2[:]),
                  reads=[Bf("p0t2")], writes=[Bf("row_c0")], dma=True)
            Sd.op("act", lambda e: e.dma_start(out=ROWD[1:2, :], in_=r_e[:]),
                  reads=[Bf("p0re")], writes=[Bf("row_a0")], dma=True)

        phase0()

        def phase1():
          with contextlib.ExitStack() as p1:
            tck = alloc("p1tck", [128, 640], F32, p1)
            tsk = alloc("p1tsk", [128, 640], F32, p1)
            tcq = alloc("p1tcq", [128, 512], F32, p1)
            tsq = alloc("p1tsq", [128, 512], F32, p1)
            rc0 = alloc("p1c0", [1, RLOC], F32, p1)
            ra0 = alloc("p1a0", [1, RLOC], F32, p1)
            Sd.op("sync", lambda e: e.dma_start(out=rc0[:], in_=ROWD[0:1, :]),
                  reads=[Bf("row_c0")], writes=[Bf("p1c0")], dma=True)
            Sd.op("sync", lambda e: e.dma_start(out=ra0[:], in_=ROWD[1:2, :]),
                  reads=[Bf("row_a0")], writes=[Bf("p1a0")], dma=True)

            wkvl = alloc("p1wkv", [128, 16 * 1024], BF16, p1)
            pump_until("wkv")
            for c in range(16):
                Sd.op("sync", lambda e, c=c: e.dma_start(
                    out=wkvl[:, 1024 * c:1024 * (c + 1)], in_=rb(WBF["wkv"])[c]),
                    reads=[Bf("wbf_wkv")], writes=[Bf("p1wkv")], dma=True)

            ln0 = [alloc("p1ln0", [128, 16 * 640], BF16, p1),
                   alloc("p1ln1", [128, 16 * 512], BF16, p1)]
            xs = [alloc(f"p1x{i}", [128, 640], F32, p1) for i in range(3)]
            wqt = [alloc(f"p1wqt{i}", [128, 256], BF16, p1) for i in range(2)]
            ps_bc = psum("p1bc", [128, 1024], F32, p1)
            ps_q = [psum(f"p1q{i}", [128, 512], F32, p1) for i in range(2)]
            ps_a = psum("p1a_", [128, 512], F32, p1)
            ps_b = psum("p1b_", [128, 512], F32, p1)
            scf = alloc("p1scf", [128, 512], F32, p1)
            rawf = [alloc(f"p1rawf{i}", [128, 512], F32, p1) for i in range(2)]
            rawb = [alloc(f"p1rawb{i}", [128, 512], BF16, p1) for i in range(2)]
            t1f = [alloc(f"p1t1f{i}", [128, 512], F32, p1) for i in range(2)]
            t2f = [alloc(f"p1t2f{i}", [128, 512], F32, p1) for i in range(2)]
            stg = [alloc(f"p1stg{i}", [128, 512], BF16, p1) for i in range(2)]

            for G in range(5):
                blocks = list(range(0, 5)) if G == 0 else list(range(4 * G + 1, 4 * G + 5))
                col0 = 128 * blocks[0]
                w_ = 128 * len(blocks)
                lsl = G % 2
                lstr = 640 if lsl == 0 else 512
                Sd.op("sync", lambda e, col0=col0, w_=w_: e.dma_start(
                    out=tck[:, 0:w_], in_=COSK[:, col0:col0 + w_]),
                    writes=[Bf("ctck")], dma=True)
                Sd.op("sync", lambda e, col0=col0, w_=w_: e.dma_start(
                    out=tsk[:, 0:w_], in_=SINK[:, col0:col0 + w_]),
                    writes=[Bf("ctsk")], dma=True)
                if G >= 1:
                    g_ = G - 1
                    Sd.op("sync", lambda e, g_=g_: e.dma_start(
                        out=tcq[:], in_=COSQ[:, 512 * g_:512 * g_ + 512]),
                        writes=[Bf("ctcq")], dma=True)
                    Sd.op("sync", lambda e, g_=g_: e.dma_start(
                        out=tsq[:], in_=SINQ[:, 512 * g_:512 * g_ + 512]),
                        writes=[Bf("ctsq")], dma=True)
                subs = [(0, 512), (512, 128)] if G == 0 else [(0, 512)]
                for (s0, sw) in subs:
                    Sd.op("pe", lambda e, col0=col0, s0=s0, sw=sw:
                          e.matmul(ps_bc[:, 0:sw], sb["c_ones1f"][:],
                                   rc0[:, col0 + s0:col0 + s0 + sw],
                                   start=True, stop=True),
                          reads=[Bf("p1c0"), Bf("c_ones1f")], writes=[Bf("p1bc")])
                    Sd.op("pe", lambda e, col0=col0, s0=s0, sw=sw:
                          e.matmul(ps_bc[:, 512:512 + sw], sb["c_ones1f"][:],
                                   ra0[:, col0 + s0:col0 + s0 + sw],
                                   start=True, stop=True),
                          reads=[Bf("p1a0"), Bf("c_ones1f")], writes=[Bf("p1bc")])
                    for c in range(16):
                        xsl = (G * 16 + c) % 3
                        Sd.op("sync", lambda e, c=c, xsl=xsl, col0=col0, s0=s0, sw=sw:
                              e.dma_start(out=xs[xsl][:, 0:sw],
                                          in_=X[c, :, col0 + s0:col0 + s0 + sw]),
                              writes=[Bf(f"p1x{xsl}")], dma=True)
                        Sd.op("dve", lambda e, xsl=xsl, sw=sw:
                              e.tensor_tensor(scf[:, 0:sw], xs[xsl][:, 0:sw],
                                              ps_bc[:, 0:sw], ALU.mult),
                              reads=[Bf(f"p1x{xsl}"), Bf("p1bc")], writes=[Bf("p1scf")])
                        Sd.op("dve", lambda e, lsl=lsl, lstr=lstr, c=c, s0=s0, sw=sw:
                              e.tensor_tensor(ln0[lsl][:, lstr * c + s0:lstr * c + s0 + sw],
                                              scf[:, 0:sw], ps_bc[:, 512:512 + sw], ALU.add),
                              reads=[Bf("p1scf"), Bf("p1bc")], writes=[Bf(f"p1ln{lsl}")])
                        pump(1)
                for kc in range(4):
                    for si, (s0, sw) in enumerate(subs):
                        pp = (kc + si) % 2
                        psk = [ps_a, ps_b][pp]
                        pskb = Bf(f"p1ab{pp}")
                        for c in range(16):
                            Sd.op("pe", lambda e, kc=kc, c=c, lsl=lsl, lstr=lstr, s0=s0, sw=sw, psk=psk:
                                  e.matmul(psk[:, 0:sw],
                                           wkvl[:, 1024 * c + 128 * kc:1024 * c + 128 * kc + 128],
                                           ln0[lsl][:, lstr * c + s0:lstr * c + s0 + sw],
                                           start=(c == 0), stop=(c == 15)),
                                  reads=[Bf("p1wkv"), Bf(f"p1ln{lsl}")], writes=[pskb])
                        Sd.op("act", lambda e, kc=kc, sw=sw, psk=psk, pp=pp:
                              e.add(rawf[pp][:, 0:sw], psk[:, 0:sw],
                                    sb["c_bkvk"][:, kc:kc + 1]),
                              reads=[pskb, Bf("c_bkvk")], writes=[Bf(f"p1rawf{pp}")])
                        rope4(rawf[pp][:, 0:sw], Bf(f"p1rawf{pp}"),
                              rawb[pp][:, 0:sw], Bf(f"p1rawb{pp}"),
                              tck[:, s0:s0 + sw], tsk[:, s0:s0 + sw],
                              stg[pp][:, 0:sw], Bf(f"p1stg{pp}"), sw,
                              "c_rspt", ps_bc, Bf("p1bc"),
                              t1f[pp], Bf(f"p1t1f{pp}"), t2f[pp], Bf(f"p1t2f{pp}"),
                              Bf("ctck"), Bf("ctsk"))
                        Sd.op("act", lambda e, kc=kc, col0=col0, s0=s0, sw=sw, pp=pp:
                              e.dma_start(out=KSPD[kc, :, col0 + s0:col0 + s0 + sw],
                                          in_=stg[pp][:, 0:sw]),
                              reads=[Bf(f"p1stg{pp}")], writes=[Bf("ksp_done")], dma=True)
                        pump(1)
                for bi, blk in enumerate(blocks):
                    pp = bi % 2
                    psv = [ps_a, ps_b][pp]
                    psvb = Bf(f"p1ab{pp}")
                    for c in range(16):
                        Sd.op("pe", lambda e, c=c, bi=bi, lsl=lsl, lstr=lstr, psv=psv:
                              e.matmul(psv[:],
                                       ln0[lsl][:, lstr * c + 128 * bi:lstr * c + 128 * bi + 128],
                                       wkvl[:, 1024 * c + 512:1024 * c + 1024],
                                       start=(c == 0), stop=False),
                              reads=[Bf(f"p1ln{lsl}"), Bf("p1wkv")], writes=[psvb])
                    Sd.op("pe", lambda e, psv=psv:
                          e.matmul(psv[:], sb["c_ones1"][:], sb["c_bkvv_bf"][:],
                                   start=False, stop=True),
                          reads=[Bf("c_ones1"), Bf("c_bkvv_bf")], writes=[psvb])
                    Sd.op("act", lambda e, pp=pp, psv=psv: e.copy(stg[pp][:], psv[:]),
                          reads=[psvb], writes=[Bf(f"p1stg{pp}")])
                    Sd.op("act", lambda e, blk=blk, pp=pp:
                          e.dma_start(out=VSPD[blk], in_=stg[pp][:]),
                          reads=[Bf(f"p1stg{pp}")], writes=[Bf("vsp_done")], dma=True)
                    pump(1)
                if G >= 1:
                    g = G - 1
                    pump_until("wq")
                    for mg in range(8):
                        for c in range(16):
                            wsl = c % 2
                            Sd.op("sync", lambda e, mg=mg, c=c, wsl=wsl: e.dma_start(
                                out=wqt[wsl][:],
                                in_=rb(WBF["wq"])[c][:, 256 * mg:256 * mg + 256]),
                                reads=[Bf("wbf_wq")], writes=[Bf(f"p1wqt{wsl}")], dma=True)
                            for mi in range(2):
                                Sd.op("pe", lambda e, mi=mi, c=c, lsl=lsl, lstr=lstr, wsl=wsl:
                                      e.matmul(ps_q[mi][:],
                                               wqt[wsl][:, 128 * mi:128 * mi + 128],
                                               ln0[lsl][:, lstr * c:lstr * c + 512],
                                               start=(c == 0), stop=(c == 15)),
                                      reads=[Bf(f"p1wqt{wsl}"), Bf(f"p1ln{lsl}")],
                                      writes=[Bf(f"p1q{mi}")])
                        for mi in range(2):
                            m = 2 * mg + mi
                            pp = mi % 2
                            Sd.op("act", lambda e, m=m, mi=mi, pp=pp:
                                  e.add(rawf[pp][:], ps_q[mi][:], sb["c_bq"][:, m:m + 1]),
                                  reads=[Bf(f"p1q{mi}"), Bf("c_bq")], writes=[Bf(f"p1rawf{pp}")])
                            rope4(rawf[pp][:], Bf(f"p1rawf{pp}"),
                                  rawb[pp][:], Bf(f"p1rawb{pp}"),
                                  tcq[:], tsq[:],
                                  stg[pp][:], Bf(f"p1stg{pp}"), 512,
                                  "c_rspt", ps_bc, Bf("p1bc"),
                                  t1f[pp], Bf(f"p1t1f{pp}"), t2f[pp], Bf(f"p1t2f{pp}"),
                                  Bf("ctcq"), Bf("ctsq"))
                            Sd.op("act", lambda e, m=m, g=g, pp=pp:
                                  e.dma_start(out=QSPD[m, :, 512 * g:512 * g + 512],
                                              in_=stg[pp][:]),
                                  reads=[Bf(f"p1stg{pp}")], writes=[Bf("qsp_done")], dma=True)
                        pump(1)
            Sd.barrier()

        phase1()
        p0s.close()

        def phase2():
          with contextlib.ExitStack() as p2:
            load_packed("c_mwin", 128, 36, 512, BF16,
                        MWIN.rearrange("g r p q -> p (g r) q"), p2)
            qsp = alloc("p2qsp", [128, 16 * 512], BF16, p2)
            ksp = alloc("p2ksp", [128, 4 * 1152], BF16, p2)
            vsp = alloc("p2vsp", [128, 9 * 512], BF16, p2)
            prb2 = [alloc(f"p2prb{i}", [128, 9 * 512], BF16, p2) for i in range(2)]
            y2g = alloc("p2y2", [128, 16 * 512], BF16, p2)
            tmpf = alloc("p2tmpf", [128, 512], F32, p2)
            srow = alloc("p2srow", [1, 512], F32, p2)
            rrow = alloc("p2rrow", [1, 512], F32, p2)
            r2a = alloc("p2r2a", [1, 512], F32, p2)
            r2b = alloc("p2r2b", [1, 512], F32, p2)
            wom = [alloc(f"p2wo{i}", [128, 16 * 128], BF16, p2) for i in range(2)]
            xcc = [alloc(f"p2xc{i}", [128, 512], F32, p2) for i in range(2)]
            xmf = alloc("p2xmf", [128, 512], F32, p2)
            xsq = alloc("p2xsq", [128, 512], F32, p2)
            ps_sc = [psum(f"p2sc{i}", [128, 512], F32, p2) for i in range(2)]
            ps_av = psum("p2av", [128, 512], F32, p2)
            ps_s1 = psum("p2s1", [1, 512], F32, p2)
            ps_rb = psum("p2rb", [128, 512], F32, p2)
            ps_wo = psum("p2wo_", [128, 512], F32, p2)
            ps_st_m, ps_st_e = sb["p0m"], sb["p0e"]

            for g in range(4):
                qc0 = 512 * g
                for m in range(16):
                    Sd.op("sync", lambda e, m=m, qc0=qc0: e.dma_start(
                        out=qsp[:, 512 * m:512 * m + 512],
                        in_=QSPD[m, :, qc0:qc0 + 512]),
                        reads=[Bf("qsp_done")], writes=[Bf("p2qsp")], dma=True)
                for kc in range(4):
                    Sd.op("sync", lambda e, kc=kc: e.dma_start(
                        out=ksp[:, 1152 * kc:1152 * kc + 128], in_=KSPD[kc, :, 0:128]),
                        reads=[Bf("ksp_done")], writes=[Bf("p2ksp")], dma=True)
                    Sd.op("sync", lambda e, kc=kc, g=g: e.dma_start(
                        out=ksp[:, 1152 * kc + 128:1152 * kc + 1152],
                        in_=KSPD[kc, :, 128 * (4 * g + 1):128 * (4 * g + 9)]),
                        reads=[Bf("ksp_done")], writes=[Bf("p2ksp")], dma=True)
                for r in range(9):
                    blk = 0 if r == 0 else 4 * g + r
                    Sd.op("sync", lambda e, r=r, blk=blk: e.dma_start(
                        out=vsp[:, 512 * r:512 * r + 512], in_=VSPD[blk]),
                        reads=[Bf("vsp_done")], writes=[Bf("p2vsp")], dma=True)

                for h in range(16):
                    gv = h // 4
                    prb = prb2[h % 2]
                    prbB = Bf(f"p2prb{h % 2}")
                    for r in range(9):
                        pp = r % 2
                        Sd.op("pe", lambda e, h=h, gv=gv, r=r, pp=pp:
                              e.matmul(ps_sc[pp][:],
                                       ksp[:, 1152 * gv + 128 * r:1152 * gv + 128 * r + 128],
                                       qsp[:, 512 * h:512 * h + 512],
                                       start=True, stop=True),
                              reads=[Bf("p2ksp"), Bf("p2qsp")], writes=[Bf(f"p2sc{pp}")])
                        Sd.op("act", lambda e, r=r, pp=pp, prb=prb:
                              e.activation(prb[:, 512 * r:512 * r + 512], ps_sc[pp][:],
                                           AF.Exp),
                              reads=[Bf(f"p2sc{pp}")], writes=[prbB])
                    Sd.op("pool", lambda e, g=g, prb=prb:
                          e.tensor_mul(prb[:], prb[:],
                                       sb["c_mwin"][:, 4608 * g:4608 * (g + 1)]),
                          reads=[Bf("c_mwin")], writes=[prbB])
                    for r in range(9):
                        Sd.op("pe", lambda e, r=r, prb=prb:
                              e.matmul(ps_s1[:], sb["c_ones128"][:],
                                       prb[:, 512 * r:512 * r + 512],
                                       start=(r == 0), stop=(r == 8)),
                              reads=[prbB, Bf("c_ones128")], writes=[Bf("p2s1")])
                        Sd.op("pe", lambda e, gv=gv, r=r, prb=prb:
                              e.matmul(ps_av[:],
                                       vsp[:, 512 * r + 128 * gv:512 * r + 128 * gv + 128],
                                       prb[:, 512 * r:512 * r + 512],
                                       start=(r == 0), stop=(r == 8)),
                              reads=[Bf("p2vsp"), prbB], writes=[Bf("p2av")])
                    Sd.op("act", lambda e: e.copy(srow[:], ps_s1[:]),
                          reads=[Bf("p2s1")], writes=[Bf("p2srow")])
                    Sd.op("dve", lambda e: e.reciprocal(rrow[:], srow[:]),
                          reads=[Bf("p2srow")], writes=[Bf("p2rrow")])
                    Sd.op("pe", lambda e: e.matmul(ps_rb[:], sb["c_ones1f"][:],
                                                   rrow[:], start=True, stop=True),
                          reads=[Bf("p2rrow"), Bf("c_ones1f")], writes=[Bf("p2rb")])
                    Sd.op("act", lambda e: e.copy(tmpf[:], ps_av[:]),
                          reads=[Bf("p2av")], writes=[Bf("p2tmpf")])
                    Sd.op("dve", lambda e, h=h:
                          e.tensor_tensor(y2g[:, 512 * h:512 * h + 512], tmpf[:],
                                          ps_rb[:], ALU.mult),
                          reads=[Bf("p2tmpf"), Bf("p2rb")], writes=[Bf("p2y2")])
                    pump(2)
                if TAPS:
                    Sd.op("act", lambda e, g=g: e.dma_start(
                        out=Y2D[g], in_=y2g[:]),
                        reads=[Bf("p2y2")], writes=[Bf("y2_done")], dma=True)
                pump_until("wo")
                for m in range(16):
                    wsl = m % 2
                    pump(1)
                    Sd.op("sync", lambda e, m=m, wsl=wsl: e.dma_start(
                        out=wom[wsl][:].rearrange("p (n q) -> p n q", q=128),
                        in_=rb(WBF["wo"])[:, :, 128 * m:128 * m + 128]
                        .rearrange("n p q -> p n q")),
                        reads=[Bf("wbf_wo")], writes=[Bf(f"p2wo{wsl}")], dma=True)
                    for c in range(16):
                        Sd.op("pe", lambda e, c=c, wsl=wsl:
                              e.matmul(ps_wo[:], wom[wsl][:, 128 * c:128 * c + 128],
                                       y2g[:, 512 * c:512 * c + 512],
                                       start=(c == 0), stop=(c == 15)),
                              reads=[Bf(f"p2wo{wsl}"), Bf("p2y2")], writes=[Bf("p2wops")])
                    xsl = m % 2
                    Sd.op("sync", lambda e, m=m, qc0=qc0, xsl=xsl: e.dma_start(
                        out=xcc[xsl][:], in_=X[m, :, OWN0 + qc0:OWN0 + qc0 + 512]),
                        writes=[Bf(f"p2xc{xsl}")], dma=True)
                    Sd.op("dve", lambda e, m=m, xsl=xsl:
                          e.scalar_tensor_tensor(xmf[:], ps_wo[:],
                                                 sb["c_bo"][:, m:m + 1], xcc[xsl][:],
                                                 ALU.add, ALU.add),
                          reads=[Bf("p2wops"), Bf("c_bo"), Bf(f"p2xc{xsl}")],
                          writes=[Bf("p2xmf")])
                    Sd.op("act", lambda e, m=m, qc0=qc0: e.dma_start(
                        out=XMIDD[m, :, qc0:qc0 + 512], in_=xmf[:]),
                        reads=[Bf("p2xmf")], writes=[Bf("xmid_done")], dma=True)
                    Sd.op("pool", lambda e: e.tensor_mul(xsq[:], xmf[:], xmf[:]),
                          reads=[Bf("p2xmf")], writes=[Bf("p2xsq")])
                    Sd.op("pe", lambda e, m=m, ps_st_m=ps_st_m:
                          e.matmul(ps_st_m[:], sb["c_ones128f"][:], xmf[:],
                                   start=(m == 0), stop=(m == 15)),
                          reads=[Bf("p2xmf"), Bf("c_ones128f")], writes=[Bf("p0m")])
                    Sd.op("pe", lambda e, m=m, ps_st_e=ps_st_e:
                          e.matmul(ps_st_e[:], sb["c_ones128f"][:], xsq[:],
                                   start=(m == 0), stop=(m == 15)),
                          reads=[Bf("p2xsq"), Bf("c_ones128f")], writes=[Bf("p0e")])
                Sd.op("act", lambda e, ps_st_m=ps_st_m: e.copy(r2a[:], ps_st_m[:]),
                      reads=[Bf("p0m")], writes=[Bf("p2r2a")])
                Sd.op("act", lambda e, ps_st_e=ps_st_e: e.copy(r2b[:], ps_st_e[:]),
                      reads=[Bf("p0e")], writes=[Bf("p2r2b")])
                Sd.op("act", lambda e, qc0=qc0: e.dma_start(
                    out=ROWD[2:3, qc0:qc0 + 512], in_=r2a[:]),
                    reads=[Bf("p2r2a")], writes=[Bf("row_mf")], dma=True)
                Sd.op("act", lambda e, qc0=qc0: e.dma_start(
                    out=ROWD[3:4, qc0:qc0 + 512], in_=r2b[:]),
                    reads=[Bf("p2r2b")], writes=[Bf("row_ef")], dma=True)
            Sd.barrier()

        phase2()

        def phase34():
          with contextlib.ExitStack() as p34:
            qp_sb = alloc("qp_sb", [128, 8 * 1024], BF16, p34)
            qr_sb = alloc("qr_sb", [128, 8 * 1024], BF16, p34)
            rc1 = alloc("p3c1", [1, NQ], F32, p34)
            rac1 = alloc("p3ac1", [1, NQ], F32, p34)

            with contextlib.ExitStack() as p3:
                r3m = alloc("p3rm", [1, NQ], F32, p3)
                r3e = alloc("p3re", [1, NQ], F32, p3)
                r3t1 = alloc("p3t1", [1, NQ], F32, p3)
                r3sf = alloc("p3sf", [1, NQ], F32, p3)
                r3asf = alloc("p3asf", [1, NQ], F32, p3)
                r3t2 = r3sf
                Sd.op("sync", lambda e: e.dma_start(out=r3m[:], in_=ROWD[2:3, 0:NQ]),
                      reads=[Bf("row_mf")], writes=[Bf("p3rm")], dma=True)
                Sd.op("sync", lambda e: e.dma_start(out=r3e[:], in_=ROWD[3:4, 0:NQ]),
                      reads=[Bf("row_ef")], writes=[Bf("p3re")], dma=True)
                row_coefs(r3m, Bf("p3rm"), r3e, Bf("p3re"),
                          r3sf, Bf("p3sf"), r3asf, Bf("p3asf"),
                          r3t1, Bf("p3t1"), r3t2, Bf("p3sf"), NQ)
                Sd.op("dve", lambda e: e.tensor_scalar(r3m[:], r3t1[:], EPS, None, ALU.add),
                      reads=[Bf("p3t1")], writes=[Bf("p3rm")])
                Sd.op("dve", lambda e: e.reciprocal(r3m[:], r3m[:]),
                      reads=[Bf("p3rm")], writes=[Bf("p3rm")])
                Sd.op("dve", lambda e: e.tensor_mul(r3m[:], r3t1[:], r3m[:]),
                      reads=[Bf("p3t1")], writes=[Bf("p3rm")])
                Sd.op("act", lambda e: e.activation(r3m[:], r3m[:], AF.Sqrt, bias=sb["c_eps"][:]),
                      reads=[Bf("p3rm"), Bf("c_eps")], writes=[Bf("p3rm")])
                Sd.op("dve", lambda e: e.reciprocal(r3m[:], r3m[:]),
                      reads=[Bf("p3rm")], writes=[Bf("p3rm")])
                Sd.op("dve", lambda e: e.tensor_mul(rc1[:], r3sf[:], r3m[:]),
                      reads=[Bf("p3sf"), Bf("p3rm")], writes=[Bf("p3c1")])
                Sd.op("dve", lambda e: e.tensor_mul(rac1[:], r3asf[:], r3m[:]),
                      reads=[Bf("p3asf"), Bf("p3rm")], writes=[Bf("p3ac1")])
                Sd.op("act", lambda e: e.dma_start(out=ROWD[4:5, 0:NQ], in_=r3sf[:]),
                      reads=[Bf("p3sf")], writes=[Bf("row_sf")], dma=True)
                Sd.op("act", lambda e: e.dma_start(out=ROWD[5:6, 0:NQ], in_=r3asf[:]),
                      reads=[Bf("p3asf")], writes=[Bf("row_asf")], dma=True)
                ln1rf = load_const("c_ln1r", [1, 2048], F32, LN1R, p3)
                tcm = alloc("p3tcm", [128, 512], F32, p3)
                tsm = alloc("p3tsm", [128, 512], F32, p3)

                kul = alloc("p3ku", [128, 4 * 1024], BF16, p3)
                qul = alloc("p3qu", [128, 4 * 1024], BF16, p3)
                rql = alloc("p3rq", [128, 4 * 1024], BF16, p3)
                pump_until("ropeq")
                for nm, t in [("ku", kul), ("qu", qul), ("ropeq", rql)]:
                    for c in range(4):
                        Sd.op("sync", lambda e, nm=nm, t=t, c=c: e.dma_start(
                            out=t[:, 1024 * c:1024 * (c + 1)], in_=rb(WBF[nm])[c]),
                            reads=[Bf("wbf_" + nm)], writes=[Bf("p3w_" + nm)], dma=True)

                xmc = [alloc(f"p3xm{i}", [128, 512], F32, p3) for i in range(2)]
                rms1 = alloc("p3rms1", [128, 16 * 512], BF16, p3)
                kvdb = alloc("p3kvd", [128, 4 * 512], BF16, p3)
                qdb = alloc("p3qd", [128, 4 * 512], BF16, p3)
                stg = [alloc(f"p3stg{i}", [128, 512], BF16, p3) for i in range(2)]
                rawf = [alloc(f"p3rawf{i}", [128, 512], F32, p3) for i in range(2)]
                rawb = [alloc(f"p3rawb{i}", [128, 512], BF16, p3) for i in range(2)]
                t1f = [alloc(f"p3t1f{i}", [128, 512], F32, p3) for i in range(2)]
                t2f = [alloc(f"p3t2f{i}", [128, 512], F32, p3) for i in range(2)]
                wst = [alloc(f"p3wst{i}", [128, 2048], BF16, p3) for i in range(2)]
                ps_bc = psum("p3bc", [128, 1024], F32, p3)
                ps_a = psum("p3a_", [128, 512], F32, p3)
                ps_b = psum("p3b_", [128, 512], F32, p3)
                ps_rot = psum("p3rot", [128, 512], F32, p3)

                for g in range(4):
                    qc0 = 512 * g
                    Sd.op("sync", lambda e, qc0=qc0: e.dma_start(
                        out=tcm[:], in_=COSM[:, qc0:qc0 + 512]),
                        writes=[Bf("ctcm")], dma=True)
                    Sd.op("sync", lambda e, qc0=qc0: e.dma_start(
                        out=tsm[:], in_=SINM[:, qc0:qc0 + 512]),
                        writes=[Bf("ctsm")], dma=True)
                    Sd.op("pe", lambda e, qc0=qc0: e.matmul(
                        ps_bc[:, 0:512], sb["c_ones1f"][:], rc1[:, qc0:qc0 + 512],
                        start=True, stop=True),
                        reads=[Bf("p3c1"), Bf("c_ones1f")], writes=[Bf("p3bc")])
                    for c in range(16):
                        xsl = c % 2
                        Sd.op("sync", lambda e, c=c, qc0=qc0, xsl=xsl: e.dma_start(
                            out=xmc[xsl][:], in_=XMIDD[c, :, qc0:qc0 + 512]),
                            reads=[Bf("xmid_done")], writes=[Bf(f"p3xm{xsl}")], dma=True)
                        Sd.op("pe", lambda e, c=c, qc0=qc0: e.matmul(
                            ps_bc[:, 512:1024], ln1rf[:, 128 * c:128 * c + 128],
                            rac1[:, qc0:qc0 + 512], start=True, stop=True),
                            reads=[Bf("p3ac1"), Bf("c_ln1r")], writes=[Bf("p3bc2")])
                        Sd.op("dve", lambda e, xsl=xsl: e.tensor_tensor(
                            xmc[xsl][:], xmc[xsl][:], ps_bc[:, 0:512], ALU.mult),
                            reads=[Bf("p3bc")], writes=[Bf(f"p3xm{xsl}")])
                        Sd.op("dve", lambda e, c=c, xsl=xsl: e.scalar_tensor_tensor(
                            rms1[:, 512 * c:512 * c + 512], xmc[xsl][:],
                            sb["c_ln1p"][:, c:c + 1], ps_bc[:, 512:1024],
                            ALU.mult, ALU.add),
                            reads=[Bf(f"p3xm{xsl}"), Bf("c_ln1p"), Bf("p3bc2")],
                            writes=[Bf("p3rms1")])
                    projs = [("kvd", kvdb, "p3kvdb")]
                    if g < 2:
                        projs.append(("qd", qdb, "p3qdb"))
                    for nm, dstt, dstb in projs:
                        for mc in range(4):
                            wsl = mc % 2
                            Sd.op("sync", lambda e, nm=nm, mc=mc, wsl=wsl: e.dma_start(
                                out=wst[wsl][:].rearrange("p (n q) -> p n q", q=128),
                                in_=rb(WBF[nm])[:, :, 128 * mc:128 * mc + 128]
                                .rearrange("n p q -> p n q")),
                                reads=[Bf("wbf_" + nm)], writes=[Bf(f"p3wst{wsl}")], dma=True)
                            pp = mc % 2
                            psx = [ps_a, ps_b][pp]
                            for c in range(16):
                                Sd.op("pe", lambda e, c=c, wsl=wsl, psx=psx: e.matmul(
                                    psx[:], wst[wsl][:, 128 * c:128 * c + 128],
                                    rms1[:, 512 * c:512 * c + 512],
                                    start=(c == 0), stop=(c == 15)),
                                    reads=[Bf(f"p3wst{wsl}"), Bf("p3rms1")],
                                    writes=[Bf(f"p3ab{pp}")])
                            Sd.op("act", lambda e, dstt=dstt, mc=mc, psx=psx: e.copy(
                                dstt[:, 512 * mc:512 * mc + 512], psx[:]),
                                reads=[Bf(f"p3ab{pp}")], writes=[Bf(dstb)])
                    for pc in range(8):
                        pp = pc % 2
                        psx = [ps_a, ps_b][pp]
                        for kc in range(4):
                            Sd.op("pe", lambda e, pc=pc, kc=kc, psx=psx: e.matmul(
                                psx[:], kul[:, 1024 * kc + 128 * pc:1024 * kc + 128 * pc + 128],
                                kvdb[:, 512 * kc:512 * kc + 512],
                                start=(kc == 0), stop=(kc == 3)),
                                reads=[Bf("p3w_ku"), Bf("p3kvdb")], writes=[Bf(f"p3ab{pp}")])
                        Sd.op("act", lambda e, pp=pp, psx=psx: e.copy(stg[pp][:], psx[:]),
                              reads=[Bf(f"p3ab{pp}")], writes=[Bf(f"p3stg{pp}")])
                        Sd.op("act", lambda e, pc=pc, qc0=qc0, pp=pp: e.dma_start(
                            out=KPD[pc, :, qc0:qc0 + 512], in_=stg[pp][:]),
                            reads=[Bf(f"p3stg{pp}")], writes=[Bf("kp_done")], dma=True)
                        pump(1)
                    for pc in range(8):
                        wsl = pc % 2
                        pp = pc % 2
                        psx = [ps_a, ps_b][pp]
                        Sd.op("sync", lambda e, pc=pc, wsl=wsl: e.dma_start(
                            out=wst[wsl][:].rearrange("p (n q) -> p n q", q=128),
                            in_=rb(WBF["ropek"])[:, :, 128 * pc:128 * pc + 128]
                            .rearrange("n p q -> p n q")),
                            reads=[Bf("wbf_ropek")], writes=[Bf(f"p3wst{wsl}")], dma=True)
                        for c in range(16):
                            Sd.op("pe", lambda e, c=c, wsl=wsl, psx=psx: e.matmul(
                                psx[:], wst[wsl][:, 128 * c:128 * c + 128],
                                rms1[:, 512 * c:512 * c + 512],
                                start=(c == 0), stop=(c == 15)),
                                reads=[Bf(f"p3wst{wsl}"), Bf("p3rms1")],
                                writes=[Bf(f"p3ab{pp}")])
                        Sd.op("act", lambda e, pp=pp, psx=psx: e.copy(rawf[pp][:], psx[:]),
                              reads=[Bf(f"p3ab{pp}")], writes=[Bf(f"p3rawf{pp}")])
                        rope4(rawf[pp][:], Bf(f"p3rawf{pp}"),
                              rawb[pp][:], Bf(f"p3rawb{pp}"),
                              tcm[:], tsm[:],
                              stg[pp][:], Bf(f"p3stg{pp}"), 512,
                              "c_rmlat", ps_rot, Bf("p3rot"),
                              t1f[pp], Bf(f"p3t1f{pp}"), t2f[pp], Bf(f"p3t2f{pp}"),
                              Bf("ctcm"), Bf("ctsm"))
                        Sd.op("act", lambda e, pc=pc, qc0=qc0, pp=pp: e.dma_start(
                            out=KRD[pc, :, qc0:qc0 + 512], in_=stg[pp][:]),
                            reads=[Bf(f"p3stg{pp}")], writes=[Bf("kr_done")], dma=True)
                        pump(1)
                    for vh in range(4):
                        wsl = vh % 2
                        Sd.op("sync", lambda e, vh=vh, wsl=wsl: e.dma_start(
                            out=wst[wsl][:].rearrange("p (n q) -> p n q", q=512),
                            in_=rb(WBF["vu"])[:, :, 512 * vh:512 * vh + 512]
                            .rearrange("n p q -> p n q")),
                            reads=[Bf("wbf_vu")], writes=[Bf(f"p3wst{wsl}")], dma=True)
                        for bk in range(4):
                            pp = bk % 2
                            psx = [ps_a, ps_b][pp]
                            for kc in range(4):
                                Sd.op("pe", lambda e, bk=bk, kc=kc, wsl=wsl, psx=psx:
                                      e.matmul(psx[:],
                                               kvdb[:, 512 * kc + 128 * bk:512 * kc + 128 * bk + 128],
                                               wst[wsl][:, 512 * kc:512 * kc + 512],
                                               start=(kc == 0), stop=(kc == 3)),
                                      reads=[Bf("p3kvdb"), Bf(f"p3wst{wsl}")],
                                      writes=[Bf(f"p3ab{pp}")])
                            Sd.op("act", lambda e, pp=pp, psx=psx: e.copy(stg[pp][:], psx[:]),
                                  reads=[Bf(f"p3ab{pp}")], writes=[Bf(f"p3stg{pp}")])
                            Sd.op("act", lambda e, g=g, bk=bk, vh=vh, pp=pp: e.dma_start(
                                out=VMLAD[4 * g + bk, :, 512 * vh:512 * vh + 512],
                                in_=stg[pp][:]),
                                reads=[Bf(f"p3stg{pp}")], writes=[Bf("vmla_done")], dma=True)
                            pump(1)
                    if g < 2:
                        for pc in range(8):
                            pp = pc % 2
                            psx = [ps_a, ps_b][pp]
                            for kc in range(4):
                                Sd.op("pe", lambda e, pc=pc, kc=kc, psx=psx: e.matmul(
                                    psx[:], qul[:, 1024 * kc + 128 * pc:1024 * kc + 128 * pc + 128],
                                    qdb[:, 512 * kc:512 * kc + 512],
                                    start=(kc == 0), stop=(kc == 3)),
                                    reads=[Bf("p3w_qu"), Bf("p3qdb")], writes=[Bf(f"p3ab{pp}")])
                            Sd.op("dve", lambda e, pc=pc, qc0=qc0, psx=psx: e.tensor_copy(
                                qp_sb[:, 1024 * pc + qc0:1024 * pc + qc0 + 512], psx[:]),
                                reads=[Bf(f"p3ab{pp}")], writes=[Bf("qp_sb")])
                        for pc in range(8):
                            pp = pc % 2
                            psx = [ps_a, ps_b][pp]
                            for kc in range(4):
                                Sd.op("pe", lambda e, pc=pc, kc=kc, psx=psx: e.matmul(
                                    psx[:], rql[:, 1024 * kc + 128 * pc:1024 * kc + 128 * pc + 128],
                                    qdb[:, 512 * kc:512 * kc + 512],
                                    start=(kc == 0), stop=(kc == 3)),
                                    reads=[Bf("p3w_ropeq"), Bf("p3qdb")], writes=[Bf(f"p3ab{pp}")])
                            Sd.op("act", lambda e, pp=pp, psx=psx: e.copy(rawf[pp][:], psx[:]),
                                  reads=[Bf(f"p3ab{pp}")], writes=[Bf(f"p3rawf{pp}")])
                            rope4(rawf[pp][:], Bf(f"p3rawf{pp}"),
                                  rawb[pp][:], Bf(f"p3rawb{pp}"),
                                  tcm[:], tsm[:],
                                  stg[pp][:], Bf(f"p3stg{pp}"), 512,
                                  "c_rmlat", ps_rot, Bf("p3rot"),
                                  t1f[pp], Bf(f"p3t1f{pp}"), t2f[pp], Bf(f"p3t2f{pp}"),
                                  Bf("ctcm"), Bf("ctsm"))
                            Sd.op("dve", lambda e, pc=pc, qc0=qc0, pp=pp: e.tensor_copy(
                                qr_sb[:, 1024 * pc + qc0:1024 * pc + qc0 + 512], stg[pp][:]),
                                reads=[Bf(f"p3stg{pp}")], writes=[Bf("qr_sb")])
                Sd.barrier()

            with contextlib.ExitStack() as p4:
                load_packed("c_tri", 128, 8, NOWN, BF16,
                            TRIM.rearrange("j p q -> p j q"), p4)
                y2 = alloc("p4y2", [128, 16 * 1024], BF16, p4)
                r4sf = alloc("p4rsf", [1, NOWN], F32, p4)
                r4asf = alloc("p4rasf", [1, NOWN], F32, p4)
                Sd.op("sync", lambda e: e.dma_start(out=r4sf[:], in_=ROWD[4:5, 0:NOWN]),
                      reads=[Bf("row_sf")], writes=[Bf("p4rsf")], dma=True)
                Sd.op("sync", lambda e: e.dma_start(out=r4asf[:], in_=ROWD[5:6, 0:NOWN]),
                      reads=[Bf("row_asf")], writes=[Bf("p4rasf")], dma=True)

                with contextlib.ExitStack() as p4a:
                    kpr = alloc("p4kpr", [128, 4 * 2048], BF16, p4a)
                    qpr = alloc("p4qpr", [128, 4 * 1024], BF16, p4a)
                    vh_sb = alloc("p4v", [128, 16 * 1024], BF16, p4a)
                    prb2 = [alloc(f"p4prb{i}", [128, 1024], BF16, p4a) for i in range(2)]
                    tmpf = alloc("p4tmpf", [128, 1024], F32, p4a)
                    srow = alloc("p4srow", [1, 1024], F32, p4a)
                    rrow = alloc("p4rrow", [1, 1024], F32, p4a)
                    ps_sc = [psum(f"p4sc{i}", [128, 1024], F32, p4a) for i in range(2)]
                    ps_av = psum("p4av", [128, 1024], F32, p4a)
                    ps_s1h = [sb["p0m"], sb["p0e"]]

                    def load_kq_round(hh, r):
                        for a in range(4 * r, 4 * r + 4):
                            h = 8 * hh + a
                            pc, base = h // 2, 64 * (h % 2)
                            s = a % 4
                            Sd.op("sync", lambda e, s=s, pc=pc, base=base: e.dma_start(
                                out=kpr[0:64, 2048 * s:2048 * (s + 1)],
                                in_=KPD[pc, base:base + 64, :]),
                                reads=[Bf("kp_done")], writes=[Bf("p4kpr")], dma=True)
                            Sd.op("sync", lambda e, s=s, pc=pc, base=base: e.dma_start(
                                out=kpr[64:128, 2048 * s:2048 * (s + 1)],
                                in_=KRD[pc, base:base + 64, :]),
                                reads=[Bf("kr_done")], writes=[Bf("p4kpr")], dma=True)
                            Sd.op("sync", lambda e, s=s, pc=pc, base=base: e.dma_start(
                                out=qpr[0:64, 1024 * s:1024 * (s + 1)],
                                in_=qp_sb[base:base + 64, 1024 * pc:1024 * pc + 1024]),
                                reads=[Bf("qp_sb")], writes=[Bf("p4qpr")], dma=True)
                            Sd.op("sync", lambda e, s=s, pc=pc, base=base: e.dma_start(
                                out=qpr[64:128, 1024 * s:1024 * (s + 1)],
                                in_=qr_sb[base:base + 64, 1024 * pc:1024 * pc + 1024]),
                                reads=[Bf("qr_sb")], writes=[Bf("p4qpr")], dma=True)

                    for hh in range(2):
                        load_kq_round(hh, 0)
                        for j in range(16):
                            Sd.op("sync", lambda e, hh=hh, j=j: e.dma_start(
                                out=vh_sb[:, 1024 * j:1024 * (j + 1)],
                                in_=VMLAD[j, :, 1024 * hh:1024 * hh + 1024]),
                                reads=[Bf("vmla_done")], writes=[Bf("p4v")], dma=True)
                        for h in range(8 * hh, 8 * hh + 8):
                            a = h - 8 * hh
                            if a == 4:
                                load_kq_round(hh, 1)
                            s = a % 4

                            def sums_av(jj, h=h, hh=hh):
                                prb = prb2[jj % 2]
                                prbB = Bf(f"p4prb{jj % 2}")
                                for qh in range(2):
                                    Sd.op("pe", lambda e, jj=jj, qh=qh, prb=prb, prbB=prbB:
                                          e.matmul(ps_s1h[qh][:],
                                                   sb["c_ones128"][:],
                                                   prb[:, 512 * qh:512 * qh + 512],
                                                   start=(jj == 0), stop=(jj == 15)),
                                          reads=[prbB, Bf("c_ones128")], writes=[Bf("p0m"), Bf("p0e")])
                                    Sd.op("pe", lambda e, h=h, hh=hh, jj=jj, qh=qh, prb=prb, prbB=prbB:
                                          e.matmul(ps_av[:, 512 * qh:512 * qh + 512],
                                                   vh_sb[:, 1024 * jj + 128 * (h - 8 * hh):
                                                         1024 * jj + 128 * (h - 8 * hh) + 128],
                                                   prb[:, 512 * qh:512 * qh + 512],
                                                   start=(jj == 0), stop=(jj == 15)),
                                          reads=[Bf("p4v"), prbB], writes=[Bf("p4av")])

                            for j in range(16):
                                pp = j % 2
                                prb = prb2[pp]
                                prbB = Bf(f"p4prb{pp}")
                                for qh in range(2):
                                    Sd.op("pe", lambda e, s=s, j=j, qh=qh, pp=pp:
                                          e.matmul(ps_sc[pp][:, 512 * qh:512 * qh + 512],
                                                   kpr[:, 2048 * s + 128 * j:2048 * s + 128 * j + 128],
                                                   qpr[:, 1024 * s + 512 * qh:1024 * s + 512 * qh + 512],
                                                   start=True, stop=True),
                                          reads=[Bf("p4kpr"), Bf("p4qpr")], writes=[Bf(f"p4sc{pp}")])
                                Sd.op("act", lambda e, j=j, pp=pp, prb=prb:
                                      e.activation(prb[:], ps_sc[pp][:], AF.Exp,
                                                   bias=sb["c_bmla"][:, j:j + 1]),
                                      reads=[Bf(f"p4sc{pp}"), Bf("c_bmla")], writes=[prbB])
                                pump(1)
                                if j < 8:
                                    Sd.op("pool", lambda e, j=j, prb=prb:
                                          e.tensor_mul(prb[:], prb[:],
                                                       sb["c_tri"][:, NOWN * j:NOWN * (j + 1)]),
                                          reads=[Bf("c_tri")], writes=[prbB])
                                if j >= 1:
                                    sums_av(j - 1)
                            sums_av(15)
                            Sd.op("act", lambda e: e.copy(srow[:, 0:512], ps_s1h[0][:]),
                                  reads=[Bf("p0m")], writes=[Bf("p4srow")])
                            Sd.op("act", lambda e: e.copy(srow[:, 512:1024], ps_s1h[1][:]),
                                  reads=[Bf("p0e")], writes=[Bf("p4srow")])
                            Sd.op("dve", lambda e: e.reciprocal(rrow[:], srow[:]),
                                  reads=[Bf("p4srow")], writes=[Bf("p4rrow")])
                            for qh in range(2):
                                Sd.op("pe", lambda e, qh=qh: e.matmul(
                                    ps_sc[0][:, 512 * qh:512 * qh + 512], sb["c_ones1f"][:],
                                    rrow[:, 512 * qh:512 * qh + 512], start=True, stop=True),
                                    reads=[Bf("p4rrow"), Bf("c_ones1f")], writes=[Bf("p4sc0")])
                            Sd.op("act", lambda e: e.copy(tmpf[:], ps_av[:]),
                                  reads=[Bf("p4av")], writes=[Bf("p4tmpf")])
                            Sd.op("dve", lambda e, h=h: e.tensor_tensor(
                                y2[:, 1024 * h:1024 * (h + 1)], tmpf[:], ps_sc[0][:], ALU.mult),
                                reads=[Bf("p4tmpf"), Bf("p4sc0")], writes=[Bf("p4y2")])
                    Sd.barrier()

                with contextlib.ExitStack() as p4b:
                    wom = [alloc(f"p4wo{i}", [128, 16 * 128], BF16, p4b) for i in range(2)]
                    xmc4 = [alloc(f"p4xm{i}", [128, 1024], F32, p4b) for i in range(2)]
                    hcc = alloc("p4hc", [128, 1024], F32, p4b)
                    scc = alloc("p4scc", [128, 1024], F32, p4b)
                    t2c = alloc("p4t2c", [128, 1024], F32, p4b)
                    ps_ow = psum("p4ow", [128, 1024], F32, p4b)
                    ps_sf = psum("p4sf_", [128, 1024], F32, p4b)
                    ps_asf = psum("p4asf_", [128, 1024], F32, p4b)
                    for qh in range(2):
                        Sd.op("pe", lambda e, qh=qh: e.matmul(
                            ps_sf[:, 512 * qh:512 * qh + 512], sb["c_ones1f"][:],
                            r4sf[:, 512 * qh:512 * qh + 512], start=True, stop=True),
                            reads=[Bf("p4rsf"), Bf("c_ones1f")], writes=[Bf("p4psf")])
                        Sd.op("pe", lambda e, qh=qh: e.matmul(
                            ps_asf[:, 512 * qh:512 * qh + 512], sb["c_ones1f"][:],
                            r4asf[:, 512 * qh:512 * qh + 512], start=True, stop=True),
                            reads=[Bf("p4rasf"), Bf("c_ones1f")], writes=[Bf("p4pasf")])
                    pump_until("ow")
                    for m in range(16):
                        wsl = m % 2
                        pump(1)
                        Sd.op("sync", lambda e, m=m, wsl=wsl: e.dma_start(
                            out=wom[wsl][:].rearrange("p (n q) -> p n q", q=128),
                            in_=rb(WBF["ow"])[:, :, 128 * m:128 * m + 128]
                            .rearrange("n p q -> p n q")),
                            reads=[Bf("wbf_ow")], writes=[Bf(f"p4wo{wsl}")], dma=True)
                        Sd.op("sync", lambda e, m=m, wsl=wsl: e.dma_start(
                            out=xmc4[wsl][:], in_=XMIDD[m, :, 0:1024]),
                            reads=[Bf("xmid_done")], writes=[Bf(f"p4xm{wsl}")], dma=True)
                        for c in range(16):
                            for qh in range(2):
                                Sd.op("pe", lambda e, c=c, qh=qh, wsl=wsl:
                                      e.matmul(ps_ow[:, 512 * qh:512 * qh + 512],
                                               wom[wsl][:, 128 * c:128 * c + 128],
                                               y2[:, 1024 * c + 512 * qh:1024 * c + 512 * qh + 512],
                                               start=(c == 0), stop=(c == 15)),
                                      reads=[Bf(f"p4wo{wsl}"), Bf("p4y2")], writes=[Bf("p4ow")])
                        Sd.op("dve", lambda e, wsl=wsl: e.tensor_tensor(
                            t2c[:], xmc4[wsl][:], ps_sf[:], ALU.mult),
                            reads=[Bf(f"p4xm{wsl}"), Bf("p4psf")], writes=[Bf("p4t2c")])
                        Sd.op("dve", lambda e: e.tensor_tensor(
                            t2c[:], t2c[:], ps_asf[:], ALU.add),
                            reads=[Bf("p4pasf")], writes=[Bf("p4t2c")])
                        Sd.op("dve", lambda e: e.tensor_tensor(
                            hcc[:], t2c[:], ps_ow[:], ALU.add),
                            reads=[Bf("p4t2c"), Bf("p4ow")], writes=[Bf("p4hc")])
                        Sd.op("pool", lambda e, wsl=wsl: e.tensor_add(
                            scc[:], hcc[:], xmc4[wsl][:]),
                            reads=[Bf("p4hc"), Bf(f"p4xm{wsl}")], writes=[Bf("p4scc")])
                        Sd.op("act", lambda e, m=m: e.dma_start(out=HDD[m], in_=hcc[:]),
                              reads=[Bf("p4hc")], writes=[Bf("h_done")], dma=True)
                        Sd.op("act", lambda e, m=m: e.dma_start(out=SDD[m], in_=scc[:]),
                              reads=[Bf("p4scc")], writes=[Bf("s_done")], dma=True)
                    Sd.barrier()

        if PHASES_MAX >= 3:
            phase34()

        def phase5():
          with contextlib.ExitStack() as p5:
            rms2 = alloc("p5rms2", [128, 16 * 1024], BF16, p5)
            hcs = [alloc(f"p5h{i}", [128, 1024], F32, p5) for i in range(2)]
            hsq = alloc("p5hsq", [128, 1024], F32, p5)
            r5a = alloc("p5ra", [1, NOWN], F32, p5)
            r5b = alloc("p5rb", [1, NOWN], F32, p5)
            ps_g = psum("p5g", [128, 1024], F32, p5)
            ps_u = psum("p5u", [128, 1024], F32, p5)
            ps_w = psum("p5w", [128, 1024], F32, p5)

            for c in range(16):
                sl = c % 2
                Sd.op("sync", lambda e, c=c, sl=sl: e.dma_start(
                    out=hcs[sl][:], in_=HDD[c]),
                    reads=[Bf("h_done")], writes=[Bf(f"p5h{sl}")], dma=True)
                Sd.op("pool", lambda e, sl=sl: e.tensor_mul(hsq[:], hcs[sl][:], hcs[sl][:]),
                      reads=[Bf(f"p5h{sl}")], writes=[Bf("p5hsq")])
                for qh in range(2):
                    pst = [sb["p0m"], sb["p0e"]][qh]
                    Sd.op("pe", lambda e, c=c, qh=qh, pst=pst: e.matmul(
                        pst[:], sb["c_ones128f"][:],
                        hsq[:, 512 * qh:512 * qh + 512],
                        start=(c == 0), stop=(c == 15)),
                        reads=[Bf("p5hsq"), Bf("c_ones128f")], writes=[Bf("p0m"), Bf("p0e")])
            Sd.op("act", lambda e: e.copy(r5a[:, 0:512], sb["p0m"][:]),
                  reads=[Bf("p0m")], writes=[Bf("p5ra")])
            Sd.op("act", lambda e: e.copy(r5a[:, 512:1024], sb["p0e"][:]),
                  reads=[Bf("p0e")], writes=[Bf("p5ra")])
            Sd.op("dve", lambda e: e.tensor_scalar_mul(r5a[:], r5a[:], 1.0 / E),
                  reads=[Bf("p5ra")], writes=[Bf("p5ra")])
            Sd.op("act", lambda e: e.activation(r5b[:], r5a[:], AF.Sqrt, bias=sb["c_eps"][:]),
                  reads=[Bf("p5ra"), Bf("c_eps")], writes=[Bf("p5rb")])
            Sd.op("dve", lambda e: e.reciprocal(r5b[:], r5b[:]),
                  reads=[Bf("p5rb")], writes=[Bf("p5rb")])
            for qh in range(2):
                Sd.op("pe", lambda e, qh=qh: e.matmul(
                    ps_w[:, 512 * qh:512 * qh + 512], sb["c_ones1f"][:],
                    r5b[:, 512 * qh:512 * qh + 512], start=True, stop=True),
                    reads=[Bf("p5rb"), Bf("c_ones1f")], writes=[Bf("p5w")])
            for c in range(16):
                sl = c % 2
                Sd.op("sync", lambda e, c=c, sl=sl: e.dma_start(
                    out=hcs[sl][:], in_=HDD[c]),
                    reads=[Bf("h_done")], writes=[Bf(f"p5h{sl}")], dma=True)
                Sd.op("dve", lambda e, sl=sl: e.tensor_tensor(
                    hcs[sl][:], hcs[sl][:], ps_w[:], ALU.mult),
                    reads=[Bf("p5w")], writes=[Bf(f"p5h{sl}")])
                Sd.op("dve", lambda e, c=c, sl=sl: e.tensor_scalar_mul(
                    rms2[:, 1024 * c:1024 * (c + 1)], hcs[sl][:],
                    sb["c_ln2p"][:, c:c + 1]),
                    reads=[Bf(f"p5h{sl}"), Bf("c_ln2p")], writes=[Bf("p5rms2")])
            Sd.barrier()

            routw = alloc("p5routw", [128, 16 * 7], BF16, p5)
            pump_until("routerw")
            for c in range(16):
                Sd.op("sync", lambda e, c=c: e.dma_start(
                    out=routw[:, 7 * c:7 * (c + 1)], in_=rb(WBF["routerw"])[c]),
                    reads=[Bf("wbf_routerw")], writes=[Bf("p5routw")], dma=True)
            ptm8 = alloc("p5ptm8", [128, 64], F32, p5)
            mx8 = alloc("p5mx8", [128, 64], F32, p5)
            idx8 = alloc("p5idx8", [128, 64], mybir.dt.uint32, p5)
            idxf = alloc("p5idxf", [128, 16], F32, p5)
            sel1 = alloc("p5sel1", [128, 8 * 7], F32, p5)
            sel2 = alloc("p5sel2", [128, 8 * 7], F32, p5)
            wden = alloc("p5wden", [128, 8], F32, p5)
            w1c = alloc("p5w1c", [128, 8], F32, p5)
            w2c = alloc("p5w2c", [128, 8], F32, p5)
            wtm = alloc("p5wtm", [128, 8 * 7], F32, p5)
            wpad = alloc("p5wpad", [128, 128], F32, p5)
            wrows = alloc("p5wrows", [8, 1024], F32, p5)
            wone = alloc("p5wone", [1, 1024], F32, p5)
            RBuf = Bf("p5ptmB")

            Sd.op("dve", lambda e: e.memset(ptm8[:], NEG), writes=[RBuf])
            Sd.op("dve", lambda e: e.memset(wpad[:], 0.0), writes=[Bf("p5wpad")])
            for t in range(8):
                for c in range(16):
                    Sd.op("pe", lambda e, t=t, c=c: e.matmul(
                        ps_u[:, 0:7], rms2[:, 1024 * c + 128 * t:1024 * c + 128 * t + 128],
                        routw[:, 7 * c:7 * (c + 1)], start=(c == 0), stop=False),
                        reads=[Bf("p5rms2"), Bf("p5routw")], writes=[Bf("p5u")])
                Sd.op("pe", lambda e: e.matmul(
                    ps_u[:, 0:7], sb["c_ones1"][:], sb["c_brout_bf"][:],
                    start=False, stop=True),
                    reads=[Bf("c_ones1"), Bf("c_brout_bf")], writes=[Bf("p5u")])
                Sd.op("act", lambda e, t=t: e.activation(
                    ptm8[:, 8 * t:8 * t + 7], ps_u[:, 0:7], AF.Sigmoid),
                    reads=[Bf("p5u")], writes=[RBuf])
            for t in range(8):
                Sd.op("dve", lambda e, t=t: e.max(
                    mx8[:, 8 * t:8 * (t + 1)], ptm8[:, 8 * t:8 * (t + 1)]),
                    reads=[RBuf], writes=[Bf("p5mx8")])
                Sd.op("dve", lambda e, t=t: e.max_index(
                    idx8[:, 8 * t:8 * (t + 1)], mx8[:, 8 * t:8 * (t + 1)],
                    ptm8[:, 8 * t:8 * (t + 1)]),
                    reads=[RBuf, Bf("p5mx8")], writes=[Bf("p5idx8")])
                Sd.op("dve", lambda e, t=t: e.tensor_copy(
                    idxf[:, 2 * t:2 * t + 2], idx8[:, 8 * t:8 * t + 2]),
                    reads=[Bf("p5idx8")], writes=[Bf("p5idxf")])
                Sd.op("dve", lambda e, t=t: e.tensor_single_scalar(
                    sel1[:, 7 * t:7 * t + 7], sb["c_iota7"][:],
                    idxf[:, 2 * t:2 * t + 1], ALU.is_equal),
                    reads=[Bf("p5idxf"), Bf("c_iota7")], writes=[Bf("p5sel1")])
                Sd.op("dve", lambda e, t=t: e.tensor_single_scalar(
                    sel2[:, 7 * t:7 * t + 7], sb["c_iota7"][:],
                    idxf[:, 2 * t + 1:2 * t + 2], ALU.is_equal),
                    reads=[Bf("p5idxf"), Bf("c_iota7")], writes=[Bf("p5sel2")])
                Sd.op("dve", lambda e, t=t: e.tensor_add(
                    wden[:, t:t + 1], mx8[:, 8 * t:8 * t + 1],
                    mx8[:, 8 * t + 1:8 * t + 2]),
                    reads=[Bf("p5mx8")], writes=[Bf("p5wden")])
            Sd.op("dve", lambda e: e.reciprocal(wden[:], wden[:]),
                  reads=[Bf("p5wden")], writes=[Bf("p5wden")])
            for t in range(8):
                Sd.op("dve", lambda e, t=t: e.tensor_mul(
                    w1c[:, t:t + 1], mx8[:, 8 * t:8 * t + 1], wden[:, t:t + 1]),
                    reads=[Bf("p5mx8"), Bf("p5wden")], writes=[Bf("p5w1c")])
                Sd.op("dve", lambda e, t=t: e.tensor_mul(
                    w2c[:, t:t + 1], mx8[:, 8 * t + 1:8 * t + 2], wden[:, t:t + 1]),
                    reads=[Bf("p5mx8"), Bf("p5wden")], writes=[Bf("p5w2c")])
            for t in range(8):
                Sd.op("dve", lambda e, t=t: e.tensor_scalar_mul(
                    wtm[:, 7 * t:7 * t + 7], sel1[:, 7 * t:7 * t + 7], w1c[:, t:t + 1]),
                    reads=[Bf("p5sel1"), Bf("p5w1c")], writes=[Bf("p5wtm")])
                Sd.op("dve", lambda e, t=t: e.scalar_tensor_tensor(
                    wtm[:, 7 * t:7 * t + 7], sel2[:, 7 * t:7 * t + 7], w2c[:, t:t + 1],
                    wtm[:, 7 * t:7 * t + 7], ALU.mult, ALU.add),
                    reads=[Bf("p5sel2"), Bf("p5w2c")], writes=[Bf("p5wtm")])
            for t in range(8):
                Sd.op("dve", lambda e, t=t: e.tensor_copy(
                    wpad[:, 0:7], wtm[:, 7 * t:7 * t + 7]),
                    reads=[Bf("p5wtm"), Bf("p5wpad")], writes=[Bf("p5wpad")])
                Sd.op("pe", lambda e: e.transpose(ps_g[:, 0:128], wpad[:], sb["c_idf"][:]),
                      reads=[Bf("p5wpad"), Bf("c_idf")], writes=[Bf("p5g")])
                Sd.op("act", lambda e, t=t: e.mul(
                    wrows[0:8, 128 * t:128 * t + 128], ps_g[0:8, 0:128], 0.125),
                    reads=[Bf("p5g")], writes=[Bf("p5wrows")])
            if TAPS:
                for ti, (tt, tb) in enumerate([(ptm8, RBuf), (mx8, Bf("p5mx8")),
                                               (sel1, Bf("p5sel1")), (sel2, Bf("p5sel2")),
                                               (idxf, Bf("p5idxf"))]):
                    Sd.op("act", lambda e, ti=ti, tt=tt: e.dma_start(
                        out=PTMD[:, 64 * ti:64 * ti + tt.shape[1]], in_=tt[:]),
                        reads=[tb], writes=[Bf("ptm_done")], dma=True)
            Sd.op("act", lambda e: e.dma_start(out=WROWD[:], in_=wrows[:]),
                  reads=[Bf("p5wrows")], writes=[Bf("wrow_done")], dma=True)

            hp = alloc("p5hp", [128, 32 * 1024], BF16, p5)
            silu_sb = alloc("p5silu", [128, 1024], F32, p5)
            wg = [alloc(f"p5wg{i}", [128, 16 * 128], BF16, p5) for i in range(2)]
            wu = [alloc(f"p5wu{i}", [128, 16 * 128], BF16, p5) for i in range(2)]
            wd = [alloc(f"p5wd{i}", [128, 32 * 128], BF16, p5) for i in range(2)]
            scs = [alloc(f"p5scs{i}", [128, 1024], F32, p5) for i in range(2)]
            oat = alloc("p5oat", [128, 1024], F32, p5)
            gav, uav, dav = rb(WBF["gate_all"]), rb(WBF["up_all"]), rb(WBF["down_all"])

            pump_until("up_all")
            for pa in range(2):
                for kk, k in enumerate(range(4 * pa, 4 * pa + 4)):
                    if k == 0:
                        wsrc, wsrcB = sb["c_shw"], Bf("c_shw")
                    else:
                        Sd.op("sync", lambda e, k=k: e.dma_start(
                            out=wone[:], in_=WROWD[k - 1:k, :]),
                            reads=[Bf("wrow_done")], writes=[Bf("p5wone")], dma=True)
                        wsrc, wsrcB = wone, Bf("p5wone")
                    for qh in range(2):
                        Sd.op("pe", lambda e, qh=qh, wsrc=wsrc: e.matmul(
                            ps_w[:, 512 * qh:512 * qh + 512], sb["c_ones1f"][:],
                            wsrc[:, 512 * qh:512 * qh + 512], start=True, stop=True),
                            reads=[wsrcB, Bf("c_ones1f")], writes=[Bf("p5w")])
                    for ic in range(8):
                        wsl = ic % 2
                        Sd.op("sync", lambda e, k=k, ic=ic, wsl=wsl: e.dma_start(
                            out=wg[wsl][:].rearrange("p (n q) -> p n q", q=128),
                            in_=gav[16 * k:16 * k + 16, :, 128 * ic:128 * ic + 128]
                            .rearrange("n p q -> p n q")),
                            reads=[Bf("wbf_gate_all")], writes=[Bf(f"p5wg{wsl}")], dma=True)
                        Sd.op("sync", lambda e, k=k, ic=ic, wsl=wsl: e.dma_start(
                            out=wu[wsl][:].rearrange("p (n q) -> p n q", q=128),
                            in_=uav[16 * k:16 * k + 16, :, 128 * ic:128 * ic + 128]
                            .rearrange("n p q -> p n q")),
                            reads=[Bf("wbf_up_all")], writes=[Bf(f"p5wu{wsl}")], dma=True)
                        for c in range(16):
                            for qh in range(2):
                                Sd.op("pe", lambda e, c=c, qh=qh, wsl=wsl: e.matmul(
                                    ps_g[:, 512 * qh:512 * qh + 512],
                                    wg[wsl][:, 128 * c:128 * c + 128],
                                    rms2[:, 1024 * c + 512 * qh:1024 * c + 512 * qh + 512],
                                    start=(c == 0), stop=(c == 15)),
                                    reads=[Bf(f"p5wg{wsl}"), Bf("p5rms2")], writes=[Bf("p5g")])
                                Sd.op("pe", lambda e, c=c, qh=qh, wsl=wsl: e.matmul(
                                    ps_u[:, 512 * qh:512 * qh + 512],
                                    wu[wsl][:, 128 * c:128 * c + 128],
                                    rms2[:, 1024 * c + 512 * qh:1024 * c + 512 * qh + 512],
                                    start=(c == 0), stop=(c == 15)),
                                    reads=[Bf(f"p5wu{wsl}"), Bf("p5rms2")], writes=[Bf("p5u")])
                        Sd.op("act", lambda e: e.activation(silu_sb[:], ps_g[:], AF.Silu),
                              reads=[Bf("p5g")], writes=[Bf("p5silu")])
                        Sd.op("dve", lambda e: e.tensor_tensor(
                            silu_sb[:], ps_u[:], silu_sb[:], ALU.mult),
                            reads=[Bf("p5u")], writes=[Bf("p5silu")])
                        Sd.op("dve", lambda e, kk=kk, ic=ic: e.tensor_tensor(
                            hp[:, 1024 * (8 * kk + ic):1024 * (8 * kk + ic + 1)],
                            silu_sb[:], ps_w[:], ALU.mult),
                            reads=[Bf("p5silu"), Bf("p5w")], writes=[Bf("p5hp")])
                        pump(3)
                pump_until("down_all")
                for dm in range(16):
                    wsl = dm % 2
                    Sd.op("sync", lambda e, pa=pa, dm=dm, wsl=wsl: e.dma_start(
                        out=wd[wsl][:].rearrange("p (n q) -> p n q", q=128),
                        in_=dav[32 * pa:32 * pa + 32, :, 128 * dm:128 * dm + 128]
                        .rearrange("n p q -> p n q")),
                        reads=[Bf("wbf_down_all")], writes=[Bf(f"p5wd{wsl}")], dma=True)
                    if pa == 0:
                        Sd.op("sync", lambda e, dm=dm, wsl=wsl: e.dma_start(
                            out=scs[wsl][:], in_=SDD[dm]),
                            reads=[Bf("s_done")], writes=[Bf(f"p5scs{wsl}")], dma=True)
                    else:
                        Sd.op("sync", lambda e, dm=dm, wsl=wsl: e.dma_start(
                            out=scs[wsl][:], in_=OAD[dm]),
                            reads=[Bf("oa_done")], writes=[Bf(f"p5scs{wsl}")], dma=True)
                    for kx in range(32):
                        for qh in range(2):
                            Sd.op("pe", lambda e, kx=kx, qh=qh, wsl=wsl: e.matmul(
                                ps_g[:, 512 * qh:512 * qh + 512],
                                wd[wsl][:, 128 * kx:128 * kx + 128],
                                hp[:, 1024 * kx + 512 * qh:1024 * kx + 512 * qh + 512],
                                start=(kx == 0), stop=(kx == 31)),
                                reads=[Bf(f"p5wd{wsl}"), Bf("p5hp")], writes=[Bf("p5g")])
                    Sd.op("dve", lambda e, wsl=wsl: e.scalar_tensor_tensor(
                        oat[:], ps_g[:], 1.0, scs[wsl][:], ALU.mult, ALU.add),
                        reads=[Bf("p5g"), Bf(f"p5scs{wsl}")], writes=[Bf("p5oat")])
                    if pa == 0:
                        Sd.op("act", lambda e, dm=dm: e.dma_start(
                            out=OAD[dm], in_=oat[:]),
                            reads=[Bf("p5oat")], writes=[Bf("oa_done")], dma=True)
                    else:
                        Sd.op("act", lambda e, dm=dm: e.dma_start(
                            out=OUT[dm], in_=oat[:]),
                            reads=[Bf("p5oat")], writes=[Bf("out_done")], dma=True)
            Sd.barrier()

        if PHASES_MAX >= 5:
            phase5()

        cum_c = {}
        for e in Sched.ENGS:
            vc, tc = {}, 0
            for i, rec in enumerate(Sd.ops[e]):
                if rec[3] is None and i in Sd.marked[e]:
                    tc += 1
                vc[i] = tc
            cum_c[e] = vc

        def emit(eng_name, eng):
            waited_c = {}
            waited_l = {}

            def do_wait(we, wi):
                dep_lane = Sd.ops[we][wi][3]
                if dep_lane is not None:
                    lane, use = dep_lane
                    v = 16 * (use + 1)
                    key = (we, lane)
                    if waited_l.get(key, 0) < v:
                        eng.wait_ge(sems_dl[we][lane], v)
                        waited_l[key] = v
                else:
                    v = cum_c[we][wi]
                    if v > 0 and waited_c.get(we, 0) < v:
                        eng.wait_ge(sems[we], v)
                        waited_c[we] = v

            for i, rec in enumerate(Sd.ops[eng_name]):
                thunk, waits, amt, lane_info = rec[0], rec[1], rec[2], rec[3]
                for key in sorted(waits, key=str):
                    do_wait(key[0], waits[key])
                if len(rec) > 4:
                    for (we, wi) in rec[4]:
                        do_wait(we, wi)
                if lane_info is not None:
                    lane, use = lane_info
                    if use > 0:
                        key = (eng_name, lane)
                        v = 16 * use
                        if waited_l.get(key, 0) < v:
                            eng.wait_ge(sems_dl[eng_name][lane], v)
                            waited_l[key] = v
                inst = eng.nop() if thunk is None else thunk(eng)
                if lane_info is not None:
                    inst.then_inc(sems_dl[eng_name][lane_info[0]], 16)
                elif i in Sd.marked[eng_name]:
                    inst.then_inc(sems[eng_name], 1)

        with nc.Block() as block:

            @block.tensor
            def _(eng):
                emit("pe", eng)

            @block.scalar
            def _(eng):
                emit("act", eng)

            @block.vector
            def _(eng):
                emit("dve", eng)

            @block.gpsimd
            def _(eng):
                emit("pool", eng)

            @block.sync
            def _(eng):
                emit("sync", eng)

    n_ops = sum(len(Sd.ops[e]) for e in Sched.ENGS)
    n_marks = sum(len(Sd.marked[e]) for e in Sched.ENGS)
    return nc, n_ops, n_marks


_PROG_CACHE = {}


def _get_program():
    if "nc" not in _PROG_CACHE:
        nc, n_ops, n_marks = build_program()
        _PROG_CACHE["nc"] = nc
    return _PROG_CACHE["nc"]


def kernel(**inputs):
    in_maps = _stage_inputs(inputs)
    nc = _get_program()
    res = run_bass_kernel_spmd(nc, in_maps, core_ids=list(range(8)))
    out = np.zeros((B, S, E), np.float32)
    for c in range(8):
        b, h = c // 2, c % 2
        o = np.asarray(res.results[c]["out"])
        out[b, 1024 * h:1024 * h + 1024, :] = o.reshape(E, NOWN).T
    return out
